# version 62
# baseline (speedup 1.0000x reference)
import numpy as np
import concourse.bass as bass
import concourse.mybir as mybir
from concourse.bass_utils import run_bass_kernel_spmd
from contextlib import ExitStack

F32 = mybir.dt.float32
BF16 = mybir.dt.bfloat16
AF = mybir.ActivationFunctionType
ALU = mybir.AluOpType
AXX = mybir.AxisListType.X

S, DM = 4096, 1024
NH = 16
NTC, TOKC = 8, 512
NQ1, QS1 = 16, 256
NQ2, QC2 = 8, 512
LN_EPS = 1e-5
LNB_EPS = 1e-3

ENGS = ("sync", "gpsimd", "tensor", "vector", "scalar")


class Sem:
    __slots__ = ("h", "n")

    def __init__(self, h):
        self.h = h
        self.n = 0


class Prog:

    def __init__(self):
        self.streams = {k: [] for k in ENGS}

    def op(self, eng, build, sem=None, inc=1):
        if sem is None:
            self.streams[eng].append(("op", build, None, 0, 0))
        else:
            sem.n += inc
            self.streams[eng].append(("op", build, sem, inc, sem.n))

    def wait(self, eng, sem, val=None):
        v = sem.n if val is None else val
        if v <= 0:
            return
        self.streams[eng].append(("wait", sem, v))

    def finalize(self):
        awaited = {}
        for eng in ENGS:
            for ent in self.streams[eng]:
                if ent[0] == "wait":
                    _, sem, v = ent
                    awaited.setdefault(id(sem), set()).add(v)
        rank = {}
        for eng in ENGS:
            for ent in self.streams[eng]:
                if ent[0] == "op" and ent[2] is not None and ent[3] == 1:
                    sid = id(ent[2])
                    if sid not in rank:
                        rank[sid] = {v: i + 1 for i, v in
                                     enumerate(sorted(awaited.get(sid, ())))}
        out = {k: [] for k in ENGS}
        for eng in ENGS:
            seen = {}
            for ent in self.streams[eng]:
                if ent[0] == "op":
                    _, build, sem, inc, cnt = ent
                    if sem is None:
                        out[eng].append(build)
                    elif inc != 1:
                        out[eng].append(lambda e, b=build, h=sem.h, k=inc:
                                        b(e).then_inc(h, k))
                    else:
                        r = rank[id(sem)].get(cnt)
                        if r is None:
                            out[eng].append(build)
                        else:
                            out[eng].append(lambda e, b=build, h=sem.h:
                                            b(e).then_inc(h, 1))
                else:
                    _, sem, v = ent
                    sid = id(sem)
                    if sid in rank:
                        vv = rank[sid].get(v)
                        assert vv is not None
                    else:
                        vv = v
                    if seen.get(sid, 0) >= vv:
                        continue
                    seen[sid] = vv
                    out[eng].append(lambda e, h=sem.h, v=vv: e.wait_ge(h, v))
        self.streams = out


def build_nc(debug=False):
    nc = bass.Bass("TRN2", target_bir_lowering=False)

    x_in = nc.dram_tensor("x", [S, DM], F32, kind="ExternalInput").ap()
    names = ["wq_a", "wk_a", "wv_a", "wo_a", "wo_b"]
    dws = {n: nc.dram_tensor(n, [DM, DM], F32, kind="ExternalInput").ap() for n in names}
    for n in ("wq_b", "wk_b", "wv_b"):
        dws[n] = nc.dram_tensor(n, [NH, DM, 64], F32, kind="ExternalInput").ap()
    gam = nc.dram_tensor("gam", [DM], F32, kind="ExternalInput").ap()
    bet = nc.dram_tensor("bet", [DM], F32, kind="ExternalInput").ap()
    f_win = nc.dram_tensor("f_win", [DM, 64], F32, kind="ExternalInput").ap()
    f_bin = nc.dram_tensor("f_bin", [64], F32, kind="ExternalInput").ap()
    f_w1 = nc.dram_tensor("f_w1", [64, 16], F32, kind="ExternalInput").ap()
    f_b1 = nc.dram_tensor("f_b1", [64, 16], F32, kind="ExternalInput").ap()
    f_w2 = nc.dram_tensor("f_w2", [64, 16], F32, kind="ExternalInput").ap()
    f_b2 = nc.dram_tensor("f_b2", [64], F32, kind="ExternalInput").ap()
    f_wout = nc.dram_tensor("f_wout", [64, DM], F32, kind="ExternalInput").ap()
    f_bout = nc.dram_tensor("f_bout", [DM], F32, kind="ExternalInput").ap()
    out = nc.dram_tensor("out", [S, DM], F32, kind="ExternalOutput").ap()
    dbg = nc.dram_tensor("dbg", [S, DM], F32, kind="Internal" if not debug else "ExternalOutput").ap()
    dbg2 = nc.dram_tensor("dbg2", [NQ2, 128, 8, QC2], BF16, kind="Internal" if not debug else "ExternalOutput").ap()

    xs1 = nc.dram_tensor("xs1", [S, DM], F32, kind="Internal").ap()
    q_scr = nc.dram_tensor("q_scr", [8, 128, S], BF16, kind="Internal").ap()
    k_scr = nc.dram_tensor("k_scr", [8, 128, S], BF16, kind="Internal").ap()
    wscr = {n: nc.dram_tensor(n + "_s", [8, 128, DM], BF16, kind="Internal").ap()
            for n in ("wqT", "wkT", "wvT", "woT", "wqB", "wkB", "wvB", "woB")}

    p = Prog()
    ctx_stack = ExitStack()
    e = ctx_stack.enter_context

    k_sb = e(nc.sbuf_tensor("k_sb", [128, 8, S], BF16))
    v_sb = e(nc.sbuf_tensor("v_sb", [128, 32, 1040], BF16))
    v_sbB = v_sb.ap().rearrange("p a (h c) -> p a h c", h=16)

    ident = e(nc.sbuf_tensor("ident", [128, 128], BF16))
    ones_colb = e(nc.sbuf_tensor("ones_colb", [128, 1], BF16))
    ones_rowf = e(nc.sbuf_tensor("ones_rowf", [128, 128], F32))
    negC = e(nc.sbuf_tensor("negC", [128, 2], F32))
    nmax = e(nc.sbuf_tensor("nmax", [1, 8], F32))
    stat = e(nc.sbuf_tensor("stat", [128, 16], F32))
    win_sb = e(nc.sbuf_tensor("win_sb", [128, 8, 128], BF16))
    stack2 = e(nc.sbuf_tensor("stack2", [128, 64], F32))
    bin_sb = e(nc.sbuf_tensor("bin_sb", [128, 1], F32))
    w1p = e(nc.sbuf_tensor("w1p", [128, 8], F32))
    b1p = e(nc.sbuf_tensor("b1p", [128, 8], F32))
    w2p = e(nc.sbuf_tensor("w2p", [128, 8], F32))
    b2_sb = e(nc.sbuf_tensor("b2_sb", [64, 1], F32))
    wout_aug = e(nc.sbuf_tensor("wout_aug", [65, DM], BF16))
    g_aug = e(nc.sbuf_tensor("g_aug", [65, 256], BF16))
    epsb = e(nc.sbuf_tensor("epsb", [128, 3], F32))

    arena_b = e(nc.sbuf_tensor("arena_b", [128, 39 * 512], BF16))
    ab = arena_b.ap()
    arena_f = e(nc.sbuf_tensor("arena_f", [128, 34 * 256], F32))
    af = arena_f.ap()

    def view(base_ap, unit, off_kib, *shape):
        n = 1
        for s_ in shape:
            n *= s_
        o = int(off_kib * unit)
        flat = base_ap[:, o:o + n]
        if len(shape) == 1:
            return flat
        spec = "p (" + " ".join(f"d{i}" for i in range(len(shape))) + ") -> p " + \
               " ".join(f"d{i}" for i in range(len(shape)))
        return flat.rearrange(spec, **{f"d{i}": shape[i] for i in range(len(shape))})

    def bv(off, *shape):
        return view(ab, 512, off, *shape)

    def fv(off, *shape):
        return view(af, 256, off, *shape)

    wbuf = bv(0, 3, 4, DM)
    lnx_tm = bv(24, 2, DM)
    lnxT = bv(28, 8, TOKC)
    qstage = bv(36, 1, TOKC)
    nsq = bv(37, 2, TOKC)
    attT1 = bv(0, 32, QS1)
    qb1 = bv(16, 8, TOKC)
    wostr = bv(24, 2, DM)
    lnf2b = bv(24, DM)
    lnf_tm = bv(28, DM)
    lnfT = bv(30, 8, 256)
    ctxN1 = bv(34, 8, QS1)
    kstr2 = bv(0, 2, 2048)
    attb2 = bv(8, 4, QC2)
    ctxN2 = bv(12, 8, QC2)
    qb2 = bv(20, 8, QC2)
    wostr2 = bv(28, 2, DM)
    lnf2 = bv(28, DM)
    lnf_tm2 = bv(32, DM)
    lnfT2 = bv(34, 8, 256)

    xin = fv(0, 2, DM)
    xin2 = fv(24, 2, DM)
    wstage = fv(8, 4, DM)
    gamB = fv(8, DM)
    betB = fv(12, DM)
    XW = [fv(0, DM), fv(4, DM), fv(16, DM), fv(20, DM)]
    qstage2 = af[:, 16 * 256:16 * 256 + 256].bitcast(BF16)
    YT = fv(24, DM)
    recipB = fv(28, 512)
    s2w = fv(30, 256)
    h_w = fv(31, 256)
    g2a = fv(32, 256)
    g2b = fv(33, 256)

    psQ = e(nc.psum_tensor("psQ", [128, 2048], F32))
    pc4 = e(nc.psum_tensor("pc4", [128, 512], F32))
    pc5 = e(nc.psum_tensor("pc5", [128, 512], F32))
    tb = [e(nc.psum_tensor(f"tb{i}", [128, 1024], BF16)) for i in range(2)]
    tbv = [t.ap().rearrange("p (a b) -> p a b", a=8) for t in tb]

    class APH:
        __slots__ = ("_a",)
        def __init__(self, a):
            self._a = a
        def ap(self):
            return self._a
    pb = [APH(psQ.ap()[:, i * 512:(i + 1) * 512]) for i in range(4)] +          [APH(pc4.ap()), APH(pc5.ap())]

    sem_names = ["s_pe", "s_dve", "s_sc", "s_pl",
                 "s_x0", "s_x1", "s_w0", "s_w1", "s_w2",
                 "s_q0", "s_q1", "s_k0", "s_k1", "s_wo0", "s_wo1",
                 "s_m0", "s_stq0", "s_stq1", "s_stk0", "s_stk1",
                 "s_stx", "s_stout", "s_stw", "s_xa", "s_xb", "s_dbg", "s_cst",
                 "s_pl0", "s_pl1", "s_pl2", "s_pl3", "s_ps0", "s_ps1",
                 "s_ph0", "s_ph1", "s_wb0", "s_wb1", "s_xc", "s_xd", "s_k2",
                 "s_xe", "s_wc0", "s_wc1"]
    sm = {n: Sem(e(nc.semaphore(n))) for n in sem_names}
    PE, DVE, SC, PL = sm["s_pe"], sm["s_dve"], sm["s_sc"], sm["s_pl"]
    DMA_SEMS = [sm[n] for n in sem_names[4:]]

    def dma(eng, out_ap, in_ap, sem):
        p.op(eng, lambda e_, o=out_ap, i=in_ap: e_.dma_start(out=o, in_=i), sem, 16)

    def barrier():
        for eng in ENGS:
            for s_ in (PE, DVE, SC, PL):
                p.wait(eng, s_)
            for s_ in DMA_SEMS:
                p.wait(eng, s_)

    from concourse.hw_specs import get_activation_tables
    set_id = None
    for idx, (nmset, funcs) in enumerate(get_activation_tables(nc.m.arch).items()):
        if nmset == "natural_log_exp_and_others":
            set_id = idx
    assert set_id is not None

    p.op("scalar", lambda e_: e_.add_instruction(mybir.InstLoadActFuncSet(
        name=e_.bass.get_next_instruction_name(), act_func_set_id=set_id,
        ins=[], outs=[])), SC)

    p.op("vector", lambda e_: e_.memset(ident[:, :], 0.0), DVE)
    p.wait("gpsimd", DVE)
    p.op("gpsimd", lambda e_: e_.affine_select(
        out=ident[:, :], in_=ident[:, :], compare_op=ALU.not_equal, fill=1.0,
        base=0, pattern=[[-1, 128]], channel_multiplier=1), PL)
    p.op("vector", lambda e_: e_.memset(ones_colb.ap()[:, :], 1.0), DVE)
    p.op("vector", lambda e_: e_.memset(ones_rowf.ap()[:, :], 1.0), DVE)
    p.op("vector", lambda e_: e_.memset(nmax.ap()[:, :], 0.0), DVE)
    p.op("vector", lambda e_: e_.memset(g_aug.ap()[64:65, :], 1.0), DVE)
    p.op("vector", lambda e_: e_.memset(v_sb.ap()[:, :, 1024:1025], 1.0), DVE)
    p.op("vector", lambda e_: e_.memset(epsb.ap()[:, 0:1], LN_EPS), DVE)
    p.op("vector", lambda e_: e_.memset(epsb.ap()[:, 1:2], LNB_EPS), DVE)
    p.op("vector", lambda e_: e_.memset(epsb.ap()[:, 2:3], 1e-30), DVE)
    p.op("vector", lambda e_: e_.memset(stack2.ap()[:, :], 0.0), DVE)
    p.wait("gpsimd", DVE)
    p.op("gpsimd", lambda e_: e_.affine_select(
        out=stack2.ap()[:, :], in_=stack2.ap()[:, :], compare_op=ALU.not_equal,
        fill=1.0, base=0, pattern=[[-1, 64]], channel_multiplier=1), PL)
    p.wait("gpsimd", PL)
    p.op("gpsimd", lambda e_: e_.affine_select(
        out=stack2.ap()[:, :], in_=stack2.ap()[:, :], compare_op=ALU.not_equal,
        fill=1.0, base=-64, pattern=[[-1, 64]], channel_multiplier=1), PL)
    p.wait("tensor", PL)
    p.wait("tensor", DVE)

    sm0 = sm["s_m0"]
    dma("sync", bin_sb.ap()[0:64, :], f_bin.rearrange("(a b) -> a b", b=1), sm["s_xb"])
    dma("sync", bin_sb.ap()[64:128, :], f_bin.rearrange("(a b) -> a b", b=1), sm["s_xb"])
    dma("sync", b2_sb.ap()[:, :], f_b2.rearrange("(a b) -> a b", b=1), sm["s_xb"])
    for src, dst in ((f_w1, w1p), (f_b1, b1p), (f_w2, w2p)):
        dma("sync", dst.ap()[0:64, :], src[:, 0:8], sm["s_xb"])
        dma("sync", dst.ap()[64:128, :], src[:, 8:16], sm["s_xb"])
    dma("sync", wstage[:, 0, 0:512].rearrange("p (c x) -> p c x", c=8),
        f_win.rearrange("(c p) x -> p c x", p=128), sm0)
    p.wait("vector", sm0)
    p.op("vector", lambda e_: e_.tensor_copy(
        win_sb.ap()[:, :, 0:64], wstage[:, 0, 0:512].rearrange("p (c x) -> p c x", c=8)), DVE)
    p.op("vector", lambda e_: e_.tensor_copy(
        win_sb.ap()[:, :, 64:128], wstage[:, 0, 0:512].rearrange("p (c x) -> p c x", c=8)), DVE)
    p.wait("sync", DVE)
    dma("sync", wstage[0:64, 2, :], f_wout, sm0)
    dma("sync", wstage[64:65, 2, :], f_bout.rearrange("(a b) -> a b", a=1), sm0)
    p.wait("vector", sm0)
    p.op("vector", lambda e_: e_.tensor_copy(wout_aug.ap()[0:64, :], wstage[0:64, 2, :]), DVE)
    p.op("vector", lambda e_: e_.tensor_copy(wout_aug.ap()[64:65, :], wstage[64:65, 2, :]), DVE)

    smw_o = {1: sm["s_ps1"], 2: sm["s_ps0"]}
    smw_h = [sm["s_ph0"], sm["s_ph1"]]
    ALL_STORE_SEMS = (sm["s_ps0"], sm["s_ps1"], sm["s_ph0"], sm["s_ph1"])
    pslot_sems = [sm["s_pl0"], sm["s_pl1"], sm["s_pl2"], sm["s_pl3"]]
    pst = {"tq": 0, "cast_rel": {}, "tgt": {}, "tp_rel": {}, "st_rel": {}}
    pst["cast_rel"][0] = DVE.n
    pst["cast_rel"][2] = DVE.n

    def t_unit(src, dst, scale, ecg):
        def load(eng, sl):
            dma(eng, wstage[:, sl, :], src[ecg * 128:(ecg + 1) * 128, :],
                pslot_sems[sl])

        def proc(sl):
            k = pst["tq"]
            pst["tq"] += 1
            tbk = k % 2
            oslot = 1 + tbk
            p.wait("vector", pslot_sems[sl], pst["tgt"][sl])
            p.wait("vector", PE, pst["tp_rel"].get(sl, 0))
            p.op("vector", lambda e_, sl=sl: e_.tensor_copy(
                wbuf[:, 0, sl, :], wstage[:, sl, :]), DVE)
            pst["cast_rel"][sl] = DVE.n
            p.wait("tensor", DVE)
            for dc in range(8):
                p.op("tensor", lambda e_, sl=sl, dc=dc, tbk=tbk: e_.transpose(
                    tbv[tbk][:, dc, :], wbuf[:, 0, sl, dc * 128:(dc + 1) * 128],
                    ident[:, :]), PE if dc == 7 else None)
            pst["tp_rel"][sl] = PE.n
            p.wait("vector", PE)
            p.wait("vector", smw_o[oslot], pst["st_rel"].get(oslot, 0))
            p.op("vector", lambda e_, oslot=oslot, tbk=tbk, sc=scale: e_.tensor_scalar(
                wbuf[:, oslot, 0, :].rearrange("p (c x) -> p c x", c=8),
                tbv[tbk][:, :, :], sc, None, ALU.mult), DVE)
            seng = "scalar"
            p.wait(seng, DVE)
            dma(seng, dst.rearrange("c p d -> p c d")[:, :, ecg * 128:(ecg + 1) * 128],
                wbuf[:, oslot, 0, :].rearrange("p (c x) -> p c x", c=8), smw_o[oslot])
            pst["st_rel"][oslot] = smw_o[oslot].n

        return load, proc

    def h_unit(src_hde, dst, scale, key, h):
        acc = wbuf.rearrange("p s a d -> p (s a d)")[:, 0:8192].rearrange(
            "p (c x) -> p c x", c=8)

        def load(eng, sl):
            dma(eng, wstage[:, sl, 0:512].rearrange("p (c x) -> p c x", c=8),
                src_hde[h].rearrange("(c p) x -> p c x", p=128), pslot_sems[sl])

        def proc(sl):
            if h == 0:
                if key == 0:
                    p.wait("vector", PE)
                    p.wait("vector", smw_o[1])
                else:
                    p.wait("vector", smw_h[0])
                    p.wait("vector", smw_h[1])
            p.wait("vector", pslot_sems[sl], pst["tgt"][sl])
            p.op("vector", lambda e_, sl=sl, h=h, sc=scale: e_.tensor_scalar(
                acc[:, :, h * 64:(h + 1) * 64],
                wstage[:, sl, 0:512].rearrange("p (c x) -> p c x", c=8),
                sc, None, ALU.mult), DVE)
            pst["cast_rel"][sl] = DVE.n
            if h % 2 == 1:
                par = (h // 2) % 2
                seng = "scalar"
                p.wait(seng, DVE)
                p.wait(seng, smw_h[par], pst.get(("hst", par), 0))
                dma(seng, dst.rearrange("c p d -> p c d")[:, :, (h - 1) * 64:(h + 1) * 64],
                    acc[:, :, (h - 1) * 64:(h + 1) * 64], smw_h[par])
                pst[("hst", par)] = smw_h[par].n

        return load, proc

    def p_unit(src, dst, c):
        def load(eng, sl):
            dma(eng, wstage[:, sl, :],
                src.rearrange("(c p) d -> p c d", p=128)[:, c, :], pslot_sems[sl])

        def proc(sl):
            k = pst["tq"]
            pst["tq"] += 1
            oslot = 1 + k % 2
            if c == 0:
                p.wait("vector", smw_h[0])
                p.wait("vector", smw_h[1])
            p.wait("vector", pslot_sems[sl], pst["tgt"][sl])
            p.wait("vector", smw_o[oslot], pst["st_rel"].get(oslot, 0))
            p.op("vector", lambda e_, sl=sl, oslot=oslot: e_.tensor_copy(
                wbuf[:, oslot, 0, :], wstage[:, sl, :]), DVE)
            pst["cast_rel"][sl] = DVE.n
            seng = "scalar"
            p.wait(seng, DVE)
            dma(seng, dst.rearrange("c p d -> p c d")[:, c, :],
                wbuf[:, oslot, 0, :], smw_o[oslot])
            pst["st_rel"][oslot] = smw_o[oslot].n

        return load, proc

    units = []
    for nm, dn, sc in (("wq_a", "wqT", 1.0 / 32.0), ("wk_a", "wkT", 1.0),
                       ("wo_a", "woT", 1.0)):
        units += [t_unit(dws[nm], wscr[dn], sc, ecg) for ecg in range(8)]
    for key, (nm, dn, sc) in enumerate((("wq_b", "wqB", 0.125),
                                        ("wk_b", "wkB", 1.0),
                                        ("wv_b", "wvB", 1.0))):
        units += [h_unit(dws[nm], wscr[dn], sc, key, h) for h in range(NH)]
    units += [p_unit(dws["wo_b"], wscr["woB"], c) for c in range(8)]

    def issue_load(i):
        eng = "sync" if i % 2 == 0 else "gpsimd"
        sl = i % 4
        p.wait(eng, DVE, pst["cast_rel"].get(sl, 0))
        units[i][0](eng, sl)
        pst["tgt"][sl] = pslot_sems[sl].n

    for i in range(min(4, len(units))):
        issue_load(i)
    for i in range(len(units)):
        units[i][1](i % 4)
        if i + 4 < len(units):
            issue_load(i + 4)
    wvres = wbuf.rearrange("p s a d -> p (s a d)")[:, 0:8192].rearrange(
        "p (c x) -> p c x", c=8)
    wostg = wbuf.rearrange("p s a d -> p (s a d)")[:, 8192:12288].rearrange(
        "p (c x) -> p c x", c=8)
    for s_ in ALL_STORE_SEMS:
        p.wait("sync", s_)
        p.wait("gpsimd", s_)
        p.wait("vector", s_)
    p.wait("vector", PE)
    wvo = {}
    for kc in range(8):
        sl = kc % 2
        eng = "sync" if sl == 0 else "gpsimd"
        p.wait(eng, DVE, wvo.get(("c", sl), pst["cast_rel"].get(sl, 0)))
        dma(eng, wstage[:, sl, :], dws["wv_a"][kc * 128:(kc + 1) * 128, :],
            pslot_sems[sl])
        p.wait("vector", pslot_sems[sl], pslot_sems[sl].n)
        p.op("vector", lambda e_, sl=sl, kc=kc: e_.tensor_copy(
            wvres[:, kc, :], wstage[:, sl, :]), DVE)
        wvo[("c", sl)] = DVE.n
        pst["cast_rel"][sl] = DVE.n
    for ih in range(2):
        wcsem = sm["s_wc0"] if ih == 0 else sm["s_wc1"]
        p.wait("sync", PE, wvo.get("pe", 0))
        dma("sync", wostg,
            wscr["woT"].rearrange("c p d -> p c d")[:, :, ih * 512:(ih + 1) * 512],
            wcsem)
        p.wait("tensor", wcsem)
        if ih == 0:
            p.wait("tensor", DVE)
        for j in range(8):
            bank = pb[4 + j % 2]
            p.wait("tensor", DVE, wvo.get(("ev", j % 2), 0))
            for kc in range(8):
                p.op("tensor", lambda e_, b=bank, j=j, kc=kc: e_.matmul(
                    b.ap()[:, :], wvres[:, kc, j * 128:(j + 1) * 128],
                    wostg[:, kc, :], start=(kc == 0), stop=(kc == 7)),
                    PE if kc == 7 else None)
            stg = wstage[:, 2 + j % 2, 0:256].bitcast(BF16)
            p.wait("vector", PE)
            ssem = smw_o[1 + j % 2]
            p.wait("vector", ssem, wvo.get(("st", j % 2), 0))
            p.op("vector", lambda e_, b=bank, d=stg: e_.tensor_copy(
                d, b.ap()[:, :]), DVE)
            wvo[("ev", j % 2)] = DVE.n
            seng = "scalar"
            p.wait(seng, DVE)
            dma(seng, wscr["wvT"].rearrange("c p d -> p c d")[:, j, ih * 512:(ih + 1) * 512],
                stg, ssem)
            wvo[("st", j % 2)] = ssem.n
        wvo["pe"] = PE.n

    for s_ in (sm["s_ps0"], sm["s_ps1"], sm["s_ph0"], sm["s_ph1"]):
        p.wait("sync", s_)
        p.wait("gpsimd", s_)

    p.wait("sync", pslot_sems[1])
    p.wait("sync", DVE, pst["cast_rel"].get(0, 0))
    p.wait("sync", DVE, pst["cast_rel"].get(1, 0))
    dma("sync", gamB[0:1, :], gam.rearrange("(a b) -> a b", a=1), sm0)
    dma("sync", betB[0:1, :], bet.rearrange("(a b) -> a b", a=1), sm0)
    p.wait("tensor", sm0)
    p.wait("tensor", PL)
    p.wait("vector", pslot_sems[0])
    p.wait("vector", pslot_sems[1])
    for srcrow, dstB in ((gamB, gamB), (betB, betB)):
        p.wait("tensor", DVE)
        for oc in range(2):
            p.op("tensor", lambda e_, s=srcrow, oc=oc: e_.matmul(
                pb[oc].ap()[:, :], ones_rowf.ap()[0:1, :],
                s[0:1, oc * 512:(oc + 1) * 512], start=True, stop=True),
                PE)
        p.wait("vector", PE)
        for oc in range(2):
            p.op("vector", lambda e_, d=dstB, oc=oc: e_.tensor_copy(
                d[:, oc * 512:(oc + 1) * 512], pb[oc].ap()[:, :]), DVE)
    p.wait("tensor", DVE)
    prep_tb_rel = DVE.n
    prep_psum_rel = DVE.n

    def phase_a(is_b):
        xsrc = xs1 if is_b else x_in
        wq_s, wk_s, wv_s = ((wscr["wqB"], wscr["wkB"], wscr["wvB"]) if is_b
                            else (wscr["wqT"], wscr["wkT"], wscr["wvT"]))
        wsems = [sm["s_w0"], sm["s_w1"], sm["s_w2"]]
        slot_uses = [w.n // 16 for w in wsems]
        slot_pe_rel = {0: PE.n, 1: PE.n, 2: PE.n}

        xsems = [sm["s_x0"], sm["s_x1"], sm["s_xa"], sm["s_xb"]]

        def xsl(i):
            return xin[:, i, :] if i < 2 else xin2[:, i - 2, :]

        x_aff_rel = {0: 0, 1: 0, 2: 0, 3: 0}
        tb_rel = {-1: prep_tb_rel}
        evac_rel = {("qk", 0): 0, ("qk", 1): 0, ("v", 0): 0, ("v", 1): 0}
        stq_sems = [sm["s_stq0"], sm["s_stq1"]]
        stk_sems = [sm["s_stk0"], sm["s_stk1"]]
        norm_read = [0]
        norm_readb = {0: 0, 1: 0}
        aff_dve = {}
        transp_pe = {}

        def nmax_update(kind, first, nb_ap=None):
            src = pb[2].ap() if nb_ap is None else nb_ap
            p.wait("vector", PE)
            p.wait("vector", DVE)
            p.op("vector", lambda e_, src=src: e_.reduce_max(
                nmax.ap()[0:1, 4:5], src[0:1, :], axis=AXX), DVE)
            norm_read[0] = DVE.n
            col = 0 if kind == "q" else 1
            p.wait("vector", DVE)
            if first:
                p.op("vector", lambda e_, c=col: e_.tensor_copy(
                    nmax.ap()[0:1, c:c + 1], nmax.ap()[0:1, 4:5]), DVE)
            else:
                p.op("vector", lambda e_, c=col: e_.tensor_max(
                    nmax.ap()[0:1, c + 2:c + 3], nmax.ap()[0:1, c:c + 1],
                    nmax.ap()[0:1, 4:5]), DVE)
                p.wait("vector", DVE)
                p.op("vector", lambda e_, c=col: e_.tensor_copy(
                    nmax.ap()[0:1, c:c + 1], nmax.ap()[0:1, c + 2:c + 3]), DVE)

        def load_half(slot, mat_s, half):
            p.wait("sync", PE, slot_pe_rel[slot])
            dma("sync", wbuf[:, slot, :, :],
                mat_s.rearrange("c p d -> p c d")[:, half * 4:(half + 1) * 4, :],
                wsems[slot])
            slot_uses[slot] += 1

        def chunk_ln_part1(tc):
            stv = stat.ap()
            for stt in range(4):
                p.wait("sync", DVE, x_aff_rel[stt])
                r0 = tc * TOKC + stt * 128
                dma("sync", xsl(stt), xsrc[r0:r0 + 128, :], xsems[stt])
            for stt in range(4):
                p.wait("vector", xsems[stt])
                p.wait("vector", PE, transp_pe.get(tc - 1, 0))
                p.wait("vector", DVE)
                p.op("vector", lambda e_, stt=stt: e_.scalar_tensor_tensor(
                    lnx_tm[:, stt % 2, :], xsl(stt), 1.0, xsl(stt),
                    ALU.mult, ALU.mult,
                    accum_out=stat.ap()[:, 4 + stt:5 + stt]), DVE)
                p.op("vector", lambda e_, stt=stt: e_.reduce_sum(
                    stat.ap()[:, stt:stt + 1], xsl(stt), axis=AXX), DVE)
            p.wait("vector", DVE)
            p.op("vector", lambda e_: e_.tensor_scalar(
                stv[:, 8:12], stv[:, 0:4], 1.0 / DM, None, ALU.mult), DVE)
            p.wait("vector", DVE)
            p.op("vector", lambda e_: e_.tensor_tensor(
                stv[:, 12:16], stv[:, 8:12], stv[:, 8:12], ALU.mult), DVE)
            p.wait("vector", DVE)
            p.op("vector", lambda e_: e_.scalar_tensor_tensor(
                stv[:, 12:16], stv[:, 4:8], 1.0 / DM, stv[:, 12:16],
                ALU.mult, ALU.subtract), DVE)
            p.wait("scalar", DVE)
            p.op("scalar", lambda e_: e_.activation(
                stv[:, 4:8], stv[:, 12:16], AF.Ln, bias=epsb.ap()[:, 0:1]), SC)
            p.wait("scalar", SC)
            p.op("scalar", lambda e_: e_.activation(
                stv[:, 12:16], stv[:, 4:8], AF.Exp, scale=-0.5), SC)
            p.wait("vector", SC)
            for stt in range(4):
                p.op("vector", lambda e_, stt=stt: e_.tensor_scalar(
                    stv[:, 4 + stt:5 + stt], stv[:, 8 + stt:9 + stt],
                    stv[:, 12 + stt:13 + stt], -1.0, ALU.mult, ALU.mult), DVE)
            for stt in range(2):
                p.wait("vector", DVE)
                p.op("vector", lambda e_, stt=stt: e_.tensor_scalar(
                    lnx_tm[:, stt, :], xsl(stt), stv[:, 12 + stt:13 + stt],
                    stv[:, 4 + stt:5 + stt], ALU.mult, ALU.add), DVE)
            aff_dve[(tc, 0)] = DVE.n
            x_aff_rel[0] = DVE.n
            x_aff_rel[1] = DVE.n

        def chunk_top(tc):
            stv = stat.ap()
            p.wait("tensor", DVE, aff_dve[(tc, 0)])
            p.wait("tensor", DVE, tb_rel.get(tc - 1, 0))
            for stt in range(2):
                for dc in range(8):
                    p.op("tensor", lambda e_, stt=stt, dc=dc: e_.transpose(
                        tbv[stt][:, dc, :], lnx_tm[:, stt, dc * 128:(dc + 1) * 128],
                        ident[:, :]), PE if dc == 7 else None)
                p.wait("vector", PE)
                p.op("vector", lambda e_, stt=stt: e_.tensor_copy(
                    lnxT[:, :, stt * 128:(stt + 1) * 128], tbv[stt][:, :, :]), DVE)
            tp01 = PE.n
            for stt in range(2, 4):
                p.wait("vector", PE, tp01)
                p.op("vector", lambda e_, stt=stt: e_.tensor_scalar(
                    lnx_tm[:, stt % 2, :], xsl(stt), stv[:, 12 + stt:13 + stt],
                    stv[:, 4 + stt:5 + stt], ALU.mult, ALU.add), DVE)
                x_aff_rel[stt] = DVE.n
                p.wait("tensor", DVE)
                for dc in range(8):
                    p.op("tensor", lambda e_, stt=stt, dc=dc: e_.transpose(
                        tbv[stt % 2][:, dc, :], lnx_tm[:, stt % 2, dc * 128:(dc + 1) * 128],
                        ident[:, :]), PE if dc == 7 else None)
                p.wait("vector", PE)
                p.op("vector", lambda e_, stt=stt: e_.tensor_copy(
                    lnxT[:, :, stt * 128:(stt + 1) * 128], tbv[stt % 2][:, :, :]), DVE)
            transp_pe[tc] = PE.n
            tb_rel[tc] = DVE.n
            return DVE.n

        load_half(0, wk_s, 0)
        load_half(1, wk_s, 1)
        pass_idx = 0
        p.op("vector", lambda e_: e_.memset(nmax.ap()[0:1, 0:4], 0.0), DVE)
        chunk_ln_part1(0)

        for tc in range(NTC):
            mats = ((wk_s, "k"), (wq_s, "q"), (wv_s, "v"))
            lnxT_ready = chunk_top(tc)

            for mi, (mat_s, kind) in enumerate(mats):
                sa, sb = (2 * pass_idx) % 3, (2 * pass_idx + 1) % 3
                if not (mi == 2 and tc == NTC - 1):
                    nmat = mats[mi + 1][0] if mi < 2 else wk_s
                    load_half((2 * pass_idx + 2) % 3, nmat, 0)
                if kind in ("k", "q"):
                    staged = (kind == "q") or is_b
                    stsems = stq_sems if kind == "q" else stk_sems
                    sq_sc = {}

                    def emit_norm(e2):
                        p.wait("tensor", SC, sq_sc[e2])
                        if is_b:
                            nb = pb[2 + e2 % 2]
                            p.wait("tensor", DVE, norm_readb[e2 % 2])
                            p.op("tensor", lambda e_, nb=nb, e2=e2: e_.matmul(
                                nb.ap()[0:1, :], ones_colb.ap()[:, :],
                                nsq[:, e2 % 2, :],
                                start=True, stop=True), PE)
                            norm_mm_pe[e2 % 2] = PE.n
                            nmax_update(kind, tc == 0 and e2 == 0, nb.ap())
                            norm_readb[e2 % 2] = DVE.n
                        else:
                            p.op("tensor", lambda e_, e2=e2: e_.matmul(
                                pb[2].ap()[0:1, :], ones_colb.ap()[:, :],
                                nsq[:, e2 % 2, :],
                                start=(e2 == 0), stop=(e2 == 7)), PE)
                            norm_mm_pe[e2 % 2] = PE.n
                            if e2 == 7:
                                nmax_update(kind, tc == 0)

                    for ec in range(8):
                        bank = pb[ec % 2]
                        p.wait("tensor", SC, evac_rel.get(("qk", ec % 2), 0))
                        if ec == 0:
                            p.wait("tensor", wsems[sa], 16 * slot_uses[sa])
                            p.wait("tensor", DVE, lnxT_ready)
                            p.wait("tensor", DVE, norm_read[0])
                        for dc in range(8):
                            wslot = sa if dc < 4 else sb
                            if ec == 0 and dc == 4:
                                p.wait("tensor", wsems[sb], 16 * slot_uses[sb])
                            p.op("tensor", lambda e_, b=bank, ws=wslot, dc=dc, ec=ec: e_.matmul(
                                b.ap()[:, :], wbuf[:, ws, dc % 4, ec * 128:(ec + 1) * 128],
                                lnxT[:, dc, :], start=(dc == 0), stop=(dc == 7)),
                                PE if dc == 7 else None)
                        if ec >= 2:
                            emit_norm(ec - 2)
                        p.wait("scalar", PE)
                        p.op("scalar", lambda e_, b=bank, ec=ec: e_.activation(
                            nsq[:, ec % 2, :], b.ap()[:, :], AF.Square), SC)
                        sq_sc[ec] = SC.n
                        if staged:
                            p.wait("scalar", stq_sems[ec % 2])
                            p.wait("scalar", stk_sems[ec % 2])
                            dest = qstage[:, 0, :] if ec % 2 == 0 else qstage2
                        else:
                            dest = k_sb.ap()[:, ec, tc * TOKC:(tc + 1) * TOKC]
                        p.op("scalar", lambda e_, b=bank, d=dest: e_.activation(
                            d, b.ap()[:, :], AF.Copy), SC)
                        evac_rel[("qk", ec % 2)] = SC.n
                        if staged:
                            dscr = q_scr if kind == "q" else k_scr
                            p.wait("gpsimd", SC)
                            dma("gpsimd",
                                dscr[ec, :, tc * TOKC:(tc + 1) * TOKC],
                                dest, stsems[ec % 2])
                    emit_norm(6)
                    emit_norm(7)
                    if kind == "k" and tc + 1 < NTC:
                        chunk_ln_part1(tc + 1)
                else:
                    for kt in range(4):
                        for oc in range(2):
                            bank = pb[4 + (2 * kt + oc) % 2]
                            p.wait("tensor", SC, evac_rel.get(("v", (2 * kt + oc) % 2), 0))
                            if kt == 0 and oc == 0:
                                p.wait("tensor", wsems[sa], 16 * slot_uses[sa])
                                p.wait("tensor", wsems[sb], 16 * slot_uses[sb])
                                p.wait("tensor", DVE, lnxT_ready)
                            for dc in range(8):
                                wslot = sa if dc < 4 else sb
                                p.op("tensor", lambda e_, b=bank, ws=wslot, dc=dc, kt=kt, oc=oc: e_.matmul(
                                    b.ap()[:, :], lnxT[:, dc, kt * 128:(kt + 1) * 128],
                                    wbuf[:, ws, dc % 4, oc * 512:(oc + 1) * 512],
                                    start=(dc == 0), stop=(dc == 7)),
                                    PE if dc == 7 else None)
                            p.wait("scalar", PE)
                            ktg = tc * 4 + kt
                            if is_b:
                                p.op("scalar", lambda e_, b=bank, ktg=ktg, oc=oc: e_.activation(
                                    v_sbB[:, ktg, oc * 8:(oc + 1) * 8, 0:64],
                                    b.ap()[:, :].rearrange("p (h x) -> p h x", h=8),
                                    AF.Copy), SC)
                            else:
                                p.op("scalar", lambda e_, b=bank, ktg=ktg, oc=oc: e_.activation(
                                    v_sb.ap()[:, ktg, oc * 512:(oc + 1) * 512],
                                    b.ap()[:, :], AF.Copy), SC)
                            evac_rel[("v", (2 * kt + oc) % 2)] = SC.n
                slot_pe_rel[sa] = PE.n
                slot_pe_rel[sb] = PE.n
                if not (mi == 2 and tc == NTC - 1):
                    nmat = mats[mi + 1][0] if mi < 2 else wk_s
                    load_half((2 * pass_idx + 3) % 3, nmat, 1)
                pass_idx += 1

        if is_b:
            p.wait("vector", PE)
            p.op("vector", lambda e_: e_.memset(v_sbB[:, :, :, 64:65], 1.0), DVE)

        p.wait("vector", DVE)
        p.op("vector", lambda e_: e_.tensor_tensor(
            nmax.ap()[0:1, 5:6], nmax.ap()[0:1, 0:1], nmax.ap()[0:1, 1:2],
            ALU.mult), DVE)
        p.wait("scalar", DVE)
        p.op("scalar", lambda e_: e_.activation(
            nmax.ap()[0:1, 6:7], nmax.ap()[0:1, 5:6], AF.Ln,
            bias=epsb.ap()[0:1, 2:3]), SC)
        p.wait("scalar", SC)
        p.op("scalar", lambda e_: e_.activation(
            nmax.ap()[0:1, 7:8], nmax.ap()[0:1, 6:7], AF.Exp, scale=0.5), SC)
        p.wait("tensor", SC)
        p.op("tensor", lambda e_: e_.matmul(
            pb[0].ap()[:, 0:1], ones_rowf.ap()[0:1, :], nmax.ap()[0:1, 7:8],
            start=True, stop=True), PE)
        p.wait("vector", PE)
        p.op("vector", lambda e_, c=(1 if is_b else 0): e_.tensor_scalar(
            negC.ap()[:, c:c + 1], pb[0].ap()[:, 0:1], -1.0, None, ALU.mult), DVE)
        p.wait("tensor", DVE)

    norm_pe_rel = [0]
    norm_mm_pe = {}

    def emit_ln(x_ap, lnx_ap, eps, sq_ap):
        su, ssq, mean, msq, var, rstd, nmb, lnv = (stat.ap()[:, i:i + 1] for i in range(8))
        p.wait("scalar", DVE)
        p.wait("vector", DVE)
        p.op("vector", lambda e_: e_.reduce_sum(su, x_ap, axis=AXX), DVE)
        p.op("scalar", lambda e_: e_.activation(sq_ap, x_ap, AF.Square, accum_out=ssq), SC)
        p.wait("vector", DVE)
        p.op("vector", lambda e_: e_.tensor_scalar(mean, su, 1.0 / DM, None, ALU.mult), DVE)
        p.wait("vector", SC)
        p.wait("vector", DVE)
        p.op("vector", lambda e_: e_.tensor_tensor(msq, mean, mean, ALU.mult), DVE)
        p.wait("vector", DVE)
        p.op("vector", lambda e_: e_.scalar_tensor_tensor(
            var, ssq, 1.0 / DM, msq, ALU.mult, ALU.subtract), DVE)
        p.wait("scalar", DVE)
        eps_ap = epsb.ap()[:, 0:1] if eps == LN_EPS else epsb.ap()[:, 1:2]
        p.op("scalar", lambda e_, b=eps_ap: e_.activation(lnv, var, AF.Ln, bias=b), SC)
        p.wait("scalar", SC)
        p.op("scalar", lambda e_: e_.activation(rstd, lnv, AF.Exp, scale=-0.5), SC)
        p.wait("vector", SC)
        p.op("vector", lambda e_: e_.tensor_scalar(nmb, mean, rstd, -1.0, ALU.mult, ALU.mult), DVE)
        p.wait("vector", DVE)
        p.op("vector", lambda e_: e_.tensor_scalar(
            lnx_ap, x_ap, rstd, nmb, ALU.mult, ALU.add), DVE)

    _st_sc = {}

    def ln2_stats(x_aps, eps, junk_ap, st=None):
        stv = stat.ap() if st is None else st
        n = len(x_aps)
        p.wait("vector", SC, _st_sc.get(0 if st is None else 1, 0))
        for i, x in enumerate(x_aps):
            p.wait("vector", DVE)
            p.op("vector", lambda e_, x=x, i=i, stv=stv: e_.scalar_tensor_tensor(
                junk_ap, x, 1.0, x, ALU.mult, ALU.mult,
                accum_out=stv[:, 2 + i:3 + i]), DVE)
            p.op("vector", lambda e_, x=x, i=i, stv=stv: e_.reduce_sum(
                stv[:, i:i + 1], x, axis=AXX), DVE)
        p.wait("vector", DVE)
        p.op("vector", lambda e_, n=n, stv=stv: e_.tensor_scalar(
            stv[:, 4:4 + n], stv[:, 0:n], 1.0 / DM, None, ALU.mult), DVE)
        p.wait("vector", DVE)
        p.op("vector", lambda e_, n=n, stv=stv: e_.tensor_tensor(
            stv[:, 6:6 + n], stv[:, 4:4 + n], stv[:, 4:4 + n], ALU.mult), DVE)
        p.wait("vector", DVE)
        p.op("vector", lambda e_, n=n, stv=stv: e_.scalar_tensor_tensor(
            stv[:, 6:6 + n], stv[:, 2:2 + n], 1.0 / DM, stv[:, 6:6 + n],
            ALU.mult, ALU.subtract), DVE)
        p.wait("scalar", DVE)
        eps_ap = epsb.ap()[:, 0:1] if eps == LN_EPS else epsb.ap()[:, 1:2]
        p.op("scalar", lambda e_, n=n, b=eps_ap, stv=stv: e_.activation(
            stv[:, 2:2 + n], stv[:, 6:6 + n], AF.Ln, bias=b), SC)
        p.wait("scalar", SC)
        p.op("scalar", lambda e_, n=n, stv=stv: e_.activation(
            stv[:, 6:6 + n], stv[:, 2:2 + n], AF.Exp, scale=-0.5), SC)
        _st_sc[0 if st is None else 1] = SC.n
        p.wait("vector", SC)
        for i in range(n):
            p.op("vector", lambda e_, i=i, stv=stv: e_.tensor_scalar(
                stv[:, 10 + i:11 + i], stv[:, 4 + i:5 + i],
                stv[:, 6 + i:7 + i], -1.0, ALU.mult, ALU.mult), DVE)

    def ln2_affine(i, x_ap, lnx_ap, st=None):
        stv = stat.ap() if st is None else st
        p.wait("vector", DVE)
        p.op("vector", lambda e_, i=i, x=x_ap, o=lnx_ap, stv=stv: e_.tensor_scalar(
            o, x, stv[:, 6 + i:7 + i], stv[:, 10 + i:11 + i],
            ALU.mult, ALU.add), DVE)

    ffn_state = {"lnf_pe": 0, "tb": 0, "h_rel": 0, "out_evac": {}, "gaug_pe": 0}

    def ffn_part1(x2_list, lnf, lnT):
        p.wait("vector", PE, ffn_state["lnf_pe"])
        ln2_stats(x2_list, LN_EPS, lnf)
        for stt in range(2):
            if stt:
                p.wait("vector", PE)
            ln2_affine(stt, x2_list[stt], lnf)
            tbk = ffn_state["tb"]
            ffn_state["tb"] ^= 1
            p.wait("tensor", DVE)
            p.wait("tensor", SC, ffn_state.get("tb_sc", 0))
            for dc in range(8):
                p.op("tensor", lambda e_, dc=dc, t=tbk: e_.transpose(
                    tbv[t][:, dc, :], lnf[:, dc * 128:(dc + 1) * 128], ident[:, :]),
                    PE if dc == 7 else None)
            ffn_state["lnf_pe"] = PE.n
            p.wait("vector", PE)
            p.op("vector", lambda e_, t=tbk, stt=stt: e_.tensor_copy(
                lnT[:, :, stt * 128:(stt + 1) * 128], tbv[t][:, :, :]), DVE)
            ffn_state[("tbrel", tbk)] = DVE.n
        lnT_ready = DVE.n
        p.wait("tensor", DVE, lnT_ready)
        p.wait("tensor", SC)
        for dc in range(8):
            p.op("tensor", lambda e_, dc=dc: e_.matmul(
                pb[4].ap()[:, 0:256], win_sb.ap()[:, dc, :], lnT[:, dc, :],
                start=(dc == 0), stop=(dc == 7)), PE if dc == 7 else None)
        p.wait("vector", PE)
        p.op("vector", lambda e_: e_.tensor_scalar(
            s2w[:, :], pb[4].ap()[:, 0:256], bin_sb.ap()[:, :], None, ALU.add), DVE)
        ffn_state["sevac"] = DVE.n
        ffn_middle()

    def ffn_middle():
        if ffn_state.get("sc_relu"):
            h2 = ffn_state["h2"]
            p.wait("scalar", DVE)
            for j in range(8):
                hb = h_w if j % 2 == 0 else h2
                p.wait("scalar", DVE, ffn_state.get(("hfree", j % 2), 0))
                p.op("scalar", lambda e_, j=j, hb=hb: e_.activation(
                    hb[:, :], s2w[:, :], AF.Relu,
                    bias=b1p.ap()[:, j:j + 1], scale=w1p.ap()[:, j:j + 1]), SC)
                gdst = g2a if j % 2 == 0 else g2b
                p.wait("vector", SC)
                p.wait("vector", DVE)
                if j < 2:
                    p.op("vector", lambda e_, hb=hb, g=gdst, j=j: e_.tensor_scalar(
                        g[:, :], hb[:, :], w2p.ap()[:, j:j + 1], None, ALU.mult), DVE)
                else:
                    p.op("vector", lambda e_, hb=hb, g=gdst, j=j: e_.scalar_tensor_tensor(
                        g[:, :], hb[:, :], w2p.ap()[:, j:j + 1], g[:, :],
                        ALU.mult, ALU.add), DVE)
                ffn_state[("hfree", j % 2)] = DVE.n
        else:
            for j in range(8):
                p.wait("vector", DVE)
                p.op("vector", lambda e_, j=j: e_.tensor_scalar(
                    h_w[:, :], s2w[:, :], w1p.ap()[:, j:j + 1],
                    b1p.ap()[:, j:j + 1], ALU.mult, ALU.add), DVE)
                p.wait("vector", DVE)
                p.op("vector", lambda e_: e_.tensor_scalar(
                    h_w[:, :], h_w[:, :], 0.0, None, ALU.max), DVE)
                gdst = g2a if j % 2 == 0 else g2b
                p.wait("vector", DVE)
                if j < 2:
                    p.op("vector", lambda e_, g=gdst, j=j: e_.tensor_scalar(
                        g[:, :], h_w[:, :], w2p.ap()[:, j:j + 1], None,
                        ALU.mult), DVE)
                else:
                    p.op("vector", lambda e_, g=gdst, j=j: e_.scalar_tensor_tensor(
                        g[:, :], h_w[:, :], w2p.ap()[:, j:j + 1], g[:, :],
                        ALU.mult, ALU.add), DVE)

    def ffn_part1a(x2_list):
        lnfA, lnfB = ffn_state["lnfA"], ffn_state["lnfB"]
        p.wait("vector", PE, ffn_state["lnf_pe"])
        ln2_stats(x2_list, LN_EPS, lnfA[:, :])
        ln2_affine(0, x2_list[0], lnfA[:, :])
        ffn_state[("aff", 0)] = DVE.n
        ln2_affine(1, x2_list[1], lnfB[:, :])
        ffn_state[("aff", 1)] = DVE.n

    def ffn_part1b():
        lnfA, lnfB, lnT = ffn_state["lnfA"], ffn_state["lnfB"], ffn_state["lnT"]
        for stt, lsrc in ((0, lnfA), (1, lnfB)):
            tbk = ffn_state["tb"]
            ffn_state["tb"] ^= 1
            p.wait("tensor", DVE, ffn_state[("aff", stt)])
            p.wait("tensor", DVE, ffn_state.get(("tbrel", tbk), 0))
            p.wait("tensor", SC, ffn_state.get("tb_sc", 0))
            for dc in range(8):
                p.op("tensor", lambda e_, dc=dc, t=tbk, lsrc=lsrc: e_.transpose(
                    tbv[t][:, dc, :], lsrc[:, dc * 128:(dc + 1) * 128],
                    ident[:, :]), PE if dc == 7 else None)
            p.wait("vector", PE)
            p.op("vector", lambda e_, t=tbk, stt=stt, lnT=lnT: e_.tensor_copy(
                lnT[:, :, stt * 128:(stt + 1) * 128], tbv[t][:, :, :]), DVE)
            ffn_state[("tbrel", tbk)] = DVE.n
        ffn_state["lnf_pe"] = PE.n
        lnT_ready = DVE.n
        p.wait("tensor", DVE, lnT_ready)
        p.wait("tensor", SC)
        for dc in range(8):
            p.op("tensor", lambda e_, dc=dc, lnT=lnT: e_.matmul(
                pb[4].ap()[:, 0:256], win_sb.ap()[:, dc, :], lnT[:, dc, :],
                start=(dc == 0), stop=(dc == 7)), PE if dc == 7 else None)
        p.wait("vector", PE)
        p.op("vector", lambda e_: e_.tensor_scalar(
            s2w[:, :], pb[4].ap()[:, 0:256], bin_sb.ap()[:, :], None, ALU.add), DVE)
        ffn_state["sevac"] = DVE.n
        ffn_middle()

    def ffn_part2(xres_list, row0, dst_dram, st_sem, dst_list=None):
        if dst_list is None:
            dst_list = xres_list
        p.wait("tensor", DVE)
        p.op("tensor", lambda e_: e_.matmul(
            pb[5].ap()[0:64, 0:256], stack2.ap()[:, :], g2a[:, :],
            start=True, stop=False))
        p.op("tensor", lambda e_: e_.matmul(
            pb[5].ap()[0:64, 0:256], stack2.ap()[:, :], g2b[:, :],
            start=False, stop=True), PE)
        p.wait("vector", PE)
        p.wait("vector", PE, ffn_state["gaug_pe"])
        p.op("vector", lambda e_: e_.tensor_scalar(
            g_aug.ap()[0:64, :], pb[5].ap()[0:64, 0:256], b2_sb.ap()[:, :],
            None, ALU.add), DVE)
        ffn_state["h_rel"] = DVE.n
        g_ready = DVE.n
        p.wait("tensor", DVE, g_ready)
        for stt in range(2):
            for oc in range(2):
                bank = pb[4 + (2 * stt + oc) % 2]
                p.wait("tensor", DVE, ffn_state["out_evac"].get((2 * stt + oc) % 2, 0))
                p.op("tensor", lambda e_, b=bank, stt=stt, oc=oc: e_.matmul(
                    b.ap()[:, :], g_aug.ap()[:, stt * 128:(stt + 1) * 128],
                    wout_aug.ap()[:, oc * 512:(oc + 1) * 512],
                    start=True, stop=True), PE)
                p.wait("vector", PE)
                p.op("vector", lambda e_, b=bank, stt=stt, oc=oc: e_.tensor_tensor(
                    dst_list[stt][:, oc * 512:(oc + 1) * 512], b.ap()[:, :],
                    xres_list[stt][:, oc * 512:(oc + 1) * 512], ALU.add), DVE)
                ffn_state["out_evac"][(2 * stt + oc) % 2] = DVE.n
        ffn_state["gaug_pe"] = PE.n
        p.wait("gpsimd", DVE)
        for stt in range(2):
            dma("gpsimd", dst_dram[row0 + stt * 128:row0 + (stt + 1) * 128, :],
                dst_list[stt], st_sem)


    def ffn_block(x2_list, xres_list, row0, dst_dram, st_sem, lnf, lnT):
        ffn_part1(x2_list, lnf, lnT)
        ffn_part2(xres_list, row0, dst_dram, st_sem)

    def phase_1b():
        ffn_state["sc_relu"] = True
        ffn_state["h2"] = YT[:, 0:256]
        ffn_state["smm_sc_wait"] = False
        ffn_state["lnfA"] = lnf_tm
        ffn_state["lnfB"] = lnf2b
        ffn_state["lnT"] = lnfT
        exp_done = {}
        ctx_evac = {0: 0}
        q_pe_rel = {}
        pair_dve = {}
        xpairs = [[XW[0], XW[1]], [XW[2], XW[3]]]
        xsems1b = [(sm["s_xa"], sm["s_xb"]), (sm["s_xc"], sm["s_xd"])]
        tb0f = tb[0].ap().bitcast(F32)
        tb1f = tb[1].ap().bitcast(F32)

        def attn_loads(qs):
            q0 = qs * QS1
            par = qs % 2
            qsem = sm["s_q0"] if par == 0 else sm["s_q1"]
            p.wait("sync", PE, q_pe_rel.get(par, 0))
            dma("sync", qb1[:, :, par * QS1:(par + 1) * QS1],
                q_scr.rearrange("c p t -> p c t")[:, :, q0:q0 + QS1], qsem)

        def attn_part(qs, dve_pre=None, post_kc0=None):
            q0 = qs * QS1
            kcnt = 2 * qs + 2
            par = qs % 2
            P = xpairs[par]
            xsems = xsems1b[par]
            qsem = sm["s_q0"] if par == 0 else sm["s_q1"]
            for stt in range(2):
                p.wait("sync", sm["s_stx"], 16 * 2 * (qs - 1) if qs else 0)
                p.wait("sync", DVE, pair_dve.get(par, 0))
                dma("sync", P[stt][:, :], x_in[q0 + stt * 128:q0 + (stt + 1) * 128, :],
                    xsems[stt])
            p.wait("tensor", qsem)
            if dve_pre is None:
                p.wait("tensor", DVE)
            exp_at = {}
            mask_at = {}
            AVB = [[pb[3].ap()[:, :], pb[4].ap()[:, :]],
                   [pb[5].ap()[:, :], tb1f[:, :]]]
            SB = [pb[2].ap()[:, 0:1], tb0f[:, 0:1]]

            def emit_avs(kc):
                for tch in range(2):
                    p.wait("tensor", SC, exp_at[kc])
                    if mask_at[kc]:
                        p.wait("tensor", PL, mask_at[kc])
                    if kc == 0:
                        p.wait("tensor", DVE, ctx_evac[0])
                    att = attT1[:, kc, tch * 128:(tch + 1) * 128]
                    p.op("tensor", lambda e_, att=att, tch=tch, kc=kc, kcnt=kcnt: e_.matmul(
                        SB[tch], att, ones_colb.ap()[:, :],
                        start=(kc == 0), stop=(kc == kcnt - 1)))
                    for dh in range(2):
                        p.op("tensor", lambda e_, att=att, tch=tch, dh=dh, kc=kc, kcnt=kcnt: e_.matmul(
                            AVB[tch][dh], att,
                            v_sb.ap()[:, kc, dh * 512:(dh + 1) * 512],
                            start=(kc == 0), stop=(kc == kcnt - 1)),
                            PE if (kc == kcnt - 1 and tch == 1 and dh == 1) else None)

            for kc in range(kcnt):
                bank = pb[kc % 2]
                p.wait("tensor", SC, exp_done.get(kc % 2, 0))
                for ec in range(8):
                    p.op("tensor", lambda e_, b=bank, ec=ec, kc=kc, par=par: e_.matmul(
                        b.ap()[:, 0:QS1], k_sb.ap()[:, ec, kc * 128:(kc + 1) * 128],
                        qb1[:, ec, par * QS1:(par + 1) * QS1],
                        start=(ec == 0), stop=(ec == 7)), PE if ec == 7 else None)
                p.wait("scalar", PE)
                p.op("scalar", lambda e_, b=bank, kc=kc: e_.activation(
                    attT1[:, kc, :], b.ap()[:, 0:QS1], AF.Exp,
                    bias=negC.ap()[:, 0:1]), SC)
                exp_done[kc % 2] = SC.n
                exp_at[kc] = SC.n
                if kc >= kcnt - 2:
                    p.wait("gpsimd", SC)
                    p.op("gpsimd", lambda e_, kc=kc, q0=q0: e_.affine_select(
                        out=attT1[:, kc, :], in_=attT1[:, kc, :],
                        compare_op=ALU.is_ge, fill=0.0,
                        base=q0 - kc * 128, pattern=[[1, QS1]],
                        channel_multiplier=-1), PL)
                    mask_at[kc] = PL.n
                else:
                    mask_at[kc] = 0
                if kc == 1 and post_kc0 is not None:
                    post_kc0()
                if kc >= 2:
                    emit_avs(kc - 2)
            if kcnt >= 2:
                emit_avs(kcnt - 2)
            emit_avs(kcnt - 1)
            p.wait("scalar", PE)
            for tch in range(2):
                p.op("scalar", lambda e_, tch=tch: e_.activation(
                    stat.ap()[:, 12 + tch:13 + tch], SB[tch], AF.Ln,
                    bias=epsb.ap()[:, 2:3]), SC)
            p.wait("scalar", SC)
            for tch in range(2):
                p.op("scalar", lambda e_, tch=tch: e_.activation(
                    stat.ap()[:, 14 + tch:15 + tch],
                    stat.ap()[:, 12 + tch:13 + tch], AF.Exp, scale=-1.0), SC)
            rc = SC.n
            for tch in range(2):
                p.wait("vector", xsems[tch])
                p.wait("vector", SC, rc)
                p.wait("vector", PE)
                p.wait("vector", DVE)
                for dh in range(2):
                    p.op("vector", lambda e_, b=AVB[tch][dh], tch=tch, dh=dh, P=P: e_.scalar_tensor_tensor(
                        P[tch][:, dh * 512:(dh + 1) * 512], b,
                        stat.ap()[:, 14 + tch:15 + tch],
                        P[tch][:, dh * 512:(dh + 1) * 512],
                        ALU.mult, ALU.add), DVE)
            ctx_evac[0] = DVE.n
            ffn_state["tb_sc"] = rc
            pair_dve[par] = DVE.n
            q_pe_rel[par] = PE.n
            return DVE.n

        attn_loads(0)
        attn_part(0)
        P0 = xpairs[0]
        ffn_part1a([P0[0][:, :], P0[1][:, :]])
        attn_loads(1)
        for qs in range(1, NQ1):
            dve_pre = DVE.n

            def hook():
                ffn_part1b()
                ctx_evac[0] = ffn_state["sevac"]

            attn_part(qs, dve_pre, post_kc0=hook)
            Pp = xpairs[(qs - 1) % 2]
            ffn_part2([Pp[0][:, :], Pp[1][:, :]], (qs - 1) * QS1, xs1,
                      sm["s_stx"])
            ctx_evac[0] = DVE.n
            pair_dve[(qs - 1) % 2] = DVE.n
            Pc = xpairs[qs % 2]
            ffn_part1a([Pc[0][:, :], Pc[1][:, :]])
            if qs + 1 < NQ1:
                attn_loads(qs + 1)
        ffn_part1b()
        Pl = xpairs[(NQ1 - 1) % 2]
        ffn_part2([Pl[0][:, :], Pl[1][:, :]], (NQ1 - 1) * QS1, xs1,
                  sm["s_stx"])

    xw_dve_rel = {}

    def phase_2b():
        exp_done = {}
        q_pe_rel = [0]
        ksems3 = [sm["s_k0"], sm["s_k1"], sm["s_k2"]]
        kuse3 = [0, 0, 0]
        kpe_rel3 = {0: 0, 1: 0, 2: 0}
        kg = [0]
        kslot_of = {}
        ktgt = {}
        ctxn_dve = [0]
        opj23_free = [0]
        pair_rel = {0: 0, 1: 0}
        pendf = []
        wsb = k_sb.ap()[:, :, 0:1024]
        AP2 = [attb2[:, 0:2, :], attb2[:, 2:4, :],
               k_sb.ap()[:, 5, 2048:3072].rearrange("p (a b) -> p a b", a=2)]
        Y2 = [k_sb.ap()[:, j, 1024:3072].bitcast(F32) for j in range(4)]
        Jnk = k_sb.ap()[:, 4, 1024:3072].bitcast(F32)
        stat2 = k_sb.ap()[:, 5, 1024:1056].bitcast(F32)
        kstr3 = k_sb.ap()[:, 6, 1024:3072]
        kslots = [kstr2[:, 0, :], kstr2[:, 1, :], kstr3]
        NORM1 = k_sb.ap()[:, 7, 2048:4096].bitcast(F32)
        sts = [None, stat2]
        ffn_state["sc_relu"] = False
        ffn_state["smm_sc_wait"] = True
        ffn_state["h2"] = k_sb.ap()[:, 5, 1536:2048].bitcast(F32)
        ffn_state["lnfA"] = lnf_tm2
        ffn_state["lnfB"] = lnf2
        ffn_state["lnT"] = lnfT2
        pairs = [[XW[2], XW[3]], [XW[0], XW[1]]]
        dma("sync", wsb[:, 0:4, :],
            wscr["woB"].rearrange("c p d -> p c d")[:, 0:4, :], sm["s_wb0"])
        dma("gpsimd", wsb[:, 4:8, :],
            wscr["woB"].rearrange("c p d -> p c d")[:, 4:8, :], sm["s_wb1"])
        wsb_wait = [True]
        for qc in range(NQ2):
            q0 = qc * QC2
            kcnt = 4 * qc + 4
            npiece = (kcnt * 128 + 2047) // 2048
            p.wait("sync", PE, q_pe_rel[0])
            dma("sync", qb2[:, :, :],
                q_scr.rearrange("c p t -> p c t")[:, :, q0:q0 + QC2], sm["s_q0"])
            p.wait("tensor", sm["s_q0"])
            p.wait("tensor", DVE, ctxn_dve[0])

            xs_sems = [(sm["s_xa"], sm["s_xb"]), (sm["s_xc"], sm["s_xd"])]

            def ep_loads(sc2):
                rows0 = q0 + sc2 * 256
                for stt in range(2):
                    p.wait("sync", sm["s_stout"], 16 * (4 * qc - 2 + 2 * sc2))
                    p.wait("sync", DVE, pair_rel[sc2])
                    dma("sync", pairs[sc2][stt][:, :],
                        xs1[rows0 + stt * 128:rows0 + (stt + 1) * 128, :],
                        xs_sems[sc2][stt])

            def ep_stats(sc2):
                for stt in range(2):
                    p.wait("vector", xs_sems[sc2][stt])
                ln2_stats([pairs[sc2][0][:, :], pairs[sc2][1][:, :]], LN_EPS,
                          Jnk, st=sts[sc2])

            def issue_k_loads(pr):
                for pc in range(npiece):
                    sl = kg[0] % 3
                    kg[0] += 1
                    kslot_of[(qc, pr, pc)] = sl
                    p.wait("sync", PE, kpe_rel3[sl])
                    n_tok = min(2048, kcnt * 128 - pc * 2048)
                    dma("sync", kslots[sl][:, 0:n_tok],
                        k_scr[pr, :, pc * 2048:pc * 2048 + n_tok], ksems3[sl])
                    kuse3[sl] += 1
                    ktgt[(qc, pr, pc)] = 16 * kuse3[sl]

            for pr in range(8):
                if pr == 0:
                    issue_k_loads(0)
                def emit_av(kc, pos):
                    for hh in range(2):
                        p.wait("tensor", SC, exp_at[kc])
                        if mask_at[kc]:
                            p.wait("tensor", PL, mask_at[kc])
                        if pos == 0:
                            p.wait("tensor", DVE, ctxn_dve[0])
                        h = 2 * pr + hh
                        att = AP2[pos % 3][:, hh, :]
                        p.op("tensor", lambda e_, hh=hh, kc=kc, h=h, att=att, pos=pos, kcnt=kcnt: e_.matmul(
                            pb[4 + hh].ap()[0:65, :], v_sbB[:, kc, h, :],
                            att, start=(pos == 0), stop=(pos == kcnt - 1)),
                            PE if hh == 1 else None)

                exp_at = {}
                mask_at = {}
                pend = []
                order = list(range(kcnt))
                seen_pc = set()
                for pos, kc in enumerate(order):
                    pc = (kc * 128) // 2048
                    koff = kc * 128 - pc * 2048
                    half = pos % 2
                    ksl = kslot_of[(qc, pr, pc)]
                    if pc not in seen_pc:
                        seen_pc.add(pc)
                        p.wait("tensor", ksems3[ksl], ktgt[(qc, pr, pc)])
                    if pos == 1:
                        p.wait("tensor", DVE, max(ctxn_dve[0], opj23_free[0]))
                    pe_hh = {}
                    for hh in range(2):
                        bank = pb[2 * half + hh]
                        p.wait("tensor", SC, exp_done.get((half, hh), 0))
                        p.op("tensor", lambda e_, b=bank, hh=hh, ks=kslots[ksl], koff=koff, pr=pr: e_.matmul(
                            b.ap()[:, :], ks[hh * 64:(hh + 1) * 64, koff:koff + 128],
                            qb2[hh * 64:(hh + 1) * 64, pr, :],
                            start=True, stop=True, tile_position=(hh * 64, 0)), PE)
                        pe_hh[hh] = PE.n
                    for hh in range(2):
                        p.wait("scalar", PE, pe_hh[hh])
                        if pos % 3 == 0 and hh == 0:
                            p.wait("scalar", sm["s_cst"])
                        p.op("scalar", lambda e_, ap=AP2[pos % 3], half=half, hh=hh: e_.activation(
                            ap[:, hh, :],
                            psQ.ap()[:, half * 1024 + hh * 512:half * 1024 + (hh + 1) * 512],
                            AF.Exp, bias=negC.ap()[:, 1:2]), SC)
                        exp_done[(half, hh)] = SC.n
                    exp_at[kc] = SC.n
                    if kc >= kcnt - 4:
                        p.wait("gpsimd", SC)
                        for hh in range(2):
                            p.op("gpsimd", lambda e_, a_=AP2[pos % 3][:, hh, :], kc=kc, q0=q0: e_.affine_select(
                                out=a_, in_=a_,
                                compare_op=ALU.is_ge, fill=0.0,
                                base=q0 - kc * 128, pattern=[[1, QC2]],
                                channel_multiplier=-1), PL)
                        mask_at[kc] = PL.n
                    else:
                        mask_at[kc] = 0
                    if len(pend) == 2:
                        emit_av(*pend.pop(0))
                    pend.append((kc, pos))
                for pk, pp in pend:
                    emit_av(pk, pp)
                for pc in range(npiece):
                    kpe_rel3[kslot_of[(qc, pr, pc)]] = PE.n
                if pr < 7:
                    issue_k_loads(pr + 1)
                p.wait("scalar", PE)
                p.wait("scalar", DVE, opj23_free[0])
                p.op("scalar", lambda e_: e_.activation(
                    YT[64:65, 0:512], pb[4].ap()[64:65, :], AF.Ln,
                    bias=epsb.ap()[64:65, 2:3]), SC)
                p.op("scalar", lambda e_: e_.activation(
                    NORM1[64:65, 0:512], pb[5].ap()[64:65, :], AF.Ln,
                    bias=epsb.ap()[64:65, 2:3]), SC)
                p.wait("scalar", SC)
                p.op("scalar", lambda e_: e_.activation(
                    YT[64:65, 512:1024], YT[64:65, 0:512], AF.Exp,
                    scale=-1.0), SC)
                e0 = SC.n
                p.op("scalar", lambda e_: e_.activation(
                    NORM1[64:65, 512:1024], NORM1[64:65, 0:512], AF.Exp,
                    scale=-1.0), SC)
                e1 = SC.n
                p.wait("tensor", SC, e0)
                p.wait("tensor", DVE)
                p.op("tensor", lambda e_: e_.matmul(
                    pb[2].ap()[0:64, :], ones_rowf.ap()[64:65, 0:64],
                    YT[64:65, 512:1024], start=True, stop=True), PE)
                b0 = PE.n
                p.wait("tensor", SC, e1)
                p.op("tensor", lambda e_: e_.matmul(
                    pb[3].ap()[0:64, :], ones_rowf.ap()[64:65, 0:64],
                    NORM1[64:65, 512:1024], start=True, stop=True), PE)
                b1 = PE.n
                p.wait("vector", PE, b0)
                p.wait("vector", DVE)
                p.op("vector", lambda e_: e_.tensor_copy(
                    recipB[0:64, 0:QC2], pb[2].ap()[0:64, :]), DVE)
                p.wait("vector", PE, b1)
                p.wait("vector", DVE)
                p.op("vector", lambda e_: e_.tensor_copy(
                    NORM1[0:64, 0:512], pb[3].ap()[0:64, :]), DVE)
                p.wait("vector", DVE)
                p.op("vector", lambda e_, pr=pr: e_.tensor_tensor(
                    ctxN2[0:64, pr, :], pb[4].ap()[0:64, :],
                    recipB[0:64, 0:QC2], ALU.mult), DVE)
                p.wait("vector", sm["s_cst"])
                p.wait("vector", DVE)
                p.op("vector", lambda e_, pr=pr: e_.tensor_tensor(
                    attb2[0:64, 0, :], pb[5].ap()[0:64, :],
                    NORM1[0:64, 0:512], ALU.mult), DVE)
                p.wait("gpsimd", DVE)
                dma("gpsimd", ctxN2[64:128, pr, :], attb2[0:64, 0, :],
                    sm["s_cst"])
                ctxn_dve[0] = DVE.n
                if pr < len(pendf):
                    fn, kind, sc2i = pendf[pr]
                    fn()
                    if kind == "b":
                        ctxn_dve[0] = max(ctxn_dve[0], ffn_state["sevac"])
                    elif kind == "2":
                        ctxn_dve[0] = DVE.n
                        pair_rel[sc2i] = DVE.n
                if pr == 3:
                    ep_loads(0)
                if pr == 5:
                    ep_stats(0)
                if pr == 7:
                    ep_loads(1)
            pendf = []
            ctx_ready = DVE.n
            q_pe_rel[0] = PE.n
            p.wait("tensor", sm["s_cst"])
            if debug:
                p.wait("gpsimd", DVE, ctx_ready)
                p.wait("gpsimd", sm["s_cst"])
                dma("gpsimd", dbg2[qc], ctxN2[:, :, :], sm["s_dbg"])
                p.wait("tensor", sm["s_dbg"])
            sc_pre = SC.n
            p.wait("tensor", DVE, ctx_ready)
            ep_stats(1)

            def ep_oproj_pass(sc2, stt, bks, dve_gate):
                for hc in range(8):
                    if hc == 0:
                        if wsb_wait[0]:
                            p.wait("tensor", sm["s_wb0"])
                            p.wait("tensor", sm["s_wb1"])
                            wsb_wait[0] = False
                        p.wait("tensor", SC, sc_pre)
                        if dve_gate:
                            p.wait("tensor", DVE, dve_gate)
                    for oc in range(2):
                        p.op("tensor", lambda e_, b=bks[oc], hc=hc, stt=stt, oc=oc, sc2=sc2: e_.matmul(
                            b.ap()[:, :],
                            ctxN2[:, hc, sc2 * 256 + stt * 128:sc2 * 256 + (stt + 1) * 128],
                            wsb[:, hc, oc * 512:(oc + 1) * 512],
                            start=(hc == 0), stop=(hc == 7)),
                            PE if (hc == 7 and oc == 1) else None)
                return PE.n

            def ep_adds(sc2, stt, bks, ydst, pass_pe):
                ln2_affine(stt, pairs[sc2][stt][:, :], YT[:, :], st=sts[sc2])
                p.wait("vector", PE, pass_pe)
                p.wait("vector", DVE)
                for oc in range(2):
                    p.op("vector", lambda e_, b=bks[oc], oc=oc, ydst=ydst: e_.tensor_tensor(
                        ydst[:, oc * 512:(oc + 1) * 512], b.ap()[:, :],
                        YT[:, oc * 512:(oc + 1) * 512], ALU.add), DVE)
                return DVE.n

            def ep_lnb(sc2):
                rows0 = q0 + sc2 * 256
                ys = (Y2[2 * sc2], Y2[2 * sc2 + 1])
                ln2_stats([ys[0][:, :], ys[1][:, :]], LNB_EPS, Jnk,
                          st=sts[sc2])
                for stt in range(2):
                    y = ys[stt]
                    ln2_affine(stt, y[:, :], y[:, :], st=sts[sc2])
                    p.wait("vector", DVE)
                    p.op("vector", lambda e_, y=y: e_.tensor_tensor(
                        y[:, :], y[:, :], gamB[:, :], ALU.mult), DVE)
                    p.wait("vector", DVE)
                    p.op("vector", lambda e_, y=y: e_.tensor_tensor(
                        y[:, :], y[:, :], betB[:, :], ALU.add), DVE)
                    p.wait("vector", DVE)
                    p.op("vector", lambda e_, y=y, x=pairs[sc2][stt]: e_.tensor_tensor(
                        x[:, :], x[:, :], y[:, :], ALU.add), DVE)
                    if debug:
                        p.wait("gpsimd", DVE)
                        dma("gpsimd", dbg[rows0 + stt * 128:rows0 + (stt + 1) * 128, :],
                            pairs[sc2][stt][:, :], sm["s_dbg"])
                p.wait("vector", sm["s_dbg"])

            a0 = ep_oproj_pass(0, 0, (pb[4], pb[5]), None)
            b0 = ep_oproj_pass(0, 1, (pb[2], pb[3]), None)
            yA0 = ep_adds(0, 0, (pb[4], pb[5]), Y2[0], a0)
            yB0 = ep_adds(0, 1, (pb[2], pb[3]), Y2[1], b0)
            a1 = ep_oproj_pass(1, 0, (pb[4], pb[5]), yA0)
            b1 = ep_oproj_pass(1, 1, (pb[2], pb[3]), yB0)
            ep_adds(1, 0, (pb[4], pb[5]), Y2[2], a1)
            yB1 = ep_adds(1, 1, (pb[2], pb[3]), Y2[3], b1)
            ctxn_dve[0] = DVE.n
            opj23_free[0] = yB1
            pendf = [
                (lambda f=ep_lnb: f(0), "a", None),
                (lambda: ffn_part1a([XW[2][:, :], XW[3][:, :]]), "a", None),
                (ffn_part1b, "b", None),
                (lambda q0=q0: ffn_part2([XW[2][:, :], XW[3][:, :]], q0, out,
                                         sm["s_stout"]), "2", 0),
                (lambda f=ep_lnb: f(1), "a", None),
                (lambda: ffn_part1a([XW[0][:, :], XW[1][:, :]]), "a", None),
                (ffn_part1b, "b", None),
                (lambda q0=q0: ffn_part2([XW[0][:, :], XW[1][:, :]], q0 + 256,
                                         out, sm["s_stout"]), "2", 1),
            ]
        for fn, _, _ in pendf:
            fn()

    phase_a(False)
    barrier()
    phase_1b()
    barrier()
    phase_a(True)
    barrier()
    phase_2b()
    p.wait("sync", sm["s_stout"])
    p.wait("sync", sm["s_stx"])
    p.wait("sync", sm["s_dbg"])
    p.wait("sync", sm["s_cst"])

    p.finalize()
    with nc.Block() as block:
        @block.sync
        def _(eng):
            for fn in p.streams["sync"]:
                fn(eng)

        @block.gpsimd
        def _(eng):
            for fn in p.streams["gpsimd"]:
                fn(eng)

        @block.tensor
        def _(eng):
            for fn in p.streams["tensor"]:
                fn(eng)

        @block.vector
        def _(eng):
            for fn in p.streams["vector"]:
                fn(eng)

        @block.scalar
        def _(eng):
            for fn in p.streams["scalar"]:
                fn(eng)

    ctx_stack.close()
    return nc


_CACHE = {}


def kernel(L0_attn_a_in_features, L0_attn_a_w_q, L0_attn_a_w_k, L0_attn_a_w_v,
           L0_attn_a_w_o, L0_attn_b_Wq, L0_attn_b_Wk, L0_attn_b_Wv, L0_attn_b_Wo,
           L0_attn_b_gamma, L0_attn_b_beta, L0_ffn_Win, L0_ffn_bin_, L0_ffn_W1,
           L0_ffn_b1, L0_ffn_W2, L0_ffn_b2, L0_ffn_Wout, L0_ffn_bout):
    x = np.ascontiguousarray(np.asarray(L0_attn_a_in_features, dtype=np.float32))
    B = x.shape[0]
    if "nc" not in _CACHE:
        _CACHE["nc"] = build_nc()
    nc = _CACHE["nc"]
    c = lambda a: np.ascontiguousarray(np.asarray(a, dtype=np.float32))
    w = dict(wq_a=c(L0_attn_a_w_q), wk_a=c(L0_attn_a_w_k), wv_a=c(L0_attn_a_w_v),
             wo_a=c(L0_attn_a_w_o), wq_b=c(L0_attn_b_Wq), wk_b=c(L0_attn_b_Wk),
             wv_b=c(L0_attn_b_Wv), wo_b=c(L0_attn_b_Wo), gam=c(L0_attn_b_gamma),
             bet=c(L0_attn_b_beta), f_win=c(L0_ffn_Win), f_bin=c(L0_ffn_bin_),
             f_w1=c(L0_ffn_W1), f_b1=c(L0_ffn_b1), f_w2=c(L0_ffn_W2),
             f_b2=c(L0_ffn_b2), f_wout=c(L0_ffn_Wout), f_bout=c(L0_ffn_bout))
    in_maps = [dict(w, x=np.ascontiguousarray(x[b])) for b in range(B)]
    res = run_bass_kernel_spmd(nc, in_maps, core_ids=list(range(B)))
    return np.stack([res.results[b]["out"] for b in range(B)], axis=0)


# version 63
# speedup vs baseline: 1.0234x; 1.0234x over previous
import numpy as np
import concourse.bass as bass
import concourse.mybir as mybir
from concourse.bass_utils import run_bass_kernel_spmd
from contextlib import ExitStack

F32 = mybir.dt.float32
BF16 = mybir.dt.bfloat16
AF = mybir.ActivationFunctionType
ALU = mybir.AluOpType
AXX = mybir.AxisListType.X

S, DM = 4096, 1024
NH = 16
NTC, TOKC = 8, 512
NQ1, QS1 = 16, 256
NQ2, QC2 = 8, 512
LN_EPS = 1e-5
LNB_EPS = 1e-3

ENGS = ("sync", "gpsimd", "tensor", "vector", "scalar")


class Sem:
    __slots__ = ("h", "n")

    def __init__(self, h):
        self.h = h
        self.n = 0


class Prog:

    def __init__(self):
        self.streams = {k: [] for k in ENGS}

    def op(self, eng, build, sem=None, inc=1):
        if sem is None:
            self.streams[eng].append(("op", build, None, 0, 0))
        else:
            sem.n += inc
            self.streams[eng].append(("op", build, sem, inc, sem.n))

    def wait(self, eng, sem, val=None):
        v = sem.n if val is None else val
        if v <= 0:
            return
        self.streams[eng].append(("wait", sem, v))

    def finalize(self):
        awaited = {}
        for eng in ENGS:
            for ent in self.streams[eng]:
                if ent[0] == "wait":
                    _, sem, v = ent
                    awaited.setdefault(id(sem), set()).add(v)
        rank = {}
        for eng in ENGS:
            for ent in self.streams[eng]:
                if ent[0] == "op" and ent[2] is not None and ent[3] == 1:
                    sid = id(ent[2])
                    if sid not in rank:
                        rank[sid] = {v: i + 1 for i, v in
                                     enumerate(sorted(awaited.get(sid, ())))}
        out = {k: [] for k in ENGS}
        for eng in ENGS:
            seen = {}
            for ent in self.streams[eng]:
                if ent[0] == "op":
                    _, build, sem, inc, cnt = ent
                    if sem is None:
                        out[eng].append(build)
                    elif inc != 1:
                        out[eng].append(lambda e, b=build, h=sem.h, k=inc:
                                        b(e).then_inc(h, k))
                    else:
                        r = rank[id(sem)].get(cnt)
                        if r is None:
                            out[eng].append(build)
                        else:
                            out[eng].append(lambda e, b=build, h=sem.h:
                                            b(e).then_inc(h, 1))
                else:
                    _, sem, v = ent
                    sid = id(sem)
                    if sid in rank:
                        vv = rank[sid].get(v)
                        assert vv is not None
                    else:
                        vv = v
                    if seen.get(sid, 0) >= vv:
                        continue
                    seen[sid] = vv
                    out[eng].append(lambda e, h=sem.h, v=vv: e.wait_ge(h, v))
        self.streams = out


def build_nc(debug=False):
    nc = bass.Bass("TRN2", target_bir_lowering=False)

    x_in = nc.dram_tensor("x", [S, DM], F32, kind="ExternalInput").ap()
    names = ["wq_a", "wk_a", "wv_a", "wo_a", "wo_b"]
    dws = {n: nc.dram_tensor(n, [DM, DM], F32, kind="ExternalInput").ap() for n in names}
    for n in ("wq_b", "wk_b", "wv_b"):
        dws[n] = nc.dram_tensor(n, [NH, DM, 64], F32, kind="ExternalInput").ap()
    gam = nc.dram_tensor("gam", [DM], F32, kind="ExternalInput").ap()
    bet = nc.dram_tensor("bet", [DM], F32, kind="ExternalInput").ap()
    f_win = nc.dram_tensor("f_win", [DM, 64], F32, kind="ExternalInput").ap()
    f_bin = nc.dram_tensor("f_bin", [64], F32, kind="ExternalInput").ap()
    f_w1 = nc.dram_tensor("f_w1", [64, 16], F32, kind="ExternalInput").ap()
    f_b1 = nc.dram_tensor("f_b1", [64, 16], F32, kind="ExternalInput").ap()
    f_w2 = nc.dram_tensor("f_w2", [64, 16], F32, kind="ExternalInput").ap()
    f_b2 = nc.dram_tensor("f_b2", [64], F32, kind="ExternalInput").ap()
    f_wout = nc.dram_tensor("f_wout", [64, DM], F32, kind="ExternalInput").ap()
    f_bout = nc.dram_tensor("f_bout", [DM], F32, kind="ExternalInput").ap()
    out = nc.dram_tensor("out", [S, DM], F32, kind="ExternalOutput").ap()
    dbg = nc.dram_tensor("dbg", [S, DM], F32, kind="Internal" if not debug else "ExternalOutput").ap()
    dbg2 = nc.dram_tensor("dbg2", [NQ2, 128, 8, QC2], BF16, kind="Internal" if not debug else "ExternalOutput").ap()

    xs1 = nc.dram_tensor("xs1", [S, DM], F32, kind="Internal").ap()
    q_scr = nc.dram_tensor("q_scr", [8, 128, S], BF16, kind="Internal").ap()
    k_scr = nc.dram_tensor("k_scr", [8, 128, S], BF16, kind="Internal").ap()
    wscr = {n: nc.dram_tensor(n + "_s", [8, 128, DM], BF16, kind="Internal").ap()
            for n in ("wqT", "wkT", "wvT", "woT", "wqB", "wkB", "wvB", "woB")}

    p = Prog()
    ctx_stack = ExitStack()
    e = ctx_stack.enter_context

    k_sb = e(nc.sbuf_tensor("k_sb", [128, 8, S], BF16))
    v_sb = e(nc.sbuf_tensor("v_sb", [128, 32, 1040], BF16))
    v_sbB = v_sb.ap().rearrange("p a (h c) -> p a h c", h=16)

    ident = e(nc.sbuf_tensor("ident", [128, 128], BF16))
    ones_colb = e(nc.sbuf_tensor("ones_colb", [128, 1], BF16))
    ones_rowf = e(nc.sbuf_tensor("ones_rowf", [128, 128], F32))
    negC = e(nc.sbuf_tensor("negC", [128, 2], F32))
    nmax = e(nc.sbuf_tensor("nmax", [1, 8], F32))
    stat = e(nc.sbuf_tensor("stat", [128, 16], F32))
    win_sb = e(nc.sbuf_tensor("win_sb", [128, 8, 128], BF16))
    stack2 = e(nc.sbuf_tensor("stack2", [128, 64], F32))
    bin_sb = e(nc.sbuf_tensor("bin_sb", [128, 1], F32))
    w1p = e(nc.sbuf_tensor("w1p", [128, 8], F32))
    b1p = e(nc.sbuf_tensor("b1p", [128, 8], F32))
    w2p = e(nc.sbuf_tensor("w2p", [128, 8], F32))
    b2_sb = e(nc.sbuf_tensor("b2_sb", [64, 1], F32))
    wout_aug = e(nc.sbuf_tensor("wout_aug", [65, DM], BF16))
    g_aug = e(nc.sbuf_tensor("g_aug", [65, 256], BF16))
    epsb = e(nc.sbuf_tensor("epsb", [128, 3], F32))

    arena_b = e(nc.sbuf_tensor("arena_b", [128, 39 * 512], BF16))
    ab = arena_b.ap()
    arena_f = e(nc.sbuf_tensor("arena_f", [128, 34 * 256], F32))
    af = arena_f.ap()

    def view(base_ap, unit, off_kib, *shape):
        n = 1
        for s_ in shape:
            n *= s_
        o = int(off_kib * unit)
        flat = base_ap[:, o:o + n]
        if len(shape) == 1:
            return flat
        spec = "p (" + " ".join(f"d{i}" for i in range(len(shape))) + ") -> p " + \
               " ".join(f"d{i}" for i in range(len(shape)))
        return flat.rearrange(spec, **{f"d{i}": shape[i] for i in range(len(shape))})

    def bv(off, *shape):
        return view(ab, 512, off, *shape)

    def fv(off, *shape):
        return view(af, 256, off, *shape)

    wbuf = bv(0, 3, 4, DM)
    lnx_tm = bv(24, 2, DM)
    lnxT = bv(28, 8, TOKC)
    qstage = bv(36, 1, TOKC)
    nsq = bv(37, 2, TOKC)
    attT1 = bv(0, 32, QS1)
    qb1 = bv(16, 8, TOKC)
    wostr = bv(24, 2, DM)
    lnf2b = bv(24, DM)
    lnf_tm = bv(28, DM)
    lnfT = bv(30, 8, 256)
    ctxN1 = bv(34, 8, QS1)
    kstr2 = bv(0, 2, 2048)
    attb2 = bv(8, 4, QC2)
    ctxN2 = bv(12, 8, QC2)
    qb2 = bv(20, 8, QC2)
    wostr2 = bv(28, 2, DM)
    lnf2 = bv(28, DM)
    lnf_tm2 = bv(32, DM)
    lnfT2 = bv(34, 8, 256)

    xin = fv(0, 2, DM)
    xin2 = fv(24, 2, DM)
    wstage = fv(8, 4, DM)
    gamB = fv(8, DM)
    betB = fv(12, DM)
    XW = [fv(0, DM), fv(4, DM), fv(16, DM), fv(20, DM)]
    qstage2 = af[:, 16 * 256:16 * 256 + 256].bitcast(BF16)
    YT = fv(24, DM)
    recipB = fv(28, 512)
    s2w = fv(30, 256)
    h_w = fv(31, 256)
    g2a = fv(32, 256)
    g2b = fv(33, 256)

    psQ = e(nc.psum_tensor("psQ", [128, 2048], F32))
    pc4 = e(nc.psum_tensor("pc4", [128, 512], F32))
    pc5 = e(nc.psum_tensor("pc5", [128, 512], F32))
    tb = [e(nc.psum_tensor(f"tb{i}", [128, 1024], BF16)) for i in range(2)]
    tbv = [t.ap().rearrange("p (a b) -> p a b", a=8) for t in tb]

    class APH:
        __slots__ = ("_a",)
        def __init__(self, a):
            self._a = a
        def ap(self):
            return self._a
    pb = [APH(psQ.ap()[:, i * 512:(i + 1) * 512]) for i in range(4)] +          [APH(pc4.ap()), APH(pc5.ap())]

    sem_names = ["s_pe", "s_dve", "s_sc", "s_pl",
                 "s_x0", "s_x1", "s_w0", "s_w1", "s_w2",
                 "s_q0", "s_q1", "s_k0", "s_k1", "s_wo0", "s_wo1",
                 "s_m0", "s_stq0", "s_stq1", "s_stk0", "s_stk1",
                 "s_stx", "s_stout", "s_stw", "s_xa", "s_xb", "s_dbg", "s_cst",
                 "s_pl0", "s_pl1", "s_pl2", "s_pl3", "s_ps0", "s_ps1",
                 "s_ph0", "s_ph1", "s_wb0", "s_wb1", "s_xc", "s_xd", "s_k2",
                 "s_xe", "s_wc0", "s_wc1"]
    sm = {n: Sem(e(nc.semaphore(n))) for n in sem_names}
    PE, DVE, SC, PL = sm["s_pe"], sm["s_dve"], sm["s_sc"], sm["s_pl"]
    DMA_SEMS = [sm[n] for n in sem_names[4:]]

    def dma(eng, out_ap, in_ap, sem):
        p.op(eng, lambda e_, o=out_ap, i=in_ap: e_.dma_start(out=o, in_=i), sem, 16)

    def barrier():
        for eng in ENGS:
            for s_ in (PE, DVE, SC, PL):
                p.wait(eng, s_)
            for s_ in DMA_SEMS:
                p.wait(eng, s_)

    from concourse.hw_specs import get_activation_tables
    set_id = None
    for idx, (nmset, funcs) in enumerate(get_activation_tables(nc.m.arch).items()):
        if nmset == "natural_log_exp_and_others":
            set_id = idx
    assert set_id is not None

    p.op("scalar", lambda e_: e_.add_instruction(mybir.InstLoadActFuncSet(
        name=e_.bass.get_next_instruction_name(), act_func_set_id=set_id,
        ins=[], outs=[])), SC)

    p.op("vector", lambda e_: e_.memset(ident[:, :], 0.0), DVE)
    p.wait("gpsimd", DVE)
    p.op("gpsimd", lambda e_: e_.affine_select(
        out=ident[:, :], in_=ident[:, :], compare_op=ALU.not_equal, fill=1.0,
        base=0, pattern=[[-1, 128]], channel_multiplier=1), PL)
    p.op("vector", lambda e_: e_.memset(ones_colb.ap()[:, :], 1.0), DVE)
    p.op("vector", lambda e_: e_.memset(ones_rowf.ap()[:, :], 1.0), DVE)
    p.op("vector", lambda e_: e_.memset(nmax.ap()[:, :], 0.0), DVE)
    p.op("vector", lambda e_: e_.memset(g_aug.ap()[64:65, :], 1.0), DVE)
    p.op("vector", lambda e_: e_.memset(v_sb.ap()[:, :, 1024:1025], 1.0), DVE)
    p.op("vector", lambda e_: e_.memset(epsb.ap()[:, 0:1], LN_EPS), DVE)
    p.op("vector", lambda e_: e_.memset(epsb.ap()[:, 1:2], LNB_EPS), DVE)
    p.op("vector", lambda e_: e_.memset(epsb.ap()[:, 2:3], 1e-30), DVE)
    p.op("vector", lambda e_: e_.memset(stack2.ap()[:, :], 0.0), DVE)
    p.wait("gpsimd", DVE)
    p.op("gpsimd", lambda e_: e_.affine_select(
        out=stack2.ap()[:, :], in_=stack2.ap()[:, :], compare_op=ALU.not_equal,
        fill=1.0, base=0, pattern=[[-1, 64]], channel_multiplier=1), PL)
    p.wait("gpsimd", PL)
    p.op("gpsimd", lambda e_: e_.affine_select(
        out=stack2.ap()[:, :], in_=stack2.ap()[:, :], compare_op=ALU.not_equal,
        fill=1.0, base=-64, pattern=[[-1, 64]], channel_multiplier=1), PL)
    p.wait("tensor", PL)
    p.wait("tensor", DVE)

    sm0 = sm["s_m0"]
    dma("sync", bin_sb.ap()[0:64, :], f_bin.rearrange("(a b) -> a b", b=1), sm["s_xb"])
    dma("sync", bin_sb.ap()[64:128, :], f_bin.rearrange("(a b) -> a b", b=1), sm["s_xb"])
    dma("sync", b2_sb.ap()[:, :], f_b2.rearrange("(a b) -> a b", b=1), sm["s_xb"])
    for src, dst in ((f_w1, w1p), (f_b1, b1p), (f_w2, w2p)):
        dma("sync", dst.ap()[0:64, :], src[:, 0:8], sm["s_xb"])
        dma("sync", dst.ap()[64:128, :], src[:, 8:16], sm["s_xb"])
    dma("sync", wstage[:, 0, 0:512].rearrange("p (c x) -> p c x", c=8),
        f_win.rearrange("(c p) x -> p c x", p=128), sm0)
    p.wait("vector", sm0)
    p.op("vector", lambda e_: e_.tensor_copy(
        win_sb.ap()[:, :, 0:64], wstage[:, 0, 0:512].rearrange("p (c x) -> p c x", c=8)), DVE)
    p.op("vector", lambda e_: e_.tensor_copy(
        win_sb.ap()[:, :, 64:128], wstage[:, 0, 0:512].rearrange("p (c x) -> p c x", c=8)), DVE)
    p.wait("sync", DVE)
    dma("sync", wstage[0:64, 2, :], f_wout, sm0)
    dma("sync", wstage[64:65, 2, :], f_bout.rearrange("(a b) -> a b", a=1), sm0)
    p.wait("vector", sm0)
    p.op("vector", lambda e_: e_.tensor_copy(wout_aug.ap()[0:64, :], wstage[0:64, 2, :]), DVE)
    p.op("vector", lambda e_: e_.tensor_copy(wout_aug.ap()[64:65, :], wstage[64:65, 2, :]), DVE)

    smw_o = {1: sm["s_ps1"], 2: sm["s_ps0"]}
    smw_h = [sm["s_ph0"], sm["s_ph1"]]
    ALL_STORE_SEMS = (sm["s_ps0"], sm["s_ps1"], sm["s_ph0"], sm["s_ph1"])
    pslot_sems = [sm["s_pl0"], sm["s_pl1"], sm["s_pl2"], sm["s_pl3"]]
    pst = {"tq": 0, "cast_rel": {}, "tgt": {}, "tp_rel": {}, "st_rel": {}}
    pst["cast_rel"][0] = DVE.n
    pst["cast_rel"][2] = DVE.n

    def t_unit(src, dst, scale, ecg):
        def load(eng, sl):
            dma(eng, wstage[:, sl, :], src[ecg * 128:(ecg + 1) * 128, :],
                pslot_sems[sl])

        def proc(sl):
            k = pst["tq"]
            pst["tq"] += 1
            tbk = k % 2
            oslot = 1 + tbk
            p.wait("vector", pslot_sems[sl], pst["tgt"][sl])
            p.wait("vector", PE, pst["tp_rel"].get(sl, 0))
            p.op("vector", lambda e_, sl=sl: e_.tensor_copy(
                wbuf[:, 0, sl, :], wstage[:, sl, :]), DVE)
            pst["cast_rel"][sl] = DVE.n
            p.wait("tensor", DVE)
            for dc in range(8):
                p.op("tensor", lambda e_, sl=sl, dc=dc, tbk=tbk: e_.transpose(
                    tbv[tbk][:, dc, :], wbuf[:, 0, sl, dc * 128:(dc + 1) * 128],
                    ident[:, :]), PE if dc == 7 else None)
            pst["tp_rel"][sl] = PE.n
            p.wait("vector", PE)
            p.wait("vector", smw_o[oslot], pst["st_rel"].get(oslot, 0))
            p.op("vector", lambda e_, oslot=oslot, tbk=tbk, sc=scale: e_.tensor_scalar(
                wbuf[:, oslot, 0, :].rearrange("p (c x) -> p c x", c=8),
                tbv[tbk][:, :, :], sc, None, ALU.mult), DVE)
            seng = "scalar"
            p.wait(seng, DVE)
            dma(seng, dst.rearrange("c p d -> p c d")[:, :, ecg * 128:(ecg + 1) * 128],
                wbuf[:, oslot, 0, :].rearrange("p (c x) -> p c x", c=8), smw_o[oslot])
            pst["st_rel"][oslot] = smw_o[oslot].n

        return load, proc

    def h_unit(src_hde, dst, scale, key, h):
        acc = wbuf.rearrange("p s a d -> p (s a d)")[:, 0:8192].rearrange(
            "p (c x) -> p c x", c=8)

        def load(eng, sl):
            dma(eng, wstage[:, sl, 0:512].rearrange("p (c x) -> p c x", c=8),
                src_hde[h].rearrange("(c p) x -> p c x", p=128), pslot_sems[sl])

        def proc(sl):
            if h == 0:
                if key == 0:
                    p.wait("vector", PE)
                    p.wait("vector", smw_o[1])
                else:
                    p.wait("vector", smw_h[0])
                    p.wait("vector", smw_h[1])
            p.wait("vector", pslot_sems[sl], pst["tgt"][sl])
            p.op("vector", lambda e_, sl=sl, h=h, sc=scale: e_.tensor_scalar(
                acc[:, :, h * 64:(h + 1) * 64],
                wstage[:, sl, 0:512].rearrange("p (c x) -> p c x", c=8),
                sc, None, ALU.mult), DVE)
            pst["cast_rel"][sl] = DVE.n
            if h % 2 == 1:
                par = (h // 2) % 2
                seng = "scalar"
                p.wait(seng, DVE)
                p.wait(seng, smw_h[par], pst.get(("hst", par), 0))
                dma(seng, dst.rearrange("c p d -> p c d")[:, :, (h - 1) * 64:(h + 1) * 64],
                    acc[:, :, (h - 1) * 64:(h + 1) * 64], smw_h[par])
                pst[("hst", par)] = smw_h[par].n

        return load, proc

    def p_unit(src, dst, c):
        def load(eng, sl):
            dma(eng, wstage[:, sl, :],
                src.rearrange("(c p) d -> p c d", p=128)[:, c, :], pslot_sems[sl])

        def proc(sl):
            k = pst["tq"]
            pst["tq"] += 1
            oslot = 1 + k % 2
            if c == 0:
                p.wait("vector", smw_h[0])
                p.wait("vector", smw_h[1])
            p.wait("vector", pslot_sems[sl], pst["tgt"][sl])
            p.wait("vector", smw_o[oslot], pst["st_rel"].get(oslot, 0))
            p.op("vector", lambda e_, sl=sl, oslot=oslot: e_.tensor_copy(
                wbuf[:, oslot, 0, :], wstage[:, sl, :]), DVE)
            pst["cast_rel"][sl] = DVE.n
            seng = "scalar"
            p.wait(seng, DVE)
            dma(seng, dst.rearrange("c p d -> p c d")[:, c, :],
                wbuf[:, oslot, 0, :], smw_o[oslot])
            pst["st_rel"][oslot] = smw_o[oslot].n

        return load, proc

    units = []
    for nm, dn, sc in (("wq_a", "wqT", 1.0 / 32.0), ("wk_a", "wkT", 1.0),
                       ("wo_a", "woT", 1.0)):
        units += [t_unit(dws[nm], wscr[dn], sc, ecg) for ecg in range(8)]
    for key, (nm, dn, sc) in enumerate((("wq_b", "wqB", 0.125),
                                        ("wk_b", "wkB", 1.0),
                                        ("wv_b", "wvB", 1.0))):
        units += [h_unit(dws[nm], wscr[dn], sc, key, h) for h in range(NH)]
    units += [p_unit(dws["wo_b"], wscr["woB"], c) for c in range(8)]

    def issue_load(i):
        eng = "sync" if i % 2 == 0 else "gpsimd"
        sl = i % 4
        p.wait(eng, DVE, pst["cast_rel"].get(sl, 0))
        units[i][0](eng, sl)
        pst["tgt"][sl] = pslot_sems[sl].n

    for i in range(min(4, len(units))):
        issue_load(i)
    for i in range(len(units)):
        units[i][1](i % 4)
        if i + 4 < len(units):
            issue_load(i + 4)
    wvres = wbuf.rearrange("p s a d -> p (s a d)")[:, 0:8192].rearrange(
        "p (c x) -> p c x", c=8)
    wostg = wbuf.rearrange("p s a d -> p (s a d)")[:, 8192:12288].rearrange(
        "p (c x) -> p c x", c=8)
    for s_ in ALL_STORE_SEMS:
        p.wait("sync", s_)
        p.wait("gpsimd", s_)
        p.wait("vector", s_)
    p.wait("vector", PE)
    wvo = {}
    for kc in range(8):
        sl = kc % 2
        eng = "sync" if sl == 0 else "gpsimd"
        p.wait(eng, DVE, wvo.get(("c", sl), pst["cast_rel"].get(sl, 0)))
        dma(eng, wstage[:, sl, :], dws["wv_a"][kc * 128:(kc + 1) * 128, :],
            pslot_sems[sl])
        p.wait("vector", pslot_sems[sl], pslot_sems[sl].n)
        p.op("vector", lambda e_, sl=sl, kc=kc: e_.tensor_copy(
            wvres[:, kc, :], wstage[:, sl, :]), DVE)
        wvo[("c", sl)] = DVE.n
        pst["cast_rel"][sl] = DVE.n
    for ih in range(2):
        wcsem = sm["s_wc0"] if ih == 0 else sm["s_wc1"]
        p.wait("sync", PE, wvo.get("pe", 0))
        dma("sync", wostg,
            wscr["woT"].rearrange("c p d -> p c d")[:, :, ih * 512:(ih + 1) * 512],
            wcsem)
        p.wait("tensor", wcsem)
        if ih == 0:
            p.wait("tensor", DVE)
        for j in range(8):
            bank = pb[4 + j % 2]
            p.wait("tensor", DVE, wvo.get(("ev", j % 2), 0))
            for kc in range(8):
                p.op("tensor", lambda e_, b=bank, j=j, kc=kc: e_.matmul(
                    b.ap()[:, :], wvres[:, kc, j * 128:(j + 1) * 128],
                    wostg[:, kc, :], start=(kc == 0), stop=(kc == 7)),
                    PE if kc == 7 else None)
            stg = wstage[:, 2 + j % 2, 0:256].bitcast(BF16)
            p.wait("vector", PE)
            ssem = smw_o[1 + j % 2]
            p.wait("vector", ssem, wvo.get(("st", j % 2), 0))
            p.op("vector", lambda e_, b=bank, d=stg: e_.tensor_copy(
                d, b.ap()[:, :]), DVE)
            wvo[("ev", j % 2)] = DVE.n
            seng = "scalar"
            p.wait(seng, DVE)
            dma(seng, wscr["wvT"].rearrange("c p d -> p c d")[:, j, ih * 512:(ih + 1) * 512],
                stg, ssem)
            wvo[("st", j % 2)] = ssem.n
        wvo["pe"] = PE.n

    for s_ in (sm["s_ps0"], sm["s_ps1"], sm["s_ph0"], sm["s_ph1"]):
        p.wait("sync", s_)
        p.wait("gpsimd", s_)

    p.wait("sync", pslot_sems[1])
    p.wait("sync", DVE, pst["cast_rel"].get(0, 0))
    p.wait("sync", DVE, pst["cast_rel"].get(1, 0))
    dma("sync", gamB[0:1, :], gam.rearrange("(a b) -> a b", a=1), sm0)
    dma("sync", betB[0:1, :], bet.rearrange("(a b) -> a b", a=1), sm0)
    p.wait("tensor", sm0)
    p.wait("tensor", PL)
    p.wait("vector", pslot_sems[0])
    p.wait("vector", pslot_sems[1])
    for srcrow, dstB in ((gamB, gamB), (betB, betB)):
        p.wait("tensor", DVE)
        for oc in range(2):
            p.op("tensor", lambda e_, s=srcrow, oc=oc: e_.matmul(
                pb[oc].ap()[:, :], ones_rowf.ap()[0:1, :],
                s[0:1, oc * 512:(oc + 1) * 512], start=True, stop=True),
                PE)
        p.wait("vector", PE)
        for oc in range(2):
            p.op("vector", lambda e_, d=dstB, oc=oc: e_.tensor_copy(
                d[:, oc * 512:(oc + 1) * 512], pb[oc].ap()[:, :]), DVE)
    p.wait("tensor", DVE)
    prep_tb_rel = DVE.n
    prep_psum_rel = DVE.n

    def phase_a(is_b):
        xsrc = xs1 if is_b else x_in
        wq_s, wk_s, wv_s = ((wscr["wqB"], wscr["wkB"], wscr["wvB"]) if is_b
                            else (wscr["wqT"], wscr["wkT"], wscr["wvT"]))
        wsems = [sm["s_w0"], sm["s_w1"], sm["s_w2"]]
        slot_uses = [w.n // 16 for w in wsems]
        slot_pe_rel = {0: PE.n, 1: PE.n, 2: PE.n}

        xsems = [sm["s_x0"], sm["s_x1"], sm["s_xa"], sm["s_xb"]]

        def xsl(i):
            return xin[:, i, :] if i < 2 else xin2[:, i - 2, :]

        x_aff_rel = {0: 0, 1: 0, 2: 0, 3: 0}
        tb_rel = {-1: prep_tb_rel}
        evac_rel = {("qk", 0): 0, ("qk", 1): 0, ("v", 0): 0, ("v", 1): 0}
        stq_sems = [sm["s_stq0"], sm["s_stq1"]]
        stk_sems = [sm["s_stk0"], sm["s_stk1"]]
        norm_read = [0]
        norm_readb = {0: 0, 1: 0}
        aff_dve = {}
        transp_pe = {}

        def nmax_update(kind, first, nb_ap=None):
            src = pb[2].ap() if nb_ap is None else nb_ap
            p.wait("vector", PE)
            p.wait("vector", DVE)
            p.op("vector", lambda e_, src=src: e_.reduce_max(
                nmax.ap()[0:1, 4:5], src[0:1, :], axis=AXX), DVE)
            norm_read[0] = DVE.n
            col = 0 if kind == "q" else 1
            p.wait("vector", DVE)
            if first:
                p.op("vector", lambda e_, c=col: e_.tensor_copy(
                    nmax.ap()[0:1, c:c + 1], nmax.ap()[0:1, 4:5]), DVE)
            else:
                p.op("vector", lambda e_, c=col: e_.tensor_max(
                    nmax.ap()[0:1, c + 2:c + 3], nmax.ap()[0:1, c:c + 1],
                    nmax.ap()[0:1, 4:5]), DVE)
                p.wait("vector", DVE)
                p.op("vector", lambda e_, c=col: e_.tensor_copy(
                    nmax.ap()[0:1, c:c + 1], nmax.ap()[0:1, c + 2:c + 3]), DVE)

        def load_half(slot, mat_s, half):
            p.wait("sync", PE, slot_pe_rel[slot])
            dma("sync", wbuf[:, slot, :, :],
                mat_s.rearrange("c p d -> p c d")[:, half * 4:(half + 1) * 4, :],
                wsems[slot])
            slot_uses[slot] += 1

        def chunk_ln_part1(tc):
            stv = stat.ap()
            for stt in range(4):
                p.wait("sync", DVE, x_aff_rel[stt])
                r0 = tc * TOKC + stt * 128
                dma("sync", xsl(stt), xsrc[r0:r0 + 128, :], xsems[stt])
            for stt in range(4):
                p.wait("vector", xsems[stt])
                p.wait("vector", PE, transp_pe.get(tc - 1, 0))
                p.wait("vector", DVE)
                p.op("vector", lambda e_, stt=stt: e_.scalar_tensor_tensor(
                    lnx_tm[:, stt % 2, :], xsl(stt), 1.0, xsl(stt),
                    ALU.mult, ALU.mult,
                    accum_out=stat.ap()[:, 4 + stt:5 + stt]), DVE)
                p.op("vector", lambda e_, stt=stt: e_.reduce_sum(
                    stat.ap()[:, stt:stt + 1], xsl(stt), axis=AXX), DVE)
            p.wait("vector", DVE)
            p.op("vector", lambda e_: e_.tensor_scalar(
                stv[:, 8:12], stv[:, 0:4], 1.0 / DM, None, ALU.mult), DVE)
            p.wait("vector", DVE)
            p.op("vector", lambda e_: e_.tensor_tensor(
                stv[:, 12:16], stv[:, 8:12], stv[:, 8:12], ALU.mult), DVE)
            p.wait("vector", DVE)
            p.op("vector", lambda e_: e_.scalar_tensor_tensor(
                stv[:, 12:16], stv[:, 4:8], 1.0 / DM, stv[:, 12:16],
                ALU.mult, ALU.subtract), DVE)
            p.wait("scalar", DVE)
            p.op("scalar", lambda e_: e_.activation(
                stv[:, 4:8], stv[:, 12:16], AF.Ln, bias=epsb.ap()[:, 0:1]), SC)
            p.wait("scalar", SC)
            p.op("scalar", lambda e_: e_.activation(
                stv[:, 12:16], stv[:, 4:8], AF.Exp, scale=-0.5), SC)
            p.wait("vector", SC)
            for stt in range(4):
                p.op("vector", lambda e_, stt=stt: e_.tensor_scalar(
                    stv[:, 4 + stt:5 + stt], stv[:, 8 + stt:9 + stt],
                    stv[:, 12 + stt:13 + stt], -1.0, ALU.mult, ALU.mult), DVE)
            for stt in range(2):
                p.wait("vector", DVE)
                p.op("vector", lambda e_, stt=stt: e_.tensor_scalar(
                    lnx_tm[:, stt, :], xsl(stt), stv[:, 12 + stt:13 + stt],
                    stv[:, 4 + stt:5 + stt], ALU.mult, ALU.add), DVE)
            aff_dve[(tc, 0)] = DVE.n
            x_aff_rel[0] = DVE.n
            x_aff_rel[1] = DVE.n

        def chunk_top(tc):
            stv = stat.ap()
            p.wait("tensor", DVE, aff_dve[(tc, 0)])
            p.wait("tensor", DVE, tb_rel.get(tc - 1, 0))
            for stt in range(2):
                for dc in range(8):
                    p.op("tensor", lambda e_, stt=stt, dc=dc: e_.transpose(
                        tbv[stt][:, dc, :], lnx_tm[:, stt, dc * 128:(dc + 1) * 128],
                        ident[:, :]), PE if dc == 7 else None)
                p.wait("vector", PE)
                p.op("vector", lambda e_, stt=stt: e_.tensor_copy(
                    lnxT[:, :, stt * 128:(stt + 1) * 128], tbv[stt][:, :, :]), DVE)
            tp01 = PE.n
            for stt in range(2, 4):
                p.wait("vector", PE, tp01)
                p.op("vector", lambda e_, stt=stt: e_.tensor_scalar(
                    lnx_tm[:, stt % 2, :], xsl(stt), stv[:, 12 + stt:13 + stt],
                    stv[:, 4 + stt:5 + stt], ALU.mult, ALU.add), DVE)
                x_aff_rel[stt] = DVE.n
                p.wait("tensor", DVE)
                for dc in range(8):
                    p.op("tensor", lambda e_, stt=stt, dc=dc: e_.transpose(
                        tbv[stt % 2][:, dc, :], lnx_tm[:, stt % 2, dc * 128:(dc + 1) * 128],
                        ident[:, :]), PE if dc == 7 else None)
                p.wait("vector", PE)
                p.op("vector", lambda e_, stt=stt: e_.tensor_copy(
                    lnxT[:, :, stt * 128:(stt + 1) * 128], tbv[stt % 2][:, :, :]), DVE)
            transp_pe[tc] = PE.n
            tb_rel[tc] = DVE.n
            return DVE.n

        load_half(0, wk_s, 0)
        load_half(1, wk_s, 1)
        pass_idx = 0
        p.op("vector", lambda e_: e_.memset(nmax.ap()[0:1, 0:4], 0.0), DVE)
        chunk_ln_part1(0)

        for tc in range(NTC):
            mats = ((wk_s, "k"), (wq_s, "q"), (wv_s, "v"))
            lnxT_ready = chunk_top(tc)

            for mi, (mat_s, kind) in enumerate(mats):
                sa, sb = (2 * pass_idx) % 3, (2 * pass_idx + 1) % 3
                if not (mi == 2 and tc == NTC - 1):
                    nmat = mats[mi + 1][0] if mi < 2 else wk_s
                    load_half((2 * pass_idx + 2) % 3, nmat, 0)
                if kind in ("k", "q"):
                    staged = (kind == "q") or is_b
                    stsems = stq_sems if kind == "q" else stk_sems
                    sq_sc = {}

                    def emit_norm(e2):
                        p.wait("tensor", SC, sq_sc[e2])
                        if is_b:
                            nb = pb[2 + e2 % 2]
                            p.wait("tensor", DVE, norm_readb[e2 % 2])
                            p.op("tensor", lambda e_, nb=nb, e2=e2: e_.matmul(
                                nb.ap()[0:1, :], ones_colb.ap()[:, :],
                                nsq[:, e2 % 2, :],
                                start=True, stop=True), PE)
                            norm_mm_pe[e2 % 2] = PE.n
                            nmax_update(kind, tc == 0 and e2 == 0, nb.ap())
                            norm_readb[e2 % 2] = DVE.n
                        else:
                            p.op("tensor", lambda e_, e2=e2: e_.matmul(
                                pb[2].ap()[0:1, :], ones_colb.ap()[:, :],
                                nsq[:, e2 % 2, :],
                                start=(e2 == 0), stop=(e2 == 7)), PE)
                            norm_mm_pe[e2 % 2] = PE.n
                            if e2 == 7:
                                nmax_update(kind, tc == 0)

                    for ec in range(8):
                        bank = pb[ec % 2]
                        p.wait("tensor", SC, evac_rel.get(("qk", ec % 2), 0))
                        if ec == 0:
                            p.wait("tensor", wsems[sa], 16 * slot_uses[sa])
                            p.wait("tensor", DVE, lnxT_ready)
                            p.wait("tensor", DVE, norm_read[0])
                        for dc in range(8):
                            wslot = sa if dc < 4 else sb
                            if ec == 0 and dc == 4:
                                p.wait("tensor", wsems[sb], 16 * slot_uses[sb])
                            p.op("tensor", lambda e_, b=bank, ws=wslot, dc=dc, ec=ec: e_.matmul(
                                b.ap()[:, :], wbuf[:, ws, dc % 4, ec * 128:(ec + 1) * 128],
                                lnxT[:, dc, :], start=(dc == 0), stop=(dc == 7)),
                                PE if dc == 7 else None)
                        if ec >= 2:
                            emit_norm(ec - 2)
                        p.wait("scalar", PE)
                        p.op("scalar", lambda e_, b=bank, ec=ec: e_.activation(
                            nsq[:, ec % 2, :], b.ap()[:, :], AF.Square), SC)
                        sq_sc[ec] = SC.n
                        if staged:
                            p.wait("scalar", stq_sems[ec % 2])
                            p.wait("scalar", stk_sems[ec % 2])
                            dest = qstage[:, 0, :] if ec % 2 == 0 else qstage2
                        else:
                            dest = k_sb.ap()[:, ec, tc * TOKC:(tc + 1) * TOKC]
                        p.op("scalar", lambda e_, b=bank, d=dest: e_.activation(
                            d, b.ap()[:, :], AF.Copy), SC)
                        evac_rel[("qk", ec % 2)] = SC.n
                        if staged:
                            dscr = q_scr if kind == "q" else k_scr
                            p.wait("gpsimd", SC)
                            dma("gpsimd",
                                dscr[ec, :, tc * TOKC:(tc + 1) * TOKC],
                                dest, stsems[ec % 2])
                    emit_norm(6)
                    emit_norm(7)
                    if kind == "k" and tc + 1 < NTC:
                        chunk_ln_part1(tc + 1)
                else:
                    for kt in range(4):
                        for oc in range(2):
                            bank = pb[4 + (2 * kt + oc) % 2]
                            p.wait("tensor", SC, evac_rel.get(("v", (2 * kt + oc) % 2), 0))
                            if kt == 0 and oc == 0:
                                p.wait("tensor", wsems[sa], 16 * slot_uses[sa])
                                p.wait("tensor", wsems[sb], 16 * slot_uses[sb])
                                p.wait("tensor", DVE, lnxT_ready)
                            for dc in range(8):
                                wslot = sa if dc < 4 else sb
                                p.op("tensor", lambda e_, b=bank, ws=wslot, dc=dc, kt=kt, oc=oc: e_.matmul(
                                    b.ap()[:, :], lnxT[:, dc, kt * 128:(kt + 1) * 128],
                                    wbuf[:, ws, dc % 4, oc * 512:(oc + 1) * 512],
                                    start=(dc == 0), stop=(dc == 7)),
                                    PE if dc == 7 else None)
                            p.wait("scalar", PE)
                            ktg = tc * 4 + kt
                            if is_b:
                                p.op("scalar", lambda e_, b=bank, ktg=ktg, oc=oc: e_.activation(
                                    v_sbB[:, ktg, oc * 8:(oc + 1) * 8, 0:64],
                                    b.ap()[:, :].rearrange("p (h x) -> p h x", h=8),
                                    AF.Copy), SC)
                            else:
                                p.op("scalar", lambda e_, b=bank, ktg=ktg, oc=oc: e_.activation(
                                    v_sb.ap()[:, ktg, oc * 512:(oc + 1) * 512],
                                    b.ap()[:, :], AF.Copy), SC)
                            evac_rel[("v", (2 * kt + oc) % 2)] = SC.n
                slot_pe_rel[sa] = PE.n
                slot_pe_rel[sb] = PE.n
                if not (mi == 2 and tc == NTC - 1):
                    nmat = mats[mi + 1][0] if mi < 2 else wk_s
                    load_half((2 * pass_idx + 3) % 3, nmat, 1)
                pass_idx += 1

        if is_b:
            p.wait("vector", PE)
            p.op("vector", lambda e_: e_.memset(v_sbB[:, :, :, 64:65], 1.0), DVE)

        p.wait("vector", DVE)
        p.op("vector", lambda e_: e_.tensor_tensor(
            nmax.ap()[0:1, 5:6], nmax.ap()[0:1, 0:1], nmax.ap()[0:1, 1:2],
            ALU.mult), DVE)
        p.wait("scalar", DVE)
        p.op("scalar", lambda e_: e_.activation(
            nmax.ap()[0:1, 6:7], nmax.ap()[0:1, 5:6], AF.Ln,
            bias=epsb.ap()[0:1, 2:3]), SC)
        p.wait("scalar", SC)
        p.op("scalar", lambda e_: e_.activation(
            nmax.ap()[0:1, 7:8], nmax.ap()[0:1, 6:7], AF.Exp, scale=0.5), SC)
        p.wait("tensor", SC)
        p.op("tensor", lambda e_: e_.matmul(
            pb[0].ap()[:, 0:1], ones_rowf.ap()[0:1, :], nmax.ap()[0:1, 7:8],
            start=True, stop=True), PE)
        p.wait("vector", PE)
        p.op("vector", lambda e_, c=(1 if is_b else 0): e_.tensor_scalar(
            negC.ap()[:, c:c + 1], pb[0].ap()[:, 0:1], -1.0, None, ALU.mult), DVE)
        p.wait("tensor", DVE)

    norm_pe_rel = [0]
    norm_mm_pe = {}

    def emit_ln(x_ap, lnx_ap, eps, sq_ap):
        su, ssq, mean, msq, var, rstd, nmb, lnv = (stat.ap()[:, i:i + 1] for i in range(8))
        p.wait("scalar", DVE)
        p.wait("vector", DVE)
        p.op("vector", lambda e_: e_.reduce_sum(su, x_ap, axis=AXX), DVE)
        p.op("scalar", lambda e_: e_.activation(sq_ap, x_ap, AF.Square, accum_out=ssq), SC)
        p.wait("vector", DVE)
        p.op("vector", lambda e_: e_.tensor_scalar(mean, su, 1.0 / DM, None, ALU.mult), DVE)
        p.wait("vector", SC)
        p.wait("vector", DVE)
        p.op("vector", lambda e_: e_.tensor_tensor(msq, mean, mean, ALU.mult), DVE)
        p.wait("vector", DVE)
        p.op("vector", lambda e_: e_.scalar_tensor_tensor(
            var, ssq, 1.0 / DM, msq, ALU.mult, ALU.subtract), DVE)
        p.wait("scalar", DVE)
        eps_ap = epsb.ap()[:, 0:1] if eps == LN_EPS else epsb.ap()[:, 1:2]
        p.op("scalar", lambda e_, b=eps_ap: e_.activation(lnv, var, AF.Ln, bias=b), SC)
        p.wait("scalar", SC)
        p.op("scalar", lambda e_: e_.activation(rstd, lnv, AF.Exp, scale=-0.5), SC)
        p.wait("vector", SC)
        p.op("vector", lambda e_: e_.tensor_scalar(nmb, mean, rstd, -1.0, ALU.mult, ALU.mult), DVE)
        p.wait("vector", DVE)
        p.op("vector", lambda e_: e_.tensor_scalar(
            lnx_ap, x_ap, rstd, nmb, ALU.mult, ALU.add), DVE)

    _st_sc = {}

    def ln2_stats(x_aps, eps, junk_ap, st=None):
        stv = stat.ap() if st is None else st
        n = len(x_aps)
        p.wait("vector", SC, _st_sc.get(0 if st is None else 1, 0))
        for i, x in enumerate(x_aps):
            p.wait("vector", DVE)
            p.op("vector", lambda e_, x=x, i=i, stv=stv: e_.scalar_tensor_tensor(
                junk_ap, x, 1.0, x, ALU.mult, ALU.mult,
                accum_out=stv[:, 2 + i:3 + i]), DVE)
            p.op("vector", lambda e_, x=x, i=i, stv=stv: e_.reduce_sum(
                stv[:, i:i + 1], x, axis=AXX), DVE)
        p.wait("vector", DVE)
        p.op("vector", lambda e_, n=n, stv=stv: e_.tensor_scalar(
            stv[:, 4:4 + n], stv[:, 0:n], 1.0 / DM, None, ALU.mult), DVE)
        p.wait("vector", DVE)
        p.op("vector", lambda e_, n=n, stv=stv: e_.tensor_tensor(
            stv[:, 6:6 + n], stv[:, 4:4 + n], stv[:, 4:4 + n], ALU.mult), DVE)
        p.wait("vector", DVE)
        p.op("vector", lambda e_, n=n, stv=stv: e_.scalar_tensor_tensor(
            stv[:, 6:6 + n], stv[:, 2:2 + n], 1.0 / DM, stv[:, 6:6 + n],
            ALU.mult, ALU.subtract), DVE)
        p.wait("scalar", DVE)
        eps_ap = epsb.ap()[:, 0:1] if eps == LN_EPS else epsb.ap()[:, 1:2]
        p.op("scalar", lambda e_, n=n, b=eps_ap, stv=stv: e_.activation(
            stv[:, 2:2 + n], stv[:, 6:6 + n], AF.Ln, bias=b), SC)
        p.wait("scalar", SC)
        p.op("scalar", lambda e_, n=n, stv=stv: e_.activation(
            stv[:, 6:6 + n], stv[:, 2:2 + n], AF.Exp, scale=-0.5), SC)
        _st_sc[0 if st is None else 1] = SC.n
        p.wait("vector", SC)
        for i in range(n):
            p.op("vector", lambda e_, i=i, stv=stv: e_.tensor_scalar(
                stv[:, 10 + i:11 + i], stv[:, 4 + i:5 + i],
                stv[:, 6 + i:7 + i], -1.0, ALU.mult, ALU.mult), DVE)

    def ln2_affine(i, x_ap, lnx_ap, st=None):
        stv = stat.ap() if st is None else st
        p.wait("vector", DVE)
        p.op("vector", lambda e_, i=i, x=x_ap, o=lnx_ap, stv=stv: e_.tensor_scalar(
            o, x, stv[:, 6 + i:7 + i], stv[:, 10 + i:11 + i],
            ALU.mult, ALU.add), DVE)

    ffn_state = {"lnf_pe": 0, "tb": 0, "h_rel": 0, "out_evac": {}, "gaug_pe": 0}

    def ffn_part1(x2_list, lnf, lnT):
        p.wait("vector", PE, ffn_state["lnf_pe"])
        ln2_stats(x2_list, LN_EPS, lnf)
        for stt in range(2):
            if stt:
                p.wait("vector", PE)
            ln2_affine(stt, x2_list[stt], lnf)
            tbk = ffn_state["tb"]
            ffn_state["tb"] ^= 1
            p.wait("tensor", DVE)
            p.wait("tensor", SC, ffn_state.get("tb_sc", 0))
            for dc in range(8):
                p.op("tensor", lambda e_, dc=dc, t=tbk: e_.transpose(
                    tbv[t][:, dc, :], lnf[:, dc * 128:(dc + 1) * 128], ident[:, :]),
                    PE if dc == 7 else None)
            ffn_state["lnf_pe"] = PE.n
            p.wait("vector", PE)
            p.op("vector", lambda e_, t=tbk, stt=stt: e_.tensor_copy(
                lnT[:, :, stt * 128:(stt + 1) * 128], tbv[t][:, :, :]), DVE)
            ffn_state[("tbrel", tbk)] = DVE.n
        lnT_ready = DVE.n
        p.wait("tensor", DVE, lnT_ready)
        p.wait("tensor", SC)
        for dc in range(8):
            p.op("tensor", lambda e_, dc=dc: e_.matmul(
                pb[4].ap()[:, 0:256], win_sb.ap()[:, dc, :], lnT[:, dc, :],
                start=(dc == 0), stop=(dc == 7)), PE if dc == 7 else None)
        p.wait("vector", PE)
        p.op("vector", lambda e_: e_.tensor_scalar(
            s2w[:, :], pb[4].ap()[:, 0:256], bin_sb.ap()[:, :], None, ALU.add), DVE)
        ffn_state["sevac"] = DVE.n
        ffn_middle()

    def ffn_middle():
        if ffn_state.get("sc_relu"):
            h2 = ffn_state["h2"]
            p.wait("scalar", DVE)
            for j in range(8):
                hb = h_w if j % 2 == 0 else h2
                p.wait("scalar", DVE, ffn_state.get(("hfree", j % 2), 0))
                p.op("scalar", lambda e_, j=j, hb=hb: e_.activation(
                    hb[:, :], s2w[:, :], AF.Relu,
                    bias=b1p.ap()[:, j:j + 1], scale=w1p.ap()[:, j:j + 1]), SC)
                gdst = g2a if j % 2 == 0 else g2b
                p.wait("vector", SC)
                p.wait("vector", DVE)
                if j < 2:
                    p.op("vector", lambda e_, hb=hb, g=gdst, j=j: e_.tensor_scalar(
                        g[:, :], hb[:, :], w2p.ap()[:, j:j + 1], None, ALU.mult), DVE)
                else:
                    p.op("vector", lambda e_, hb=hb, g=gdst, j=j: e_.scalar_tensor_tensor(
                        g[:, :], hb[:, :], w2p.ap()[:, j:j + 1], g[:, :],
                        ALU.mult, ALU.add), DVE)
                ffn_state[("hfree", j % 2)] = DVE.n
        else:
            for j in range(8):
                p.wait("vector", DVE)
                p.op("vector", lambda e_, j=j: e_.tensor_scalar(
                    h_w[:, :], s2w[:, :], w1p.ap()[:, j:j + 1],
                    b1p.ap()[:, j:j + 1], ALU.mult, ALU.add), DVE)
                p.wait("vector", DVE)
                p.op("vector", lambda e_: e_.tensor_scalar(
                    h_w[:, :], h_w[:, :], 0.0, None, ALU.max), DVE)
                gdst = g2a if j % 2 == 0 else g2b
                p.wait("vector", DVE)
                if j < 2:
                    p.op("vector", lambda e_, g=gdst, j=j: e_.tensor_scalar(
                        g[:, :], h_w[:, :], w2p.ap()[:, j:j + 1], None,
                        ALU.mult), DVE)
                else:
                    p.op("vector", lambda e_, g=gdst, j=j: e_.scalar_tensor_tensor(
                        g[:, :], h_w[:, :], w2p.ap()[:, j:j + 1], g[:, :],
                        ALU.mult, ALU.add), DVE)

    def ffn_part1a(x2_list):
        lnfA, lnfB = ffn_state["lnfA"], ffn_state["lnfB"]
        p.wait("vector", PE, ffn_state["lnf_pe"])
        ln2_stats(x2_list, LN_EPS, lnfA[:, :])
        ln2_affine(0, x2_list[0], lnfA[:, :])
        ffn_state[("aff", 0)] = DVE.n
        ln2_affine(1, x2_list[1], lnfB[:, :])
        ffn_state[("aff", 1)] = DVE.n

    def ffn_part1b():
        lnfA, lnfB, lnT = ffn_state["lnfA"], ffn_state["lnfB"], ffn_state["lnT"]
        for stt, lsrc in ((0, lnfA), (1, lnfB)):
            tbk = ffn_state["tb"]
            ffn_state["tb"] ^= 1
            p.wait("tensor", DVE, ffn_state[("aff", stt)])
            p.wait("tensor", DVE, ffn_state.get(("tbrel", tbk), 0))
            p.wait("tensor", SC, ffn_state.get("tb_sc", 0))
            for dc in range(8):
                p.op("tensor", lambda e_, dc=dc, t=tbk, lsrc=lsrc: e_.transpose(
                    tbv[t][:, dc, :], lsrc[:, dc * 128:(dc + 1) * 128],
                    ident[:, :]), PE if dc == 7 else None)
            p.wait("vector", PE)
            p.op("vector", lambda e_, t=tbk, stt=stt, lnT=lnT: e_.tensor_copy(
                lnT[:, :, stt * 128:(stt + 1) * 128], tbv[t][:, :, :]), DVE)
            ffn_state[("tbrel", tbk)] = DVE.n
        ffn_state["lnf_pe"] = PE.n
        lnT_ready = DVE.n
        p.wait("tensor", DVE, lnT_ready)
        p.wait("tensor", SC)
        for dc in range(8):
            p.op("tensor", lambda e_, dc=dc, lnT=lnT: e_.matmul(
                pb[4].ap()[:, 0:256], win_sb.ap()[:, dc, :], lnT[:, dc, :],
                start=(dc == 0), stop=(dc == 7)), PE if dc == 7 else None)
        p.wait("vector", PE)
        p.op("vector", lambda e_: e_.tensor_scalar(
            s2w[:, :], pb[4].ap()[:, 0:256], bin_sb.ap()[:, :], None, ALU.add), DVE)
        ffn_state["sevac"] = DVE.n
        ffn_middle()

    def ffn_part2(xres_list, row0, dst_dram, st_sem, dst_list=None):
        if dst_list is None:
            dst_list = xres_list
        p.wait("tensor", DVE)
        p.op("tensor", lambda e_: e_.matmul(
            pb[5].ap()[0:64, 0:256], stack2.ap()[:, :], g2a[:, :],
            start=True, stop=False))
        p.op("tensor", lambda e_: e_.matmul(
            pb[5].ap()[0:64, 0:256], stack2.ap()[:, :], g2b[:, :],
            start=False, stop=True), PE)
        p.wait("vector", PE)
        p.wait("vector", PE, ffn_state["gaug_pe"])
        p.op("vector", lambda e_: e_.tensor_scalar(
            g_aug.ap()[0:64, :], pb[5].ap()[0:64, 0:256], b2_sb.ap()[:, :],
            None, ALU.add), DVE)
        ffn_state["h_rel"] = DVE.n
        g_ready = DVE.n
        p.wait("tensor", DVE, g_ready)
        for stt in range(2):
            for oc in range(2):
                bank = pb[4 + (2 * stt + oc) % 2]
                p.wait("tensor", DVE, ffn_state["out_evac"].get((2 * stt + oc) % 2, 0))
                p.op("tensor", lambda e_, b=bank, stt=stt, oc=oc: e_.matmul(
                    b.ap()[:, :], g_aug.ap()[:, stt * 128:(stt + 1) * 128],
                    wout_aug.ap()[:, oc * 512:(oc + 1) * 512],
                    start=True, stop=True), PE)
                p.wait("vector", PE)
                p.op("vector", lambda e_, b=bank, stt=stt, oc=oc: e_.tensor_tensor(
                    dst_list[stt][:, oc * 512:(oc + 1) * 512], b.ap()[:, :],
                    xres_list[stt][:, oc * 512:(oc + 1) * 512], ALU.add), DVE)
                ffn_state["out_evac"][(2 * stt + oc) % 2] = DVE.n
        ffn_state["gaug_pe"] = PE.n
        p.wait("gpsimd", DVE)
        for stt in range(2):
            dma("gpsimd", dst_dram[row0 + stt * 128:row0 + (stt + 1) * 128, :],
                dst_list[stt], st_sem)


    def ffn_block(x2_list, xres_list, row0, dst_dram, st_sem, lnf, lnT):
        ffn_part1(x2_list, lnf, lnT)
        ffn_part2(xres_list, row0, dst_dram, st_sem)

    def phase_1b():
        ffn_state["sc_relu"] = True
        ffn_state["h2"] = YT[:, 0:256]
        ffn_state["smm_sc_wait"] = False
        ffn_state["lnfA"] = lnf_tm
        ffn_state["lnfB"] = lnf2b
        ffn_state["lnT"] = lnfT
        exp_done = {}
        ctx_evac = {0: 0}
        q_pe_rel = {}
        pair_dve = {}
        xpairs = [[XW[0], XW[1]], [XW[2], XW[3]]]
        xsems1b = [(sm["s_xa"], sm["s_xb"]), (sm["s_xc"], sm["s_xd"])]
        tb0f = tb[0].ap().bitcast(F32)
        tb1f = tb[1].ap().bitcast(F32)

        def attn_loads(qs):
            q0 = qs * QS1
            par = qs % 2
            qsem = sm["s_q0"] if par == 0 else sm["s_q1"]
            p.wait("sync", PE, q_pe_rel.get(par, 0))
            dma("sync", qb1[:, :, par * QS1:(par + 1) * QS1],
                q_scr.rearrange("c p t -> p c t")[:, :, q0:q0 + QS1], qsem)

        def attn_part(qs, dve_pre=None, post_kc0=None):
            q0 = qs * QS1
            kcnt = 2 * qs + 2
            par = qs % 2
            P = xpairs[par]
            xsems = xsems1b[par]
            qsem = sm["s_q0"] if par == 0 else sm["s_q1"]
            for stt in range(2):
                p.wait("sync", sm["s_stx"], 16 * 2 * (qs - 1) if qs else 0)
                p.wait("sync", DVE, pair_dve.get(par, 0))
                dma("sync", P[stt][:, :], x_in[q0 + stt * 128:q0 + (stt + 1) * 128, :],
                    xsems[stt])
            p.wait("tensor", qsem)
            if dve_pre is None:
                p.wait("tensor", DVE)
            exp_at = {}
            mask_at = {}
            AVB = [[pb[3].ap()[:, :], pb[4].ap()[:, :]],
                   [pb[5].ap()[:, :], tb1f[:, :]]]
            SB = [pb[2].ap()[:, 0:1], tb0f[:, 0:1]]

            def emit_avs(kc):
                for tch in range(2):
                    p.wait("tensor", SC, exp_at[kc])
                    if mask_at[kc]:
                        p.wait("tensor", PL, mask_at[kc])
                    if kc == 0:
                        p.wait("tensor", DVE, ctx_evac[0])
                    att = attT1[:, kc, tch * 128:(tch + 1) * 128]
                    p.op("tensor", lambda e_, att=att, tch=tch, kc=kc, kcnt=kcnt: e_.matmul(
                        SB[tch], att, ones_colb.ap()[:, :],
                        start=(kc == 0), stop=(kc == kcnt - 1)))
                    for dh in range(2):
                        p.op("tensor", lambda e_, att=att, tch=tch, dh=dh, kc=kc, kcnt=kcnt: e_.matmul(
                            AVB[tch][dh], att,
                            v_sb.ap()[:, kc, dh * 512:(dh + 1) * 512],
                            start=(kc == 0), stop=(kc == kcnt - 1)),
                            PE if (kc == kcnt - 1 and tch == 1 and dh == 1) else None)

            for kc in range(kcnt):
                bank = pb[kc % 2]
                p.wait("tensor", SC, exp_done.get(kc % 2, 0))
                for ec in range(8):
                    p.op("tensor", lambda e_, b=bank, ec=ec, kc=kc, par=par: e_.matmul(
                        b.ap()[:, 0:QS1], k_sb.ap()[:, ec, kc * 128:(kc + 1) * 128],
                        qb1[:, ec, par * QS1:(par + 1) * QS1],
                        start=(ec == 0), stop=(ec == 7)), PE if ec == 7 else None)
                p.wait("scalar", PE)
                p.op("scalar", lambda e_, b=bank, kc=kc: e_.activation(
                    attT1[:, kc, :], b.ap()[:, 0:QS1], AF.Exp,
                    bias=negC.ap()[:, 0:1]), SC)
                exp_done[kc % 2] = SC.n
                exp_at[kc] = SC.n
                if kc >= kcnt - 2:
                    p.wait("gpsimd", SC)
                    p.op("gpsimd", lambda e_, kc=kc, q0=q0: e_.affine_select(
                        out=attT1[:, kc, :], in_=attT1[:, kc, :],
                        compare_op=ALU.is_ge, fill=0.0,
                        base=q0 - kc * 128, pattern=[[1, QS1]],
                        channel_multiplier=-1), PL)
                    mask_at[kc] = PL.n
                else:
                    mask_at[kc] = 0
                if kc == 1 and post_kc0 is not None:
                    post_kc0()
                if kc >= 2:
                    emit_avs(kc - 2)
            if kcnt >= 2:
                emit_avs(kcnt - 2)
            emit_avs(kcnt - 1)
            p.wait("scalar", PE)
            for tch in range(2):
                p.op("scalar", lambda e_, tch=tch: e_.activation(
                    stat.ap()[:, 12 + tch:13 + tch], SB[tch], AF.Ln,
                    bias=epsb.ap()[:, 2:3]), SC)
            p.wait("scalar", SC)
            for tch in range(2):
                p.op("scalar", lambda e_, tch=tch: e_.activation(
                    stat.ap()[:, 14 + tch:15 + tch],
                    stat.ap()[:, 12 + tch:13 + tch], AF.Exp, scale=-1.0), SC)
            rc = SC.n
            for tch in range(2):
                p.wait("vector", xsems[tch])
                p.wait("vector", SC, rc)
                p.wait("vector", PE)
                p.wait("vector", DVE)
                for dh in range(2):
                    p.op("vector", lambda e_, b=AVB[tch][dh], tch=tch, dh=dh, P=P: e_.scalar_tensor_tensor(
                        P[tch][:, dh * 512:(dh + 1) * 512], b,
                        stat.ap()[:, 14 + tch:15 + tch],
                        P[tch][:, dh * 512:(dh + 1) * 512],
                        ALU.mult, ALU.add), DVE)
            ctx_evac[0] = DVE.n
            ffn_state["tb_sc"] = rc
            pair_dve[par] = DVE.n
            q_pe_rel[par] = PE.n
            return DVE.n

        attn_loads(0)
        attn_part(0)
        P0 = xpairs[0]
        ffn_part1a([P0[0][:, :], P0[1][:, :]])
        attn_loads(1)
        for qs in range(1, NQ1):
            dve_pre = DVE.n

            def hook():
                ffn_part1b()
                ctx_evac[0] = ffn_state["sevac"]

            attn_part(qs, dve_pre, post_kc0=hook)
            Pp = xpairs[(qs - 1) % 2]
            ffn_part2([Pp[0][:, :], Pp[1][:, :]], (qs - 1) * QS1, xs1,
                      sm["s_stx"])
            ctx_evac[0] = DVE.n
            pair_dve[(qs - 1) % 2] = DVE.n
            Pc = xpairs[qs % 2]
            ffn_part1a([Pc[0][:, :], Pc[1][:, :]])
            if qs + 1 < NQ1:
                attn_loads(qs + 1)
        ffn_part1b()
        Pl = xpairs[(NQ1 - 1) % 2]
        ffn_part2([Pl[0][:, :], Pl[1][:, :]], (NQ1 - 1) * QS1, xs1,
                  sm["s_stx"])

    xw_dve_rel = {}

    def phase_2b():
        exp_done = {}
        q_pe_rel = [0]
        ksems3 = [sm["s_k0"], sm["s_k1"], sm["s_k2"]]
        kuse3 = [0, 0, 0]
        kpe_rel3 = {0: 0, 1: 0, 2: 0}
        kg = [0]
        kslot_of = {}
        ktgt = {}
        ctxn_dve = [0]
        opj23_free = [0]
        pair_rel = {0: 0, 1: 0}
        pendf = []
        wsb = k_sb.ap()[:, :, 0:1024]
        AP2 = [attb2[:, 0:2, :], attb2[:, 2:4, :],
               k_sb.ap()[:, 5, 2048:3072].rearrange("p (a b) -> p a b", a=2)]
        Y2 = [k_sb.ap()[:, j, 1024:3072].bitcast(F32) for j in range(4)]
        Jnk = k_sb.ap()[:, 4, 1024:3072].bitcast(F32)
        stat2 = k_sb.ap()[:, 5, 1024:1056].bitcast(F32)
        kstr3 = k_sb.ap()[:, 6, 1024:3072]
        kslots = [kstr2[:, 0, :], kstr2[:, 1, :], kstr3]
        NORM1 = k_sb.ap()[:, 7, 2048:4096].bitcast(F32)
        sts = [None, stat2]
        ffn_state["sc_relu"] = False
        ffn_state["smm_sc_wait"] = True
        ffn_state["h2"] = k_sb.ap()[:, 5, 1536:2048].bitcast(F32)
        ffn_state["lnfA"] = lnf_tm2
        ffn_state["lnfB"] = lnf2
        ffn_state["lnT"] = lnfT2
        pairs = [[XW[2], XW[3]], [XW[0], XW[1]]]
        dma("sync", wsb[:, 0:4, :],
            wscr["woB"].rearrange("c p d -> p c d")[:, 0:4, :], sm["s_wb0"])
        dma("gpsimd", wsb[:, 4:8, :],
            wscr["woB"].rearrange("c p d -> p c d")[:, 4:8, :], sm["s_wb1"])
        wsb_wait = [True]
        for qc in range(NQ2):
            q0 = qc * QC2
            kcnt = 4 * qc + 4
            npiece = (kcnt * 128 + 2047) // 2048
            p.wait("sync", PE, q_pe_rel[0])
            dma("sync", qb2[:, :, :],
                q_scr.rearrange("c p t -> p c t")[:, :, q0:q0 + QC2], sm["s_q0"])
            p.wait("tensor", sm["s_q0"])
            p.wait("tensor", DVE, ctxn_dve[0])

            xs_sems = [(sm["s_xa"], sm["s_xb"]), (sm["s_xc"], sm["s_xd"])]

            def ep_loads(sc2):
                rows0 = q0 + sc2 * 256
                for stt in range(2):
                    p.wait("sync", sm["s_stout"], 16 * (4 * qc - 2 + 2 * sc2))
                    p.wait("sync", DVE, pair_rel[sc2])
                    dma("sync", pairs[sc2][stt][:, :],
                        xs1[rows0 + stt * 128:rows0 + (stt + 1) * 128, :],
                        xs_sems[sc2][stt])

            def ep_stats(sc2):
                for stt in range(2):
                    p.wait("vector", xs_sems[sc2][stt])
                ln2_stats([pairs[sc2][0][:, :], pairs[sc2][1][:, :]], LN_EPS,
                          Jnk, st=sts[sc2])

            def issue_k_loads(pr):
                for pc in range(npiece):
                    sl = kg[0] % 3
                    kg[0] += 1
                    kslot_of[(qc, pr, pc)] = sl
                    p.wait("sync", PE, kpe_rel3[sl])
                    n_tok = min(2048, kcnt * 128 - pc * 2048)
                    dma("sync", kslots[sl][:, 0:n_tok],
                        k_scr[pr, :, pc * 2048:pc * 2048 + n_tok], ksems3[sl])
                    kuse3[sl] += 1
                    ktgt[(qc, pr, pc)] = 16 * kuse3[sl]

            for pr in range(8):
                if pr == 0:
                    issue_k_loads(0)
                def emit_av(kc, pos):
                    for hh in range(2):
                        p.wait("tensor", SC, exp_at[kc])
                        if mask_at[kc]:
                            p.wait("tensor", PL, mask_at[kc])
                        if pos == 0:
                            p.wait("tensor", DVE, ctxn_dve[0])
                        h = 2 * pr + hh
                        att = AP2[pos % 3][:, hh, :]
                        p.op("tensor", lambda e_, hh=hh, kc=kc, h=h, att=att, pos=pos, kcnt=kcnt: e_.matmul(
                            pb[4 + hh].ap()[0:65, :], v_sbB[:, kc, h, :],
                            att, start=(pos == 0), stop=(pos == kcnt - 1)),
                            PE if hh == 1 else None)

                exp_at = {}
                mask_at = {}
                pend = []
                if npiece == 1:
                    order = list(range(max(0, kcnt - 4), kcnt)) + \
                            list(range(0, max(0, kcnt - 4)))
                else:
                    order = list(range(kcnt))
                seen_pc = set()
                for pos, kc in enumerate(order):
                    pc = (kc * 128) // 2048
                    koff = kc * 128 - pc * 2048
                    half = pos % 2
                    ksl = kslot_of[(qc, pr, pc)]
                    if pc not in seen_pc:
                        seen_pc.add(pc)
                        p.wait("tensor", ksems3[ksl], ktgt[(qc, pr, pc)])
                    p.wait("tensor", SC, exp_done.get(half, 0))
                    if pos == 1:
                        p.wait("tensor", DVE, max(ctxn_dve[0], opj23_free[0]))
                    for hh in range(2):
                        bank = pb[2 * half + hh]
                        p.op("tensor", lambda e_, b=bank, hh=hh, ks=kslots[ksl], koff=koff, pr=pr: e_.matmul(
                            b.ap()[:, :], ks[hh * 64:(hh + 1) * 64, koff:koff + 128],
                            qb2[hh * 64:(hh + 1) * 64, pr, :],
                            start=True, stop=True, tile_position=(hh * 64, 0)), PE)
                    p.wait("scalar", PE)
                    if pos % 3 == 0:
                        p.wait("scalar", sm["s_cst"])
                    p.op("scalar", lambda e_, ap=AP2[pos % 3], half=half: e_.activation(
                        ap[:, :, :],
                        psQ.ap()[:, half * 1024:(half + 1) * 1024], AF.Exp,
                        bias=negC.ap()[:, 1:2]), SC)
                    exp_done[half] = SC.n
                    exp_at[kc] = SC.n
                    if kc >= kcnt - 4:
                        p.wait("gpsimd", SC)
                        for hh in range(2):
                            p.op("gpsimd", lambda e_, a_=AP2[pos % 3][:, hh, :], kc=kc, q0=q0: e_.affine_select(
                                out=a_, in_=a_,
                                compare_op=ALU.is_ge, fill=0.0,
                                base=q0 - kc * 128, pattern=[[1, QC2]],
                                channel_multiplier=-1), PL)
                        mask_at[kc] = PL.n
                    else:
                        mask_at[kc] = 0
                    if len(pend) == 2:
                        emit_av(*pend.pop(0))
                    pend.append((kc, pos))
                for pk, pp in pend:
                    emit_av(pk, pp)
                for pc in range(npiece):
                    kpe_rel3[kslot_of[(qc, pr, pc)]] = PE.n
                if pr < 7:
                    issue_k_loads(pr + 1)
                p.wait("scalar", PE)
                p.wait("scalar", DVE, opj23_free[0])
                p.op("scalar", lambda e_: e_.activation(
                    YT[64:65, 0:512], pb[4].ap()[64:65, :], AF.Ln,
                    bias=epsb.ap()[64:65, 2:3]), SC)
                p.op("scalar", lambda e_: e_.activation(
                    NORM1[64:65, 0:512], pb[5].ap()[64:65, :], AF.Ln,
                    bias=epsb.ap()[64:65, 2:3]), SC)
                p.wait("scalar", SC)
                p.op("scalar", lambda e_: e_.activation(
                    YT[64:65, 512:1024], YT[64:65, 0:512], AF.Exp,
                    scale=-1.0), SC)
                e0 = SC.n
                p.op("scalar", lambda e_: e_.activation(
                    NORM1[64:65, 512:1024], NORM1[64:65, 0:512], AF.Exp,
                    scale=-1.0), SC)
                e1 = SC.n
                p.wait("tensor", SC, e0)
                p.wait("tensor", DVE)
                p.op("tensor", lambda e_: e_.matmul(
                    pb[2].ap()[0:64, :], ones_rowf.ap()[64:65, 0:64],
                    YT[64:65, 512:1024], start=True, stop=True), PE)
                b0 = PE.n
                p.wait("tensor", SC, e1)
                p.op("tensor", lambda e_: e_.matmul(
                    pb[3].ap()[0:64, :], ones_rowf.ap()[64:65, 0:64],
                    NORM1[64:65, 512:1024], start=True, stop=True), PE)
                b1 = PE.n
                p.wait("vector", PE, b0)
                p.wait("vector", DVE)
                p.op("vector", lambda e_: e_.tensor_copy(
                    recipB[0:64, 0:QC2], pb[2].ap()[0:64, :]), DVE)
                p.wait("vector", PE, b1)
                p.wait("vector", DVE)
                p.op("vector", lambda e_: e_.tensor_copy(
                    NORM1[0:64, 0:512], pb[3].ap()[0:64, :]), DVE)
                p.wait("vector", DVE)
                p.op("vector", lambda e_, pr=pr: e_.tensor_tensor(
                    ctxN2[0:64, pr, :], pb[4].ap()[0:64, :],
                    recipB[0:64, 0:QC2], ALU.mult), DVE)
                p.wait("vector", sm["s_cst"])
                p.wait("vector", DVE)
                p.op("vector", lambda e_, pr=pr: e_.tensor_tensor(
                    attb2[0:64, 0, :], pb[5].ap()[0:64, :],
                    NORM1[0:64, 0:512], ALU.mult), DVE)
                p.wait("gpsimd", DVE)
                dma("gpsimd", ctxN2[64:128, pr, :], attb2[0:64, 0, :],
                    sm["s_cst"])
                ctxn_dve[0] = DVE.n
                if pr < len(pendf):
                    fn, kind, sc2i = pendf[pr]
                    fn()
                    if kind == "b":
                        ctxn_dve[0] = max(ctxn_dve[0], ffn_state["sevac"])
                    elif kind == "2":
                        ctxn_dve[0] = DVE.n
                        pair_rel[sc2i] = DVE.n
                if pr == 3:
                    ep_loads(0)
                if pr == 5:
                    ep_stats(0)
                if pr == 7:
                    ep_loads(1)
            pendf = []
            ctx_ready = DVE.n
            q_pe_rel[0] = PE.n
            p.wait("tensor", sm["s_cst"])
            if debug:
                p.wait("gpsimd", DVE, ctx_ready)
                p.wait("gpsimd", sm["s_cst"])
                dma("gpsimd", dbg2[qc], ctxN2[:, :, :], sm["s_dbg"])
                p.wait("tensor", sm["s_dbg"])
            sc_pre = SC.n
            p.wait("tensor", DVE, ctx_ready)
            ep_stats(1)

            def ep_oproj_pass(sc2, stt, bks, dve_gate):
                for hc in range(8):
                    if hc == 0:
                        if wsb_wait[0]:
                            p.wait("tensor", sm["s_wb0"])
                            p.wait("tensor", sm["s_wb1"])
                            wsb_wait[0] = False
                        p.wait("tensor", SC, sc_pre)
                        if dve_gate:
                            p.wait("tensor", DVE, dve_gate)
                    for oc in range(2):
                        p.op("tensor", lambda e_, b=bks[oc], hc=hc, stt=stt, oc=oc, sc2=sc2: e_.matmul(
                            b.ap()[:, :],
                            ctxN2[:, hc, sc2 * 256 + stt * 128:sc2 * 256 + (stt + 1) * 128],
                            wsb[:, hc, oc * 512:(oc + 1) * 512],
                            start=(hc == 0), stop=(hc == 7)),
                            PE if (hc == 7 and oc == 1) else None)
                return PE.n

            def ep_adds(sc2, stt, bks, ydst, pass_pe):
                ln2_affine(stt, pairs[sc2][stt][:, :], YT[:, :], st=sts[sc2])
                p.wait("vector", PE, pass_pe)
                p.wait("vector", DVE)
                for oc in range(2):
                    p.op("vector", lambda e_, b=bks[oc], oc=oc, ydst=ydst: e_.tensor_tensor(
                        ydst[:, oc * 512:(oc + 1) * 512], b.ap()[:, :],
                        YT[:, oc * 512:(oc + 1) * 512], ALU.add), DVE)
                return DVE.n

            def ep_lnb(sc2):
                rows0 = q0 + sc2 * 256
                ys = (Y2[2 * sc2], Y2[2 * sc2 + 1])
                ln2_stats([ys[0][:, :], ys[1][:, :]], LNB_EPS, Jnk,
                          st=sts[sc2])
                for stt in range(2):
                    y = ys[stt]
                    ln2_affine(stt, y[:, :], y[:, :], st=sts[sc2])
                    p.wait("vector", DVE)
                    p.op("vector", lambda e_, y=y: e_.tensor_tensor(
                        y[:, :], y[:, :], gamB[:, :], ALU.mult), DVE)
                    p.wait("vector", DVE)
                    p.op("vector", lambda e_, y=y: e_.tensor_tensor(
                        y[:, :], y[:, :], betB[:, :], ALU.add), DVE)
                    p.wait("vector", DVE)
                    p.op("vector", lambda e_, y=y, x=pairs[sc2][stt]: e_.tensor_tensor(
                        x[:, :], x[:, :], y[:, :], ALU.add), DVE)
                    if debug:
                        p.wait("gpsimd", DVE)
                        dma("gpsimd", dbg[rows0 + stt * 128:rows0 + (stt + 1) * 128, :],
                            pairs[sc2][stt][:, :], sm["s_dbg"])
                p.wait("vector", sm["s_dbg"])

            a0 = ep_oproj_pass(0, 0, (pb[4], pb[5]), None)
            b0 = ep_oproj_pass(0, 1, (pb[2], pb[3]), None)
            yA0 = ep_adds(0, 0, (pb[4], pb[5]), Y2[0], a0)
            yB0 = ep_adds(0, 1, (pb[2], pb[3]), Y2[1], b0)
            a1 = ep_oproj_pass(1, 0, (pb[4], pb[5]), yA0)
            b1 = ep_oproj_pass(1, 1, (pb[2], pb[3]), yB0)
            ep_adds(1, 0, (pb[4], pb[5]), Y2[2], a1)
            yB1 = ep_adds(1, 1, (pb[2], pb[3]), Y2[3], b1)
            ctxn_dve[0] = DVE.n
            opj23_free[0] = yB1
            pendf = [
                (lambda f=ep_lnb: f(0), "a", None),
                (lambda: ffn_part1a([XW[2][:, :], XW[3][:, :]]), "a", None),
                (ffn_part1b, "b", None),
                (lambda q0=q0: ffn_part2([XW[2][:, :], XW[3][:, :]], q0, out,
                                         sm["s_stout"]), "2", 0),
                (lambda f=ep_lnb: f(1), "a", None),
                (lambda: ffn_part1a([XW[0][:, :], XW[1][:, :]]), "a", None),
                (ffn_part1b, "b", None),
                (lambda q0=q0: ffn_part2([XW[0][:, :], XW[1][:, :]], q0 + 256,
                                         out, sm["s_stout"]), "2", 1),
            ]
        for fn, _, _ in pendf:
            fn()

    phase_a(False)
    barrier()
    phase_1b()
    barrier()
    phase_a(True)
    barrier()
    phase_2b()
    p.wait("sync", sm["s_stout"])
    p.wait("sync", sm["s_stx"])
    p.wait("sync", sm["s_dbg"])
    p.wait("sync", sm["s_cst"])

    p.finalize()
    with nc.Block() as block:
        @block.sync
        def _(eng):
            for fn in p.streams["sync"]:
                fn(eng)

        @block.gpsimd
        def _(eng):
            for fn in p.streams["gpsimd"]:
                fn(eng)

        @block.tensor
        def _(eng):
            for fn in p.streams["tensor"]:
                fn(eng)

        @block.vector
        def _(eng):
            for fn in p.streams["vector"]:
                fn(eng)

        @block.scalar
        def _(eng):
            for fn in p.streams["scalar"]:
                fn(eng)

    ctx_stack.close()
    return nc


_CACHE = {}


def kernel(L0_attn_a_in_features, L0_attn_a_w_q, L0_attn_a_w_k, L0_attn_a_w_v,
           L0_attn_a_w_o, L0_attn_b_Wq, L0_attn_b_Wk, L0_attn_b_Wv, L0_attn_b_Wo,
           L0_attn_b_gamma, L0_attn_b_beta, L0_ffn_Win, L0_ffn_bin_, L0_ffn_W1,
           L0_ffn_b1, L0_ffn_W2, L0_ffn_b2, L0_ffn_Wout, L0_ffn_bout):
    x = np.ascontiguousarray(np.asarray(L0_attn_a_in_features, dtype=np.float32))
    B = x.shape[0]
    if "nc" not in _CACHE:
        _CACHE["nc"] = build_nc()
    nc = _CACHE["nc"]
    c = lambda a: np.ascontiguousarray(np.asarray(a, dtype=np.float32))
    w = dict(wq_a=c(L0_attn_a_w_q), wk_a=c(L0_attn_a_w_k), wv_a=c(L0_attn_a_w_v),
             wo_a=c(L0_attn_a_w_o), wq_b=c(L0_attn_b_Wq), wk_b=c(L0_attn_b_Wk),
             wv_b=c(L0_attn_b_Wv), wo_b=c(L0_attn_b_Wo), gam=c(L0_attn_b_gamma),
             bet=c(L0_attn_b_beta), f_win=c(L0_ffn_Win), f_bin=c(L0_ffn_bin_),
             f_w1=c(L0_ffn_W1), f_b1=c(L0_ffn_b1), f_w2=c(L0_ffn_W2),
             f_b2=c(L0_ffn_b2), f_wout=c(L0_ffn_Wout), f_bout=c(L0_ffn_bout))
    in_maps = [dict(w, x=np.ascontiguousarray(x[b])) for b in range(B)]
    res = run_bass_kernel_spmd(nc, in_maps, core_ids=list(range(B)))
    return np.stack([res.results[b]["out"] for b in range(B)], axis=0)


# version 64
# speedup vs baseline: 1.0318x; 1.0082x over previous
import numpy as np
import concourse.bass as bass
import concourse.mybir as mybir
from concourse.bass_utils import run_bass_kernel_spmd
from contextlib import ExitStack

F32 = mybir.dt.float32
BF16 = mybir.dt.bfloat16
AF = mybir.ActivationFunctionType
ALU = mybir.AluOpType
AXX = mybir.AxisListType.X

S, DM = 4096, 1024
NH = 16
NTC, TOKC = 8, 512
NQ1, QS1 = 16, 256
NQ2, QC2 = 8, 512
LN_EPS = 1e-5
LNB_EPS = 1e-3

ENGS = ("sync", "gpsimd", "tensor", "vector", "scalar")


class Sem:
    __slots__ = ("h", "n")

    def __init__(self, h):
        self.h = h
        self.n = 0


class Prog:

    def __init__(self):
        self.streams = {k: [] for k in ENGS}

    def op(self, eng, build, sem=None, inc=1):
        if sem is None:
            self.streams[eng].append(("op", build, None, 0, 0))
        else:
            sem.n += inc
            self.streams[eng].append(("op", build, sem, inc, sem.n))

    def wait(self, eng, sem, val=None):
        v = sem.n if val is None else val
        if v <= 0:
            return
        self.streams[eng].append(("wait", sem, v))

    def finalize(self):
        awaited = {}
        for eng in ENGS:
            for ent in self.streams[eng]:
                if ent[0] == "wait":
                    _, sem, v = ent
                    awaited.setdefault(id(sem), set()).add(v)
        rank = {}
        for eng in ENGS:
            for ent in self.streams[eng]:
                if ent[0] == "op" and ent[2] is not None and ent[3] == 1:
                    sid = id(ent[2])
                    if sid not in rank:
                        rank[sid] = {v: i + 1 for i, v in
                                     enumerate(sorted(awaited.get(sid, ())))}
        out = {k: [] for k in ENGS}
        for eng in ENGS:
            seen = {}
            for ent in self.streams[eng]:
                if ent[0] == "op":
                    _, build, sem, inc, cnt = ent
                    if sem is None:
                        out[eng].append(build)
                    elif inc != 1:
                        out[eng].append(lambda e, b=build, h=sem.h, k=inc:
                                        b(e).then_inc(h, k))
                    else:
                        r = rank[id(sem)].get(cnt)
                        if r is None:
                            out[eng].append(build)
                        else:
                            out[eng].append(lambda e, b=build, h=sem.h:
                                            b(e).then_inc(h, 1))
                else:
                    _, sem, v = ent
                    sid = id(sem)
                    if sid in rank:
                        vv = rank[sid].get(v)
                        assert vv is not None
                    else:
                        vv = v
                    if seen.get(sid, 0) >= vv:
                        continue
                    seen[sid] = vv
                    out[eng].append(lambda e, h=sem.h, v=vv: e.wait_ge(h, v))
        self.streams = out


def build_nc(debug=False):
    nc = bass.Bass("TRN2", target_bir_lowering=False)

    x_in = nc.dram_tensor("x", [S, DM], F32, kind="ExternalInput").ap()
    names = ["wq_a", "wk_a", "wv_a", "wo_a", "wo_b"]
    dws = {n: nc.dram_tensor(n, [DM, DM], F32, kind="ExternalInput").ap() for n in names}
    for n in ("wq_b", "wk_b", "wv_b"):
        dws[n] = nc.dram_tensor(n, [NH, DM, 64], F32, kind="ExternalInput").ap()
    gam = nc.dram_tensor("gam", [DM], F32, kind="ExternalInput").ap()
    bet = nc.dram_tensor("bet", [DM], F32, kind="ExternalInput").ap()
    f_win = nc.dram_tensor("f_win", [DM, 64], F32, kind="ExternalInput").ap()
    f_bin = nc.dram_tensor("f_bin", [64], F32, kind="ExternalInput").ap()
    f_w1 = nc.dram_tensor("f_w1", [64, 16], F32, kind="ExternalInput").ap()
    f_b1 = nc.dram_tensor("f_b1", [64, 16], F32, kind="ExternalInput").ap()
    f_w2 = nc.dram_tensor("f_w2", [64, 16], F32, kind="ExternalInput").ap()
    f_b2 = nc.dram_tensor("f_b2", [64], F32, kind="ExternalInput").ap()
    f_wout = nc.dram_tensor("f_wout", [64, DM], F32, kind="ExternalInput").ap()
    f_bout = nc.dram_tensor("f_bout", [DM], F32, kind="ExternalInput").ap()
    out = nc.dram_tensor("out", [S, DM], F32, kind="ExternalOutput").ap()
    dbg = nc.dram_tensor("dbg", [S, DM], F32, kind="Internal" if not debug else "ExternalOutput").ap()
    dbg2 = nc.dram_tensor("dbg2", [NQ2, 128, 8, QC2], BF16, kind="Internal" if not debug else "ExternalOutput").ap()

    xs1 = nc.dram_tensor("xs1", [S, DM], F32, kind="Internal").ap()
    q_scr = nc.dram_tensor("q_scr", [8, 128, S], BF16, kind="Internal").ap()
    k_scr = nc.dram_tensor("k_scr", [8, 128, S], BF16, kind="Internal").ap()
    wscr = {n: nc.dram_tensor(n + "_s", [8, 128, DM], BF16, kind="Internal").ap()
            for n in ("wqT", "wkT", "wvT", "woT", "wqB", "wkB", "wvB", "woB")}

    p = Prog()
    ctx_stack = ExitStack()
    e = ctx_stack.enter_context

    k_sb = e(nc.sbuf_tensor("k_sb", [128, 8, S], BF16))
    v_sb = e(nc.sbuf_tensor("v_sb", [128, 32, 1040], BF16))
    v_sbB = v_sb.ap().rearrange("p a (h c) -> p a h c", h=16)

    ident = e(nc.sbuf_tensor("ident", [128, 128], BF16))
    ones_colb = e(nc.sbuf_tensor("ones_colb", [128, 1], BF16))
    ones_rowf = e(nc.sbuf_tensor("ones_rowf", [128, 128], F32))
    negC = e(nc.sbuf_tensor("negC", [128, 2], F32))
    nmax = e(nc.sbuf_tensor("nmax", [1, 8], F32))
    stat = e(nc.sbuf_tensor("stat", [128, 16], F32))
    win_sb = e(nc.sbuf_tensor("win_sb", [128, 8, 128], BF16))
    stack2 = e(nc.sbuf_tensor("stack2", [128, 64], F32))
    bin_sb = e(nc.sbuf_tensor("bin_sb", [128, 1], F32))
    w1p = e(nc.sbuf_tensor("w1p", [128, 8], F32))
    b1p = e(nc.sbuf_tensor("b1p", [128, 8], F32))
    w2p = e(nc.sbuf_tensor("w2p", [128, 8], F32))
    b2_sb = e(nc.sbuf_tensor("b2_sb", [64, 1], F32))
    wout_aug = e(nc.sbuf_tensor("wout_aug", [65, DM], BF16))
    g_aug = e(nc.sbuf_tensor("g_aug", [65, 256], BF16))
    epsb = e(nc.sbuf_tensor("epsb", [128, 3], F32))

    arena_b = e(nc.sbuf_tensor("arena_b", [128, 39 * 512], BF16))
    ab = arena_b.ap()
    arena_f = e(nc.sbuf_tensor("arena_f", [128, 34 * 256], F32))
    af = arena_f.ap()

    def view(base_ap, unit, off_kib, *shape):
        n = 1
        for s_ in shape:
            n *= s_
        o = int(off_kib * unit)
        flat = base_ap[:, o:o + n]
        if len(shape) == 1:
            return flat
        spec = "p (" + " ".join(f"d{i}" for i in range(len(shape))) + ") -> p " + \
               " ".join(f"d{i}" for i in range(len(shape)))
        return flat.rearrange(spec, **{f"d{i}": shape[i] for i in range(len(shape))})

    def bv(off, *shape):
        return view(ab, 512, off, *shape)

    def fv(off, *shape):
        return view(af, 256, off, *shape)

    wbuf = bv(0, 3, 4, DM)
    lnx_tm = bv(24, 2, DM)
    lnxT = bv(28, 8, TOKC)
    qstage = bv(36, 1, TOKC)
    nsq = bv(37, 2, TOKC)
    attT1 = bv(0, 32, QS1)
    qb1 = bv(16, 8, TOKC)
    wostr = bv(24, 2, DM)
    lnf2b = bv(24, DM)
    lnf_tm = bv(28, DM)
    lnfT = bv(30, 8, 256)
    ctxN1 = bv(34, 8, QS1)
    kstr2 = bv(0, 2, 2048)
    attb2 = bv(8, 4, QC2)
    ctxN2 = bv(12, 8, QC2)
    qb2 = bv(20, 8, QC2)
    wostr2 = bv(28, 2, DM)
    lnf2 = bv(28, DM)
    lnf_tm2 = bv(32, DM)
    lnfT2 = bv(34, 8, 256)

    xin = fv(0, 2, DM)
    xin2 = fv(24, 2, DM)
    wstage = fv(8, 4, DM)
    gamB = fv(8, DM)
    betB = fv(12, DM)
    XW = [fv(0, DM), fv(4, DM), fv(16, DM), fv(20, DM)]
    qstage2 = af[:, 16 * 256:16 * 256 + 256].bitcast(BF16)
    YT = fv(24, DM)
    recipB = fv(28, 512)
    s2w = fv(30, 256)
    h_w = fv(31, 256)
    g2a = fv(32, 256)
    g2b = fv(33, 256)

    psQ = e(nc.psum_tensor("psQ", [128, 2048], F32))
    pc4 = e(nc.psum_tensor("pc4", [128, 512], F32))
    pc5 = e(nc.psum_tensor("pc5", [128, 512], F32))
    tb = [e(nc.psum_tensor(f"tb{i}", [128, 1024], BF16)) for i in range(2)]
    tbv = [t.ap().rearrange("p (a b) -> p a b", a=8) for t in tb]

    class APH:
        __slots__ = ("_a",)
        def __init__(self, a):
            self._a = a
        def ap(self):
            return self._a
    pb = [APH(psQ.ap()[:, i * 512:(i + 1) * 512]) for i in range(4)] +          [APH(pc4.ap()), APH(pc5.ap())]

    sem_names = ["s_pe", "s_dve", "s_sc", "s_pl",
                 "s_x0", "s_x1", "s_w0", "s_w1", "s_w2",
                 "s_q0", "s_q1", "s_k0", "s_k1", "s_wo0", "s_wo1",
                 "s_m0", "s_stq0", "s_stq1", "s_stk0", "s_stk1",
                 "s_stx", "s_stout", "s_stw", "s_xa", "s_xb", "s_dbg", "s_cst",
                 "s_pl0", "s_pl1", "s_pl2", "s_pl3", "s_ps0", "s_ps1",
                 "s_ph0", "s_ph1", "s_wb0", "s_wb1", "s_xc", "s_xd", "s_k2",
                 "s_xe", "s_wc0", "s_wc1"]
    sm = {n: Sem(e(nc.semaphore(n))) for n in sem_names}
    PE, DVE, SC, PL = sm["s_pe"], sm["s_dve"], sm["s_sc"], sm["s_pl"]
    DMA_SEMS = [sm[n] for n in sem_names[4:]]

    def dma(eng, out_ap, in_ap, sem):
        p.op(eng, lambda e_, o=out_ap, i=in_ap: e_.dma_start(out=o, in_=i), sem, 16)

    def barrier():
        for eng in ENGS:
            for s_ in (PE, DVE, SC, PL):
                p.wait(eng, s_)
            for s_ in DMA_SEMS:
                p.wait(eng, s_)

    from concourse.hw_specs import get_activation_tables
    set_id = None
    for idx, (nmset, funcs) in enumerate(get_activation_tables(nc.m.arch).items()):
        if nmset == "natural_log_exp_and_others":
            set_id = idx
    assert set_id is not None

    p.op("scalar", lambda e_: e_.add_instruction(mybir.InstLoadActFuncSet(
        name=e_.bass.get_next_instruction_name(), act_func_set_id=set_id,
        ins=[], outs=[])), SC)

    p.op("vector", lambda e_: e_.memset(ident[:, :], 0.0), DVE)
    p.wait("gpsimd", DVE)
    p.op("gpsimd", lambda e_: e_.affine_select(
        out=ident[:, :], in_=ident[:, :], compare_op=ALU.not_equal, fill=1.0,
        base=0, pattern=[[-1, 128]], channel_multiplier=1), PL)
    p.op("vector", lambda e_: e_.memset(ones_colb.ap()[:, :], 1.0), DVE)
    p.op("vector", lambda e_: e_.memset(ones_rowf.ap()[:, :], 1.0), DVE)
    p.op("vector", lambda e_: e_.memset(nmax.ap()[:, :], 0.0), DVE)
    p.op("vector", lambda e_: e_.memset(g_aug.ap()[64:65, :], 1.0), DVE)
    p.op("vector", lambda e_: e_.memset(v_sb.ap()[:, :, 1024:1025], 1.0), DVE)
    p.op("vector", lambda e_: e_.memset(epsb.ap()[:, 0:1], LN_EPS), DVE)
    p.op("vector", lambda e_: e_.memset(epsb.ap()[:, 1:2], LNB_EPS), DVE)
    p.op("vector", lambda e_: e_.memset(epsb.ap()[:, 2:3], 1e-30), DVE)
    p.op("vector", lambda e_: e_.memset(stack2.ap()[:, :], 0.0), DVE)
    p.wait("gpsimd", DVE)
    p.op("gpsimd", lambda e_: e_.affine_select(
        out=stack2.ap()[:, :], in_=stack2.ap()[:, :], compare_op=ALU.not_equal,
        fill=1.0, base=0, pattern=[[-1, 64]], channel_multiplier=1), PL)
    p.wait("gpsimd", PL)
    p.op("gpsimd", lambda e_: e_.affine_select(
        out=stack2.ap()[:, :], in_=stack2.ap()[:, :], compare_op=ALU.not_equal,
        fill=1.0, base=-64, pattern=[[-1, 64]], channel_multiplier=1), PL)
    p.wait("tensor", PL)
    p.wait("tensor", DVE)

    sm0 = sm["s_m0"]
    dma("sync", bin_sb.ap()[0:64, :], f_bin.rearrange("(a b) -> a b", b=1), sm["s_xb"])
    dma("sync", bin_sb.ap()[64:128, :], f_bin.rearrange("(a b) -> a b", b=1), sm["s_xb"])
    dma("sync", b2_sb.ap()[:, :], f_b2.rearrange("(a b) -> a b", b=1), sm["s_xb"])
    for src, dst in ((f_w1, w1p), (f_b1, b1p), (f_w2, w2p)):
        dma("sync", dst.ap()[0:64, :], src[:, 0:8], sm["s_xb"])
        dma("sync", dst.ap()[64:128, :], src[:, 8:16], sm["s_xb"])
    dma("sync", wstage[:, 0, 0:512].rearrange("p (c x) -> p c x", c=8),
        f_win.rearrange("(c p) x -> p c x", p=128), sm0)
    p.wait("vector", sm0)
    p.op("vector", lambda e_: e_.tensor_copy(
        win_sb.ap()[:, :, 0:64], wstage[:, 0, 0:512].rearrange("p (c x) -> p c x", c=8)), DVE)
    p.op("vector", lambda e_: e_.tensor_copy(
        win_sb.ap()[:, :, 64:128], wstage[:, 0, 0:512].rearrange("p (c x) -> p c x", c=8)), DVE)
    p.wait("sync", DVE)
    dma("sync", wstage[0:64, 2, :], f_wout, sm0)
    dma("sync", wstage[64:65, 2, :], f_bout.rearrange("(a b) -> a b", a=1), sm0)
    p.wait("vector", sm0)
    p.op("vector", lambda e_: e_.tensor_copy(wout_aug.ap()[0:64, :], wstage[0:64, 2, :]), DVE)
    p.op("vector", lambda e_: e_.tensor_copy(wout_aug.ap()[64:65, :], wstage[64:65, 2, :]), DVE)

    smw_o = {1: sm["s_ps1"], 2: sm["s_ps0"]}
    smw_h = [sm["s_ph0"], sm["s_ph1"]]
    ALL_STORE_SEMS = (sm["s_ps0"], sm["s_ps1"], sm["s_ph0"], sm["s_ph1"])
    pslot_sems = [sm["s_pl0"], sm["s_pl1"], sm["s_pl2"], sm["s_pl3"]]
    pst = {"tq": 0, "cast_rel": {}, "tgt": {}, "tp_rel": {}, "st_rel": {}}
    pst["cast_rel"][0] = DVE.n
    pst["cast_rel"][2] = DVE.n

    def t_unit(src, dst, scale, ecg):
        def load(eng, sl):
            dma(eng, wstage[:, sl, :], src[ecg * 128:(ecg + 1) * 128, :],
                pslot_sems[sl])

        def proc(sl):
            k = pst["tq"]
            pst["tq"] += 1
            tbk = k % 2
            oslot = 1 + tbk
            p.wait("vector", pslot_sems[sl], pst["tgt"][sl])
            p.wait("vector", PE, pst["tp_rel"].get(sl, 0))
            p.op("vector", lambda e_, sl=sl: e_.tensor_copy(
                wbuf[:, 0, sl, :], wstage[:, sl, :]), DVE)
            pst["cast_rel"][sl] = DVE.n
            p.wait("tensor", DVE)
            for dc in range(8):
                p.op("tensor", lambda e_, sl=sl, dc=dc, tbk=tbk: e_.transpose(
                    tbv[tbk][:, dc, :], wbuf[:, 0, sl, dc * 128:(dc + 1) * 128],
                    ident[:, :]), PE if dc == 7 else None)
            pst["tp_rel"][sl] = PE.n
            p.wait("vector", PE)
            p.wait("vector", smw_o[oslot], pst["st_rel"].get(oslot, 0))
            p.op("vector", lambda e_, oslot=oslot, tbk=tbk, sc=scale: e_.tensor_scalar(
                wbuf[:, oslot, 0, :].rearrange("p (c x) -> p c x", c=8),
                tbv[tbk][:, :, :], sc, None, ALU.mult), DVE)
            seng = "scalar"
            p.wait(seng, DVE)
            dma(seng, dst.rearrange("c p d -> p c d")[:, :, ecg * 128:(ecg + 1) * 128],
                wbuf[:, oslot, 0, :].rearrange("p (c x) -> p c x", c=8), smw_o[oslot])
            pst["st_rel"][oslot] = smw_o[oslot].n

        return load, proc

    def h_unit(src_hde, dst, scale, key, h):
        acc = wbuf.rearrange("p s a d -> p (s a d)")[:, 0:8192].rearrange(
            "p (c x) -> p c x", c=8)

        def load(eng, sl):
            dma(eng, wstage[:, sl, 0:512].rearrange("p (c x) -> p c x", c=8),
                src_hde[h].rearrange("(c p) x -> p c x", p=128), pslot_sems[sl])

        def proc(sl):
            if h == 0:
                if key == 0:
                    p.wait("vector", PE)
                    p.wait("vector", smw_o[1])
                else:
                    p.wait("vector", smw_h[0])
                    p.wait("vector", smw_h[1])
            p.wait("vector", pslot_sems[sl], pst["tgt"][sl])
            p.op("vector", lambda e_, sl=sl, h=h, sc=scale: e_.tensor_scalar(
                acc[:, :, h * 64:(h + 1) * 64],
                wstage[:, sl, 0:512].rearrange("p (c x) -> p c x", c=8),
                sc, None, ALU.mult), DVE)
            pst["cast_rel"][sl] = DVE.n
            if h % 2 == 1:
                par = (h // 2) % 2
                seng = "scalar"
                p.wait(seng, DVE)
                p.wait(seng, smw_h[par], pst.get(("hst", par), 0))
                dma(seng, dst.rearrange("c p d -> p c d")[:, :, (h - 1) * 64:(h + 1) * 64],
                    acc[:, :, (h - 1) * 64:(h + 1) * 64], smw_h[par])
                pst[("hst", par)] = smw_h[par].n

        return load, proc

    def p_unit(src, dst, c):
        def load(eng, sl):
            dma(eng, wstage[:, sl, :],
                src.rearrange("(c p) d -> p c d", p=128)[:, c, :], pslot_sems[sl])

        def proc(sl):
            k = pst["tq"]
            pst["tq"] += 1
            oslot = 1 + k % 2
            if c == 0:
                p.wait("vector", smw_h[0])
                p.wait("vector", smw_h[1])
            p.wait("vector", pslot_sems[sl], pst["tgt"][sl])
            p.wait("vector", smw_o[oslot], pst["st_rel"].get(oslot, 0))
            p.op("vector", lambda e_, sl=sl, oslot=oslot: e_.tensor_copy(
                wbuf[:, oslot, 0, :], wstage[:, sl, :]), DVE)
            pst["cast_rel"][sl] = DVE.n
            seng = "scalar"
            p.wait(seng, DVE)
            dma(seng, dst.rearrange("c p d -> p c d")[:, c, :],
                wbuf[:, oslot, 0, :], smw_o[oslot])
            pst["st_rel"][oslot] = smw_o[oslot].n

        return load, proc

    units = []
    for nm, dn, sc in (("wq_a", "wqT", 1.0 / 32.0), ("wk_a", "wkT", 1.0),
                       ("wo_a", "woT", 1.0)):
        units += [t_unit(dws[nm], wscr[dn], sc, ecg) for ecg in range(8)]
    for key, (nm, dn, sc) in enumerate((("wq_b", "wqB", 0.125),
                                        ("wk_b", "wkB", 1.0),
                                        ("wv_b", "wvB", 1.0))):
        units += [h_unit(dws[nm], wscr[dn], sc, key, h) for h in range(NH)]
    units += [p_unit(dws["wo_b"], wscr["woB"], c) for c in range(8)]

    def issue_load(i):
        eng = "sync" if i % 2 == 0 else "gpsimd"
        sl = i % 4
        p.wait(eng, DVE, pst["cast_rel"].get(sl, 0))
        units[i][0](eng, sl)
        pst["tgt"][sl] = pslot_sems[sl].n

    for i in range(min(4, len(units))):
        issue_load(i)
    for i in range(len(units)):
        units[i][1](i % 4)
        if i + 4 < len(units):
            issue_load(i + 4)
    wvres = wbuf.rearrange("p s a d -> p (s a d)")[:, 0:8192].rearrange(
        "p (c x) -> p c x", c=8)
    wostg = wbuf.rearrange("p s a d -> p (s a d)")[:, 8192:12288].rearrange(
        "p (c x) -> p c x", c=8)
    for s_ in ALL_STORE_SEMS:
        p.wait("sync", s_)
        p.wait("gpsimd", s_)
        p.wait("vector", s_)
    p.wait("vector", PE)
    wvo = {}
    for kc in range(8):
        sl = kc % 2
        eng = "sync" if sl == 0 else "gpsimd"
        p.wait(eng, DVE, wvo.get(("c", sl), pst["cast_rel"].get(sl, 0)))
        dma(eng, wstage[:, sl, :], dws["wv_a"][kc * 128:(kc + 1) * 128, :],
            pslot_sems[sl])
        p.wait("vector", pslot_sems[sl], pslot_sems[sl].n)
        p.op("vector", lambda e_, sl=sl, kc=kc: e_.tensor_copy(
            wvres[:, kc, :], wstage[:, sl, :]), DVE)
        wvo[("c", sl)] = DVE.n
        pst["cast_rel"][sl] = DVE.n
    for ih in range(2):
        wcsem = sm["s_wc0"] if ih == 0 else sm["s_wc1"]
        p.wait("sync", PE, wvo.get("pe", 0))
        dma("sync", wostg,
            wscr["woT"].rearrange("c p d -> p c d")[:, :, ih * 512:(ih + 1) * 512],
            wcsem)
        p.wait("tensor", wcsem)
        if ih == 0:
            p.wait("tensor", DVE)
        for j in range(8):
            bank = pb[4 + j % 2]
            p.wait("tensor", DVE, wvo.get(("ev", j % 2), 0))
            for kc in range(8):
                p.op("tensor", lambda e_, b=bank, j=j, kc=kc: e_.matmul(
                    b.ap()[:, :], wvres[:, kc, j * 128:(j + 1) * 128],
                    wostg[:, kc, :], start=(kc == 0), stop=(kc == 7)),
                    PE if kc == 7 else None)
            stg = wstage[:, 2 + j % 2, 0:256].bitcast(BF16)
            p.wait("vector", PE)
            ssem = smw_o[1 + j % 2]
            p.wait("vector", ssem, wvo.get(("st", j % 2), 0))
            p.op("vector", lambda e_, b=bank, d=stg: e_.tensor_copy(
                d, b.ap()[:, :]), DVE)
            wvo[("ev", j % 2)] = DVE.n
            seng = "scalar"
            p.wait(seng, DVE)
            dma(seng, wscr["wvT"].rearrange("c p d -> p c d")[:, j, ih * 512:(ih + 1) * 512],
                stg, ssem)
            wvo[("st", j % 2)] = ssem.n
        wvo["pe"] = PE.n

    for s_ in (sm["s_ps0"], sm["s_ps1"], sm["s_ph0"], sm["s_ph1"]):
        p.wait("sync", s_)
        p.wait("gpsimd", s_)

    p.wait("sync", pslot_sems[1])
    p.wait("sync", DVE, pst["cast_rel"].get(0, 0))
    p.wait("sync", DVE, pst["cast_rel"].get(1, 0))
    dma("sync", gamB[0:1, :], gam.rearrange("(a b) -> a b", a=1), sm0)
    dma("sync", betB[0:1, :], bet.rearrange("(a b) -> a b", a=1), sm0)
    p.wait("tensor", sm0)
    p.wait("tensor", PL)
    p.wait("vector", pslot_sems[0])
    p.wait("vector", pslot_sems[1])
    for srcrow, dstB in ((gamB, gamB), (betB, betB)):
        p.wait("tensor", DVE)
        for oc in range(2):
            p.op("tensor", lambda e_, s=srcrow, oc=oc: e_.matmul(
                pb[oc].ap()[:, :], ones_rowf.ap()[0:1, :],
                s[0:1, oc * 512:(oc + 1) * 512], start=True, stop=True),
                PE)
        p.wait("vector", PE)
        for oc in range(2):
            p.op("vector", lambda e_, d=dstB, oc=oc: e_.tensor_copy(
                d[:, oc * 512:(oc + 1) * 512], pb[oc].ap()[:, :]), DVE)
    p.wait("tensor", DVE)
    prep_tb_rel = DVE.n
    prep_psum_rel = DVE.n

    def phase_a(is_b):
        xsrc = xs1 if is_b else x_in
        wq_s, wk_s, wv_s = ((wscr["wqB"], wscr["wkB"], wscr["wvB"]) if is_b
                            else (wscr["wqT"], wscr["wkT"], wscr["wvT"]))
        wsems = [sm["s_w0"], sm["s_w1"], sm["s_w2"]]
        slot_uses = [w.n // 16 for w in wsems]
        slot_pe_rel = {0: PE.n, 1: PE.n, 2: PE.n}

        xsems = [sm["s_x0"], sm["s_x1"], sm["s_xa"], sm["s_xb"]]

        def xsl(i):
            return xin[:, i, :] if i < 2 else xin2[:, i - 2, :]

        x_aff_rel = {0: 0, 1: 0, 2: 0, 3: 0}
        tb_rel = {-1: prep_tb_rel}
        evac_rel = {("qk", 0): 0, ("qk", 1): 0, ("v", 0): 0, ("v", 1): 0}
        stq_sems = [sm["s_stq0"], sm["s_stq1"]]
        stk_sems = [sm["s_stk0"], sm["s_stk1"]]
        norm_read = [0]
        norm_readb = {0: 0, 1: 0}
        aff_dve = {}
        transp_pe = {}

        def nmax_update(kind, first, nb_ap=None):
            src = pb[2].ap() if nb_ap is None else nb_ap
            p.wait("vector", PE)
            p.wait("vector", DVE)
            p.op("vector", lambda e_, src=src: e_.reduce_max(
                nmax.ap()[0:1, 4:5], src[0:1, :], axis=AXX), DVE)
            norm_read[0] = DVE.n
            col = 0 if kind == "q" else 1
            p.wait("vector", DVE)
            if first:
                p.op("vector", lambda e_, c=col: e_.tensor_copy(
                    nmax.ap()[0:1, c:c + 1], nmax.ap()[0:1, 4:5]), DVE)
            else:
                p.op("vector", lambda e_, c=col: e_.tensor_max(
                    nmax.ap()[0:1, c + 2:c + 3], nmax.ap()[0:1, c:c + 1],
                    nmax.ap()[0:1, 4:5]), DVE)
                p.wait("vector", DVE)
                p.op("vector", lambda e_, c=col: e_.tensor_copy(
                    nmax.ap()[0:1, c:c + 1], nmax.ap()[0:1, c + 2:c + 3]), DVE)

        def load_half(slot, mat_s, half):
            p.wait("sync", PE, slot_pe_rel[slot])
            dma("sync", wbuf[:, slot, :, :],
                mat_s.rearrange("c p d -> p c d")[:, half * 4:(half + 1) * 4, :],
                wsems[slot])
            slot_uses[slot] += 1

        def chunk_ln_part1(tc):
            stv = stat.ap()
            for stt in range(4):
                p.wait("sync", DVE, x_aff_rel[stt])
                r0 = tc * TOKC + stt * 128
                dma("sync", xsl(stt), xsrc[r0:r0 + 128, :], xsems[stt])
            for stt in range(4):
                p.wait("vector", xsems[stt])
                p.wait("vector", PE, transp_pe.get(tc - 1, 0))
                p.wait("vector", DVE)
                p.op("vector", lambda e_, stt=stt: e_.scalar_tensor_tensor(
                    lnx_tm[:, stt % 2, :], xsl(stt), 1.0, xsl(stt),
                    ALU.mult, ALU.mult,
                    accum_out=stat.ap()[:, 4 + stt:5 + stt]), DVE)
                p.op("vector", lambda e_, stt=stt: e_.reduce_sum(
                    stat.ap()[:, stt:stt + 1], xsl(stt), axis=AXX), DVE)
            p.wait("vector", DVE)
            p.op("vector", lambda e_: e_.tensor_scalar(
                stv[:, 8:12], stv[:, 0:4], 1.0 / DM, None, ALU.mult), DVE)
            p.wait("vector", DVE)
            p.op("vector", lambda e_: e_.tensor_tensor(
                stv[:, 12:16], stv[:, 8:12], stv[:, 8:12], ALU.mult), DVE)
            p.wait("vector", DVE)
            p.op("vector", lambda e_: e_.scalar_tensor_tensor(
                stv[:, 12:16], stv[:, 4:8], 1.0 / DM, stv[:, 12:16],
                ALU.mult, ALU.subtract), DVE)
            p.wait("scalar", DVE)
            p.op("scalar", lambda e_: e_.activation(
                stv[:, 4:8], stv[:, 12:16], AF.Ln, bias=epsb.ap()[:, 0:1]), SC)
            p.wait("scalar", SC)
            p.op("scalar", lambda e_: e_.activation(
                stv[:, 12:16], stv[:, 4:8], AF.Exp, scale=-0.5), SC)
            p.wait("vector", SC)
            for stt in range(4):
                p.op("vector", lambda e_, stt=stt: e_.tensor_scalar(
                    stv[:, 4 + stt:5 + stt], stv[:, 8 + stt:9 + stt],
                    stv[:, 12 + stt:13 + stt], -1.0, ALU.mult, ALU.mult), DVE)
            for stt in range(2):
                p.wait("vector", DVE)
                p.op("vector", lambda e_, stt=stt: e_.tensor_scalar(
                    lnx_tm[:, stt, :], xsl(stt), stv[:, 12 + stt:13 + stt],
                    stv[:, 4 + stt:5 + stt], ALU.mult, ALU.add), DVE)
            aff_dve[(tc, 0)] = DVE.n
            x_aff_rel[0] = DVE.n
            x_aff_rel[1] = DVE.n

        def chunk_top(tc):
            stv = stat.ap()
            p.wait("tensor", DVE, aff_dve[(tc, 0)])
            p.wait("tensor", DVE, tb_rel.get(tc - 1, 0))
            for stt in range(2):
                for dc in range(8):
                    p.op("tensor", lambda e_, stt=stt, dc=dc: e_.transpose(
                        tbv[stt][:, dc, :], lnx_tm[:, stt, dc * 128:(dc + 1) * 128],
                        ident[:, :]), PE if dc == 7 else None)
                p.wait("vector", PE)
                p.op("vector", lambda e_, stt=stt: e_.tensor_copy(
                    lnxT[:, :, stt * 128:(stt + 1) * 128], tbv[stt][:, :, :]), DVE)
            tp01 = PE.n
            for stt in range(2, 4):
                p.wait("vector", PE, tp01)
                p.op("vector", lambda e_, stt=stt: e_.tensor_scalar(
                    lnx_tm[:, stt % 2, :], xsl(stt), stv[:, 12 + stt:13 + stt],
                    stv[:, 4 + stt:5 + stt], ALU.mult, ALU.add), DVE)
                x_aff_rel[stt] = DVE.n
                p.wait("tensor", DVE)
                for dc in range(8):
                    p.op("tensor", lambda e_, stt=stt, dc=dc: e_.transpose(
                        tbv[stt % 2][:, dc, :], lnx_tm[:, stt % 2, dc * 128:(dc + 1) * 128],
                        ident[:, :]), PE if dc == 7 else None)
                p.wait("vector", PE)
                p.op("vector", lambda e_, stt=stt: e_.tensor_copy(
                    lnxT[:, :, stt * 128:(stt + 1) * 128], tbv[stt % 2][:, :, :]), DVE)
            transp_pe[tc] = PE.n
            tb_rel[tc] = DVE.n
            return DVE.n

        load_half(0, wk_s, 0)
        load_half(1, wk_s, 1)
        pass_idx = 0
        p.op("vector", lambda e_: e_.memset(nmax.ap()[0:1, 0:4], 0.0), DVE)
        chunk_ln_part1(0)

        for tc in range(NTC):
            mats = ((wk_s, "k"), (wq_s, "q"), (wv_s, "v"))
            lnxT_ready = chunk_top(tc)

            for mi, (mat_s, kind) in enumerate(mats):
                sa, sb = (2 * pass_idx) % 3, (2 * pass_idx + 1) % 3
                if not (mi == 2 and tc == NTC - 1):
                    nmat = mats[mi + 1][0] if mi < 2 else wk_s
                    load_half((2 * pass_idx + 2) % 3, nmat, 0)
                if kind in ("k", "q"):
                    staged = (kind == "q") or is_b
                    stsems = stq_sems if kind == "q" else stk_sems
                    sq_sc = {}

                    def emit_norm(e2):
                        p.wait("tensor", SC, sq_sc[e2])
                        if is_b:
                            nb = pb[2 + e2 % 2]
                            p.wait("tensor", DVE, norm_readb[e2 % 2])
                            p.op("tensor", lambda e_, nb=nb, e2=e2: e_.matmul(
                                nb.ap()[0:1, :], ones_colb.ap()[:, :],
                                nsq[:, e2 % 2, :],
                                start=True, stop=True), PE)
                            norm_mm_pe[e2 % 2] = PE.n
                            nmax_update(kind, tc == 0 and e2 == 0, nb.ap())
                            norm_readb[e2 % 2] = DVE.n
                        else:
                            p.op("tensor", lambda e_, e2=e2: e_.matmul(
                                pb[2].ap()[0:1, :], ones_colb.ap()[:, :],
                                nsq[:, e2 % 2, :],
                                start=(e2 == 0), stop=(e2 == 7)), PE)
                            norm_mm_pe[e2 % 2] = PE.n
                            if e2 == 7:
                                nmax_update(kind, tc == 0)

                    for ec in range(8):
                        bank = pb[ec % 2]
                        p.wait("tensor", SC, evac_rel.get(("qk", ec % 2), 0))
                        if ec == 0:
                            p.wait("tensor", wsems[sa], 16 * slot_uses[sa])
                            p.wait("tensor", DVE, lnxT_ready)
                            p.wait("tensor", DVE, norm_read[0])
                        for dc in range(8):
                            wslot = sa if dc < 4 else sb
                            if ec == 0 and dc == 4:
                                p.wait("tensor", wsems[sb], 16 * slot_uses[sb])
                            p.op("tensor", lambda e_, b=bank, ws=wslot, dc=dc, ec=ec: e_.matmul(
                                b.ap()[:, :], wbuf[:, ws, dc % 4, ec * 128:(ec + 1) * 128],
                                lnxT[:, dc, :], start=(dc == 0), stop=(dc == 7)),
                                PE if dc == 7 else None)
                        if ec >= 2:
                            emit_norm(ec - 2)
                        p.wait("scalar", PE)
                        p.op("scalar", lambda e_, b=bank, ec=ec: e_.activation(
                            nsq[:, ec % 2, :], b.ap()[:, :], AF.Square), SC)
                        sq_sc[ec] = SC.n
                        if staged:
                            p.wait("scalar", stq_sems[ec % 2])
                            p.wait("scalar", stk_sems[ec % 2])
                            dest = qstage[:, 0, :] if ec % 2 == 0 else qstage2
                        else:
                            dest = k_sb.ap()[:, ec, tc * TOKC:(tc + 1) * TOKC]
                        p.op("scalar", lambda e_, b=bank, d=dest: e_.activation(
                            d, b.ap()[:, :], AF.Copy), SC)
                        evac_rel[("qk", ec % 2)] = SC.n
                        if staged:
                            dscr = q_scr if kind == "q" else k_scr
                            p.wait("gpsimd", SC)
                            dma("gpsimd",
                                dscr[ec, :, tc * TOKC:(tc + 1) * TOKC],
                                dest, stsems[ec % 2])
                    emit_norm(6)
                    emit_norm(7)
                    if kind == "k" and tc + 1 < NTC:
                        chunk_ln_part1(tc + 1)
                else:
                    for kt in range(4):
                        for oc in range(2):
                            bank = pb[4 + (2 * kt + oc) % 2]
                            p.wait("tensor", SC, evac_rel.get(("v", (2 * kt + oc) % 2), 0))
                            if kt == 0 and oc == 0:
                                p.wait("tensor", wsems[sa], 16 * slot_uses[sa])
                                p.wait("tensor", wsems[sb], 16 * slot_uses[sb])
                                p.wait("tensor", DVE, lnxT_ready)
                            for dc in range(8):
                                wslot = sa if dc < 4 else sb
                                p.op("tensor", lambda e_, b=bank, ws=wslot, dc=dc, kt=kt, oc=oc: e_.matmul(
                                    b.ap()[:, :], lnxT[:, dc, kt * 128:(kt + 1) * 128],
                                    wbuf[:, ws, dc % 4, oc * 512:(oc + 1) * 512],
                                    start=(dc == 0), stop=(dc == 7)),
                                    PE if dc == 7 else None)
                            p.wait("scalar", PE)
                            ktg = tc * 4 + kt
                            if is_b:
                                p.op("scalar", lambda e_, b=bank, ktg=ktg, oc=oc: e_.activation(
                                    v_sbB[:, ktg, oc * 8:(oc + 1) * 8, 0:64],
                                    b.ap()[:, :].rearrange("p (h x) -> p h x", h=8),
                                    AF.Copy), SC)
                            else:
                                p.op("scalar", lambda e_, b=bank, ktg=ktg, oc=oc: e_.activation(
                                    v_sb.ap()[:, ktg, oc * 512:(oc + 1) * 512],
                                    b.ap()[:, :], AF.Copy), SC)
                            evac_rel[("v", (2 * kt + oc) % 2)] = SC.n
                slot_pe_rel[sa] = PE.n
                slot_pe_rel[sb] = PE.n
                if not (mi == 2 and tc == NTC - 1):
                    nmat = mats[mi + 1][0] if mi < 2 else wk_s
                    load_half((2 * pass_idx + 3) % 3, nmat, 1)
                pass_idx += 1

        if is_b:
            p.wait("vector", PE)
            p.op("vector", lambda e_: e_.memset(v_sbB[:, :, :, 64:65], 1.0), DVE)

        p.wait("vector", DVE)
        p.op("vector", lambda e_: e_.tensor_tensor(
            nmax.ap()[0:1, 5:6], nmax.ap()[0:1, 0:1], nmax.ap()[0:1, 1:2],
            ALU.mult), DVE)
        p.wait("scalar", DVE)
        p.op("scalar", lambda e_: e_.activation(
            nmax.ap()[0:1, 6:7], nmax.ap()[0:1, 5:6], AF.Ln,
            bias=epsb.ap()[0:1, 2:3]), SC)
        p.wait("scalar", SC)
        p.op("scalar", lambda e_: e_.activation(
            nmax.ap()[0:1, 7:8], nmax.ap()[0:1, 6:7], AF.Exp, scale=0.5), SC)
        p.wait("tensor", SC)
        p.op("tensor", lambda e_: e_.matmul(
            pb[0].ap()[:, 0:1], ones_rowf.ap()[0:1, :], nmax.ap()[0:1, 7:8],
            start=True, stop=True), PE)
        p.wait("vector", PE)
        p.op("vector", lambda e_, c=(1 if is_b else 0): e_.tensor_scalar(
            negC.ap()[:, c:c + 1], pb[0].ap()[:, 0:1], -1.0, None, ALU.mult), DVE)
        p.wait("tensor", DVE)

    norm_pe_rel = [0]
    norm_mm_pe = {}

    def emit_ln(x_ap, lnx_ap, eps, sq_ap):
        su, ssq, mean, msq, var, rstd, nmb, lnv = (stat.ap()[:, i:i + 1] for i in range(8))
        p.wait("scalar", DVE)
        p.wait("vector", DVE)
        p.op("vector", lambda e_: e_.reduce_sum(su, x_ap, axis=AXX), DVE)
        p.op("scalar", lambda e_: e_.activation(sq_ap, x_ap, AF.Square, accum_out=ssq), SC)
        p.wait("vector", DVE)
        p.op("vector", lambda e_: e_.tensor_scalar(mean, su, 1.0 / DM, None, ALU.mult), DVE)
        p.wait("vector", SC)
        p.wait("vector", DVE)
        p.op("vector", lambda e_: e_.tensor_tensor(msq, mean, mean, ALU.mult), DVE)
        p.wait("vector", DVE)
        p.op("vector", lambda e_: e_.scalar_tensor_tensor(
            var, ssq, 1.0 / DM, msq, ALU.mult, ALU.subtract), DVE)
        p.wait("scalar", DVE)
        eps_ap = epsb.ap()[:, 0:1] if eps == LN_EPS else epsb.ap()[:, 1:2]
        p.op("scalar", lambda e_, b=eps_ap: e_.activation(lnv, var, AF.Ln, bias=b), SC)
        p.wait("scalar", SC)
        p.op("scalar", lambda e_: e_.activation(rstd, lnv, AF.Exp, scale=-0.5), SC)
        p.wait("vector", SC)
        p.op("vector", lambda e_: e_.tensor_scalar(nmb, mean, rstd, -1.0, ALU.mult, ALU.mult), DVE)
        p.wait("vector", DVE)
        p.op("vector", lambda e_: e_.tensor_scalar(
            lnx_ap, x_ap, rstd, nmb, ALU.mult, ALU.add), DVE)

    _st_sc = {}

    def ln2_stats(x_aps, eps, junk_ap, st=None):
        stv = stat.ap() if st is None else st
        n = len(x_aps)
        p.wait("vector", SC, _st_sc.get(0 if st is None else 1, 0))
        for i, x in enumerate(x_aps):
            p.wait("vector", DVE)
            p.op("vector", lambda e_, x=x, i=i, stv=stv: e_.scalar_tensor_tensor(
                junk_ap, x, 1.0, x, ALU.mult, ALU.mult,
                accum_out=stv[:, 2 + i:3 + i]), DVE)
            p.op("vector", lambda e_, x=x, i=i, stv=stv: e_.reduce_sum(
                stv[:, i:i + 1], x, axis=AXX), DVE)
        p.wait("vector", DVE)
        p.op("vector", lambda e_, n=n, stv=stv: e_.tensor_scalar(
            stv[:, 4:4 + n], stv[:, 0:n], 1.0 / DM, None, ALU.mult), DVE)
        p.wait("vector", DVE)
        p.op("vector", lambda e_, n=n, stv=stv: e_.tensor_tensor(
            stv[:, 6:6 + n], stv[:, 4:4 + n], stv[:, 4:4 + n], ALU.mult), DVE)
        p.wait("vector", DVE)
        p.op("vector", lambda e_, n=n, stv=stv: e_.scalar_tensor_tensor(
            stv[:, 6:6 + n], stv[:, 2:2 + n], 1.0 / DM, stv[:, 6:6 + n],
            ALU.mult, ALU.subtract), DVE)
        p.wait("scalar", DVE)
        eps_ap = epsb.ap()[:, 0:1] if eps == LN_EPS else epsb.ap()[:, 1:2]
        p.op("scalar", lambda e_, n=n, b=eps_ap, stv=stv: e_.activation(
            stv[:, 2:2 + n], stv[:, 6:6 + n], AF.Ln, bias=b), SC)
        p.wait("scalar", SC)
        p.op("scalar", lambda e_, n=n, stv=stv: e_.activation(
            stv[:, 6:6 + n], stv[:, 2:2 + n], AF.Exp, scale=-0.5), SC)
        _st_sc[0 if st is None else 1] = SC.n
        p.wait("vector", SC)
        for i in range(n):
            p.op("vector", lambda e_, i=i, stv=stv: e_.tensor_scalar(
                stv[:, 10 + i:11 + i], stv[:, 4 + i:5 + i],
                stv[:, 6 + i:7 + i], -1.0, ALU.mult, ALU.mult), DVE)

    def ln2_affine(i, x_ap, lnx_ap, st=None):
        stv = stat.ap() if st is None else st
        p.wait("vector", DVE)
        p.op("vector", lambda e_, i=i, x=x_ap, o=lnx_ap, stv=stv: e_.tensor_scalar(
            o, x, stv[:, 6 + i:7 + i], stv[:, 10 + i:11 + i],
            ALU.mult, ALU.add), DVE)

    ffn_state = {"lnf_pe": 0, "tb": 0, "h_rel": 0, "out_evac": {}, "gaug_pe": 0}

    def ffn_part1(x2_list, lnf, lnT):
        p.wait("vector", PE, ffn_state["lnf_pe"])
        ln2_stats(x2_list, LN_EPS, lnf)
        for stt in range(2):
            if stt:
                p.wait("vector", PE)
            ln2_affine(stt, x2_list[stt], lnf)
            tbk = ffn_state["tb"]
            ffn_state["tb"] ^= 1
            p.wait("tensor", DVE)
            p.wait("tensor", SC, ffn_state.get("tb_sc", 0))
            for dc in range(8):
                p.op("tensor", lambda e_, dc=dc, t=tbk: e_.transpose(
                    tbv[t][:, dc, :], lnf[:, dc * 128:(dc + 1) * 128], ident[:, :]),
                    PE if dc == 7 else None)
            ffn_state["lnf_pe"] = PE.n
            p.wait("vector", PE)
            p.op("vector", lambda e_, t=tbk, stt=stt: e_.tensor_copy(
                lnT[:, :, stt * 128:(stt + 1) * 128], tbv[t][:, :, :]), DVE)
            ffn_state[("tbrel", tbk)] = DVE.n
        lnT_ready = DVE.n
        p.wait("tensor", DVE, lnT_ready)
        p.wait("tensor", SC)
        for dc in range(8):
            p.op("tensor", lambda e_, dc=dc: e_.matmul(
                pb[4].ap()[:, 0:256], win_sb.ap()[:, dc, :], lnT[:, dc, :],
                start=(dc == 0), stop=(dc == 7)), PE if dc == 7 else None)
        p.wait("vector", PE)
        p.op("vector", lambda e_: e_.tensor_scalar(
            s2w[:, :], pb[4].ap()[:, 0:256], bin_sb.ap()[:, :], None, ALU.add), DVE)
        ffn_state["sevac"] = DVE.n
        ffn_middle()

    def ffn_middle():
        if ffn_state.get("sc_relu"):
            h2 = ffn_state["h2"]
            p.wait("scalar", DVE)
            for j in range(8):
                hb = h_w if j % 2 == 0 else h2
                p.wait("scalar", DVE, ffn_state.get(("hfree", j % 2), 0))
                p.op("scalar", lambda e_, j=j, hb=hb: e_.activation(
                    hb[:, :], s2w[:, :], AF.Relu,
                    bias=b1p.ap()[:, j:j + 1], scale=w1p.ap()[:, j:j + 1]), SC)
                gdst = g2a if j % 2 == 0 else g2b
                p.wait("vector", SC)
                p.wait("vector", DVE)
                if j < 2:
                    p.op("vector", lambda e_, hb=hb, g=gdst, j=j: e_.tensor_scalar(
                        g[:, :], hb[:, :], w2p.ap()[:, j:j + 1], None, ALU.mult), DVE)
                else:
                    p.op("vector", lambda e_, hb=hb, g=gdst, j=j: e_.scalar_tensor_tensor(
                        g[:, :], hb[:, :], w2p.ap()[:, j:j + 1], g[:, :],
                        ALU.mult, ALU.add), DVE)
                ffn_state[("hfree", j % 2)] = DVE.n
        else:
            for j in range(8):
                p.wait("vector", DVE)
                p.op("vector", lambda e_, j=j: e_.tensor_scalar(
                    h_w[:, :], s2w[:, :], w1p.ap()[:, j:j + 1],
                    b1p.ap()[:, j:j + 1], ALU.mult, ALU.add), DVE)
                p.wait("vector", DVE)
                p.op("vector", lambda e_: e_.tensor_scalar(
                    h_w[:, :], h_w[:, :], 0.0, None, ALU.max), DVE)
                gdst = g2a if j % 2 == 0 else g2b
                p.wait("vector", DVE)
                if j < 2:
                    p.op("vector", lambda e_, g=gdst, j=j: e_.tensor_scalar(
                        g[:, :], h_w[:, :], w2p.ap()[:, j:j + 1], None,
                        ALU.mult), DVE)
                else:
                    p.op("vector", lambda e_, g=gdst, j=j: e_.scalar_tensor_tensor(
                        g[:, :], h_w[:, :], w2p.ap()[:, j:j + 1], g[:, :],
                        ALU.mult, ALU.add), DVE)

    def ffn_part1a(x2_list):
        lnfA, lnfB = ffn_state["lnfA"], ffn_state["lnfB"]
        p.wait("vector", PE, ffn_state["lnf_pe"])
        ln2_stats(x2_list, LN_EPS, lnfA[:, :])
        ln2_affine(0, x2_list[0], lnfA[:, :])
        ffn_state[("aff", 0)] = DVE.n
        ln2_affine(1, x2_list[1], lnfB[:, :])
        ffn_state[("aff", 1)] = DVE.n

    def ffn_part1b():
        lnfA, lnfB, lnT = ffn_state["lnfA"], ffn_state["lnfB"], ffn_state["lnT"]
        for stt, lsrc in ((0, lnfA), (1, lnfB)):
            tbk = ffn_state["tb"]
            ffn_state["tb"] ^= 1
            p.wait("tensor", DVE, ffn_state[("aff", stt)])
            p.wait("tensor", DVE, ffn_state.get(("tbrel", tbk), 0))
            p.wait("tensor", SC, ffn_state.get("tb_sc", 0))
            for dc in range(8):
                p.op("tensor", lambda e_, dc=dc, t=tbk, lsrc=lsrc: e_.transpose(
                    tbv[t][:, dc, :], lsrc[:, dc * 128:(dc + 1) * 128],
                    ident[:, :]), PE if dc == 7 else None)
            p.wait("vector", PE)
            p.op("vector", lambda e_, t=tbk, stt=stt, lnT=lnT: e_.tensor_copy(
                lnT[:, :, stt * 128:(stt + 1) * 128], tbv[t][:, :, :]), DVE)
            ffn_state[("tbrel", tbk)] = DVE.n
        ffn_state["lnf_pe"] = PE.n
        lnT_ready = DVE.n
        p.wait("tensor", DVE, lnT_ready)
        p.wait("tensor", SC)
        for dc in range(8):
            p.op("tensor", lambda e_, dc=dc, lnT=lnT: e_.matmul(
                pb[4].ap()[:, 0:256], win_sb.ap()[:, dc, :], lnT[:, dc, :],
                start=(dc == 0), stop=(dc == 7)), PE if dc == 7 else None)
        p.wait("vector", PE)
        p.op("vector", lambda e_: e_.tensor_scalar(
            s2w[:, :], pb[4].ap()[:, 0:256], bin_sb.ap()[:, :], None, ALU.add), DVE)
        ffn_state["sevac"] = DVE.n
        ffn_middle()

    def ffn_part2(xres_list, row0, dst_dram, st_sem, dst_list=None):
        if dst_list is None:
            dst_list = xres_list
        p.wait("tensor", DVE)
        p.op("tensor", lambda e_: e_.matmul(
            pb[5].ap()[0:64, 0:256], stack2.ap()[:, :], g2a[:, :],
            start=True, stop=False))
        p.op("tensor", lambda e_: e_.matmul(
            pb[5].ap()[0:64, 0:256], stack2.ap()[:, :], g2b[:, :],
            start=False, stop=True), PE)
        p.wait("vector", PE)
        p.wait("vector", PE, ffn_state["gaug_pe"])
        p.op("vector", lambda e_: e_.tensor_scalar(
            g_aug.ap()[0:64, :], pb[5].ap()[0:64, 0:256], b2_sb.ap()[:, :],
            None, ALU.add), DVE)
        ffn_state["h_rel"] = DVE.n
        g_ready = DVE.n
        p.wait("tensor", DVE, g_ready)
        for stt in range(2):
            for oc in range(2):
                bank = pb[4 + (2 * stt + oc) % 2]
                p.wait("tensor", DVE, ffn_state["out_evac"].get((2 * stt + oc) % 2, 0))
                p.op("tensor", lambda e_, b=bank, stt=stt, oc=oc: e_.matmul(
                    b.ap()[:, :], g_aug.ap()[:, stt * 128:(stt + 1) * 128],
                    wout_aug.ap()[:, oc * 512:(oc + 1) * 512],
                    start=True, stop=True), PE)
                p.wait("vector", PE)
                p.op("vector", lambda e_, b=bank, stt=stt, oc=oc: e_.tensor_tensor(
                    dst_list[stt][:, oc * 512:(oc + 1) * 512], b.ap()[:, :],
                    xres_list[stt][:, oc * 512:(oc + 1) * 512], ALU.add), DVE)
                ffn_state["out_evac"][(2 * stt + oc) % 2] = DVE.n
        ffn_state["gaug_pe"] = PE.n
        p.wait("gpsimd", DVE)
        for stt in range(2):
            dma("gpsimd", dst_dram[row0 + stt * 128:row0 + (stt + 1) * 128, :],
                dst_list[stt], st_sem)


    def ffn_block(x2_list, xres_list, row0, dst_dram, st_sem, lnf, lnT):
        ffn_part1(x2_list, lnf, lnT)
        ffn_part2(xres_list, row0, dst_dram, st_sem)

    def phase_1b():
        ffn_state["sc_relu"] = True
        ffn_state["h2"] = YT[:, 0:256]
        ffn_state["smm_sc_wait"] = False
        ffn_state["lnfA"] = lnf_tm
        ffn_state["lnfB"] = lnf2b
        ffn_state["lnT"] = lnfT
        exp_done = {}
        ctx_evac = {0: 0}
        q_pe_rel = {}
        pair_dve = {}
        xpairs = [[XW[0], XW[1]], [XW[2], XW[3]]]
        xsems1b = [(sm["s_xa"], sm["s_xb"]), (sm["s_xc"], sm["s_xd"])]
        tb0f = tb[0].ap().bitcast(F32)
        tb1f = tb[1].ap().bitcast(F32)

        def attn_loads(qs):
            q0 = qs * QS1
            par = qs % 2
            qsem = sm["s_q0"] if par == 0 else sm["s_q1"]
            p.wait("sync", PE, q_pe_rel.get(par, 0))
            dma("sync", qb1[:, :, par * QS1:(par + 1) * QS1],
                q_scr.rearrange("c p t -> p c t")[:, :, q0:q0 + QS1], qsem)

        def attn_part(qs, dve_pre=None, post_kc0=None):
            q0 = qs * QS1
            kcnt = 2 * qs + 2
            par = qs % 2
            P = xpairs[par]
            xsems = xsems1b[par]
            qsem = sm["s_q0"] if par == 0 else sm["s_q1"]
            for stt in range(2):
                p.wait("sync", sm["s_stx"], 16 * 2 * (qs - 1) if qs else 0)
                p.wait("sync", DVE, pair_dve.get(par, 0))
                dma("sync", P[stt][:, :], x_in[q0 + stt * 128:q0 + (stt + 1) * 128, :],
                    xsems[stt])
            p.wait("tensor", qsem)
            if dve_pre is None:
                p.wait("tensor", DVE)
            exp_at = {}
            mask_at = {}
            AVB = [[pb[3].ap()[:, :], pb[4].ap()[:, :]],
                   [pb[5].ap()[:, :], tb1f[:, :]]]
            SB = [pb[2].ap()[:, 0:1], tb0f[:, 0:1]]

            def emit_avs(kc):
                for tch in range(2):
                    p.wait("tensor", SC, exp_at[kc])
                    if mask_at[kc]:
                        p.wait("tensor", PL, mask_at[kc])
                    if kc == 0:
                        p.wait("tensor", DVE, ctx_evac[0])
                    att = attT1[:, kc, tch * 128:(tch + 1) * 128]
                    p.op("tensor", lambda e_, att=att, tch=tch, kc=kc, kcnt=kcnt: e_.matmul(
                        SB[tch], att, ones_colb.ap()[:, :],
                        start=(kc == 0), stop=(kc == kcnt - 1)))
                    for dh in range(2):
                        p.op("tensor", lambda e_, att=att, tch=tch, dh=dh, kc=kc, kcnt=kcnt: e_.matmul(
                            AVB[tch][dh], att,
                            v_sb.ap()[:, kc, dh * 512:(dh + 1) * 512],
                            start=(kc == 0), stop=(kc == kcnt - 1)),
                            PE if (kc == kcnt - 1 and tch == 1 and dh == 1) else None)

            for kc in range(kcnt):
                bank = pb[kc % 2]
                p.wait("tensor", SC, exp_done.get(kc % 2, 0))
                for ec in range(8):
                    p.op("tensor", lambda e_, b=bank, ec=ec, kc=kc, par=par: e_.matmul(
                        b.ap()[:, 0:QS1], k_sb.ap()[:, ec, kc * 128:(kc + 1) * 128],
                        qb1[:, ec, par * QS1:(par + 1) * QS1],
                        start=(ec == 0), stop=(ec == 7)), PE if ec == 7 else None)
                p.wait("scalar", PE)
                p.op("scalar", lambda e_, b=bank, kc=kc: e_.activation(
                    attT1[:, kc, :], b.ap()[:, 0:QS1], AF.Exp,
                    bias=negC.ap()[:, 0:1]), SC)
                exp_done[kc % 2] = SC.n
                exp_at[kc] = SC.n
                if kc >= kcnt - 2:
                    p.wait("gpsimd", SC)
                    p.op("gpsimd", lambda e_, kc=kc, q0=q0: e_.affine_select(
                        out=attT1[:, kc, :], in_=attT1[:, kc, :],
                        compare_op=ALU.is_ge, fill=0.0,
                        base=q0 - kc * 128, pattern=[[1, QS1]],
                        channel_multiplier=-1), PL)
                    mask_at[kc] = PL.n
                else:
                    mask_at[kc] = 0
                if kc == 2 and post_kc0 is not None:
                    post_kc0()
                if kc >= 3:
                    emit_avs(kc - 3)
            for k_ in range(max(0, kcnt - 3), kcnt):
                emit_avs(k_)
            p.wait("scalar", PE)
            for tch in range(2):
                p.op("scalar", lambda e_, tch=tch: e_.activation(
                    stat.ap()[:, 12 + tch:13 + tch], SB[tch], AF.Ln,
                    bias=epsb.ap()[:, 2:3]), SC)
            p.wait("scalar", SC)
            for tch in range(2):
                p.op("scalar", lambda e_, tch=tch: e_.activation(
                    stat.ap()[:, 14 + tch:15 + tch],
                    stat.ap()[:, 12 + tch:13 + tch], AF.Exp, scale=-1.0), SC)
            rc = SC.n
            for tch in range(2):
                p.wait("vector", xsems[tch])
                p.wait("vector", SC, rc)
                p.wait("vector", PE)
                p.wait("vector", DVE)
                for dh in range(2):
                    p.op("vector", lambda e_, b=AVB[tch][dh], tch=tch, dh=dh, P=P: e_.scalar_tensor_tensor(
                        P[tch][:, dh * 512:(dh + 1) * 512], b,
                        stat.ap()[:, 14 + tch:15 + tch],
                        P[tch][:, dh * 512:(dh + 1) * 512],
                        ALU.mult, ALU.add), DVE)
            ctx_evac[0] = DVE.n
            ffn_state["tb_sc"] = rc
            pair_dve[par] = DVE.n
            q_pe_rel[par] = PE.n
            return DVE.n

        attn_loads(0)
        attn_part(0)
        P0 = xpairs[0]
        ffn_part1a([P0[0][:, :], P0[1][:, :]])
        attn_loads(1)
        for qs in range(1, NQ1):
            dve_pre = DVE.n

            def hook():
                ffn_part1b()
                ctx_evac[0] = ffn_state["sevac"]

            attn_part(qs, dve_pre, post_kc0=hook)
            Pp = xpairs[(qs - 1) % 2]
            ffn_part2([Pp[0][:, :], Pp[1][:, :]], (qs - 1) * QS1, xs1,
                      sm["s_stx"])
            ctx_evac[0] = DVE.n
            pair_dve[(qs - 1) % 2] = DVE.n
            Pc = xpairs[qs % 2]
            ffn_part1a([Pc[0][:, :], Pc[1][:, :]])
            if qs + 1 < NQ1:
                attn_loads(qs + 1)
        ffn_part1b()
        Pl = xpairs[(NQ1 - 1) % 2]
        ffn_part2([Pl[0][:, :], Pl[1][:, :]], (NQ1 - 1) * QS1, xs1,
                  sm["s_stx"])

    xw_dve_rel = {}

    def phase_2b():
        exp_done = {}
        q_pe_rel = [0]
        ksems3 = [sm["s_k0"], sm["s_k1"], sm["s_k2"]]
        kuse3 = [0, 0, 0]
        kpe_rel3 = {0: 0, 1: 0, 2: 0}
        kg = [0]
        kslot_of = {}
        ktgt = {}
        ctxn_dve = [0]
        opj23_free = [0]
        pair_rel = {0: 0, 1: 0}
        pendf = []
        wsb = k_sb.ap()[:, :, 0:1024]
        AP2 = [attb2[:, 0:2, :], attb2[:, 2:4, :],
               k_sb.ap()[:, 5, 2048:3072].rearrange("p (a b) -> p a b", a=2)]
        Y2 = [k_sb.ap()[:, j, 1024:3072].bitcast(F32) for j in range(4)]
        Jnk = k_sb.ap()[:, 4, 1024:3072].bitcast(F32)
        stat2 = k_sb.ap()[:, 5, 1024:1056].bitcast(F32)
        kstr3 = k_sb.ap()[:, 6, 1024:3072]
        kslots = [kstr2[:, 0, :], kstr2[:, 1, :], kstr3]
        NORM1 = k_sb.ap()[:, 7, 2048:4096].bitcast(F32)
        sts = [None, stat2]
        ffn_state["sc_relu"] = False
        ffn_state["smm_sc_wait"] = True
        ffn_state["h2"] = k_sb.ap()[:, 5, 1536:2048].bitcast(F32)
        ffn_state["lnfA"] = lnf_tm2
        ffn_state["lnfB"] = lnf2
        ffn_state["lnT"] = lnfT2
        pairs = [[XW[2], XW[3]], [XW[0], XW[1]]]
        dma("sync", wsb[:, 0:4, :],
            wscr["woB"].rearrange("c p d -> p c d")[:, 0:4, :], sm["s_wb0"])
        dma("gpsimd", wsb[:, 4:8, :],
            wscr["woB"].rearrange("c p d -> p c d")[:, 4:8, :], sm["s_wb1"])
        wsb_wait = [True]
        for qc in range(NQ2):
            q0 = qc * QC2
            kcnt = 4 * qc + 4
            npiece = (kcnt * 128 + 2047) // 2048
            p.wait("sync", PE, q_pe_rel[0])
            dma("sync", qb2[:, :, :],
                q_scr.rearrange("c p t -> p c t")[:, :, q0:q0 + QC2], sm["s_q0"])
            p.wait("tensor", sm["s_q0"])
            p.wait("tensor", DVE, ctxn_dve[0])

            xs_sems = [(sm["s_xa"], sm["s_xb"]), (sm["s_xc"], sm["s_xd"])]

            def ep_loads(sc2):
                rows0 = q0 + sc2 * 256
                for stt in range(2):
                    p.wait("sync", sm["s_stout"], 16 * (4 * qc - 2 + 2 * sc2))
                    p.wait("sync", DVE, pair_rel[sc2])
                    dma("sync", pairs[sc2][stt][:, :],
                        xs1[rows0 + stt * 128:rows0 + (stt + 1) * 128, :],
                        xs_sems[sc2][stt])

            def ep_stats(sc2):
                for stt in range(2):
                    p.wait("vector", xs_sems[sc2][stt])
                ln2_stats([pairs[sc2][0][:, :], pairs[sc2][1][:, :]], LN_EPS,
                          Jnk, st=sts[sc2])

            def issue_k_loads(pr):
                for pc in range(npiece):
                    sl = kg[0] % 3
                    kg[0] += 1
                    kslot_of[(qc, pr, pc)] = sl
                    p.wait("sync", PE, kpe_rel3[sl])
                    n_tok = min(2048, kcnt * 128 - pc * 2048)
                    dma("sync", kslots[sl][:, 0:n_tok],
                        k_scr[pr, :, pc * 2048:pc * 2048 + n_tok], ksems3[sl])
                    kuse3[sl] += 1
                    ktgt[(qc, pr, pc)] = 16 * kuse3[sl]

            for pr in range(8):
                if pr == 0:
                    issue_k_loads(0)
                def emit_av(kc, pos):
                    for hh in range(2):
                        p.wait("tensor", SC, exp_at[kc])
                        if mask_at[kc]:
                            p.wait("tensor", PL, mask_at[kc])
                        if pos == 0:
                            p.wait("tensor", DVE, ctxn_dve[0])
                        h = 2 * pr + hh
                        att = AP2[pos % 3][:, hh, :]
                        p.op("tensor", lambda e_, hh=hh, kc=kc, h=h, att=att, pos=pos, kcnt=kcnt: e_.matmul(
                            pb[4 + hh].ap()[0:65, :], v_sbB[:, kc, h, :],
                            att, start=(pos == 0), stop=(pos == kcnt - 1)),
                            PE if hh == 1 else None)

                exp_at = {}
                mask_at = {}
                pend = []
                order = list(range(kcnt))
                seen_pc = set()
                for pos, kc in enumerate(order):
                    pc = (kc * 128) // 2048
                    koff = kc * 128 - pc * 2048
                    half = pos % 2
                    ksl = kslot_of[(qc, pr, pc)]
                    if pc not in seen_pc:
                        seen_pc.add(pc)
                        p.wait("tensor", ksems3[ksl], ktgt[(qc, pr, pc)])
                    p.wait("tensor", SC, exp_done.get(half, 0))
                    if pos == 1:
                        p.wait("tensor", DVE, max(ctxn_dve[0], opj23_free[0]))
                    for hh in range(2):
                        bank = pb[2 * half + hh]
                        p.op("tensor", lambda e_, b=bank, hh=hh, ks=kslots[ksl], koff=koff, pr=pr: e_.matmul(
                            b.ap()[:, :], ks[hh * 64:(hh + 1) * 64, koff:koff + 128],
                            qb2[hh * 64:(hh + 1) * 64, pr, :],
                            start=True, stop=True, tile_position=(hh * 64, 0)), PE)
                    p.wait("scalar", PE)
                    if pos % 3 == 0:
                        p.wait("scalar", sm["s_cst"])
                    p.op("scalar", lambda e_, ap=AP2[pos % 3], half=half: e_.activation(
                        ap[:, :, :],
                        psQ.ap()[:, half * 1024:(half + 1) * 1024], AF.Exp,
                        bias=negC.ap()[:, 1:2]), SC)
                    exp_done[half] = SC.n
                    exp_at[kc] = SC.n
                    if kc >= kcnt - 4:
                        p.wait("gpsimd", SC)
                        for hh in range(2):
                            p.op("gpsimd", lambda e_, a_=AP2[pos % 3][:, hh, :], kc=kc, q0=q0: e_.affine_select(
                                out=a_, in_=a_,
                                compare_op=ALU.is_ge, fill=0.0,
                                base=q0 - kc * 128, pattern=[[1, QC2]],
                                channel_multiplier=-1), PL)
                        mask_at[kc] = PL.n
                    else:
                        mask_at[kc] = 0
                    if len(pend) == 2:
                        emit_av(*pend.pop(0))
                    pend.append((kc, pos))
                for pk, pp in pend:
                    emit_av(pk, pp)
                for pc in range(npiece):
                    kpe_rel3[kslot_of[(qc, pr, pc)]] = PE.n
                if pr < 7:
                    issue_k_loads(pr + 1)
                p.wait("scalar", PE)
                p.wait("scalar", DVE, opj23_free[0])
                p.op("scalar", lambda e_: e_.activation(
                    YT[64:65, 0:512], pb[4].ap()[64:65, :], AF.Ln,
                    bias=epsb.ap()[64:65, 2:3]), SC)
                p.op("scalar", lambda e_: e_.activation(
                    NORM1[64:65, 0:512], pb[5].ap()[64:65, :], AF.Ln,
                    bias=epsb.ap()[64:65, 2:3]), SC)
                p.wait("scalar", SC)
                p.op("scalar", lambda e_: e_.activation(
                    YT[64:65, 512:1024], YT[64:65, 0:512], AF.Exp,
                    scale=-1.0), SC)
                e0 = SC.n
                p.op("scalar", lambda e_: e_.activation(
                    NORM1[64:65, 512:1024], NORM1[64:65, 0:512], AF.Exp,
                    scale=-1.0), SC)
                e1 = SC.n
                p.wait("tensor", SC, e0)
                p.wait("tensor", DVE)
                p.op("tensor", lambda e_: e_.matmul(
                    pb[2].ap()[0:64, :], ones_rowf.ap()[64:65, 0:64],
                    YT[64:65, 512:1024], start=True, stop=True), PE)
                b0 = PE.n
                p.wait("tensor", SC, e1)
                p.op("tensor", lambda e_: e_.matmul(
                    pb[3].ap()[0:64, :], ones_rowf.ap()[64:65, 0:64],
                    NORM1[64:65, 512:1024], start=True, stop=True), PE)
                b1 = PE.n
                p.wait("vector", PE, b0)
                p.wait("vector", DVE)
                p.op("vector", lambda e_: e_.tensor_copy(
                    recipB[0:64, 0:QC2], pb[2].ap()[0:64, :]), DVE)
                p.wait("vector", PE, b1)
                p.wait("vector", DVE)
                p.op("vector", lambda e_: e_.tensor_copy(
                    NORM1[0:64, 0:512], pb[3].ap()[0:64, :]), DVE)
                p.wait("vector", DVE)
                p.op("vector", lambda e_, pr=pr: e_.tensor_tensor(
                    ctxN2[0:64, pr, :], pb[4].ap()[0:64, :],
                    recipB[0:64, 0:QC2], ALU.mult), DVE)
                p.wait("vector", sm["s_cst"])
                p.wait("vector", DVE)
                p.op("vector", lambda e_, pr=pr: e_.tensor_tensor(
                    attb2[0:64, 0, :], pb[5].ap()[0:64, :],
                    NORM1[0:64, 0:512], ALU.mult), DVE)
                p.wait("gpsimd", DVE)
                dma("gpsimd", ctxN2[64:128, pr, :], attb2[0:64, 0, :],
                    sm["s_cst"])
                ctxn_dve[0] = DVE.n
                if pr < len(pendf):
                    fn, kind, sc2i = pendf[pr]
                    fn()
                    if kind == "b":
                        ctxn_dve[0] = max(ctxn_dve[0], ffn_state["sevac"])
                    elif kind == "2":
                        ctxn_dve[0] = DVE.n
                        pair_rel[sc2i] = DVE.n
                if pr == 3:
                    ep_loads(0)
                if pr == 5:
                    ep_stats(0)
                if pr == 7:
                    ep_loads(1)
            pendf = []
            ctx_ready = DVE.n
            q_pe_rel[0] = PE.n
            p.wait("tensor", sm["s_cst"])
            if debug:
                p.wait("gpsimd", DVE, ctx_ready)
                p.wait("gpsimd", sm["s_cst"])
                dma("gpsimd", dbg2[qc], ctxN2[:, :, :], sm["s_dbg"])
                p.wait("tensor", sm["s_dbg"])
            sc_pre = SC.n
            p.wait("tensor", DVE, ctx_ready)
            ep_stats(1)

            def ep_oproj_pass(sc2, stt, bks, dve_gate):
                for hc in range(8):
                    if hc == 0:
                        if wsb_wait[0]:
                            p.wait("tensor", sm["s_wb0"])
                            p.wait("tensor", sm["s_wb1"])
                            wsb_wait[0] = False
                        p.wait("tensor", SC, sc_pre)
                        if dve_gate:
                            p.wait("tensor", DVE, dve_gate)
                    for oc in range(2):
                        p.op("tensor", lambda e_, b=bks[oc], hc=hc, stt=stt, oc=oc, sc2=sc2: e_.matmul(
                            b.ap()[:, :],
                            ctxN2[:, hc, sc2 * 256 + stt * 128:sc2 * 256 + (stt + 1) * 128],
                            wsb[:, hc, oc * 512:(oc + 1) * 512],
                            start=(hc == 0), stop=(hc == 7)),
                            PE if (hc == 7 and oc == 1) else None)
                return PE.n

            def ep_adds(sc2, stt, bks, ydst, pass_pe):
                ln2_affine(stt, pairs[sc2][stt][:, :], YT[:, :], st=sts[sc2])
                p.wait("vector", PE, pass_pe)
                p.wait("vector", DVE)
                for oc in range(2):
                    p.op("vector", lambda e_, b=bks[oc], oc=oc, ydst=ydst: e_.tensor_tensor(
                        ydst[:, oc * 512:(oc + 1) * 512], b.ap()[:, :],
                        YT[:, oc * 512:(oc + 1) * 512], ALU.add), DVE)
                return DVE.n

            def ep_lnb(sc2):
                rows0 = q0 + sc2 * 256
                ys = (Y2[2 * sc2], Y2[2 * sc2 + 1])
                ln2_stats([ys[0][:, :], ys[1][:, :]], LNB_EPS, Jnk,
                          st=sts[sc2])
                for stt in range(2):
                    y = ys[stt]
                    ln2_affine(stt, y[:, :], y[:, :], st=sts[sc2])
                    p.wait("vector", DVE)
                    p.op("vector", lambda e_, y=y: e_.tensor_tensor(
                        y[:, :], y[:, :], gamB[:, :], ALU.mult), DVE)
                    p.wait("vector", DVE)
                    p.op("vector", lambda e_, y=y: e_.tensor_tensor(
                        y[:, :], y[:, :], betB[:, :], ALU.add), DVE)
                    p.wait("vector", DVE)
                    p.op("vector", lambda e_, y=y, x=pairs[sc2][stt]: e_.tensor_tensor(
                        x[:, :], x[:, :], y[:, :], ALU.add), DVE)
                    if debug:
                        p.wait("gpsimd", DVE)
                        dma("gpsimd", dbg[rows0 + stt * 128:rows0 + (stt + 1) * 128, :],
                            pairs[sc2][stt][:, :], sm["s_dbg"])
                p.wait("vector", sm["s_dbg"])

            a0 = ep_oproj_pass(0, 0, (pb[4], pb[5]), None)
            b0 = ep_oproj_pass(0, 1, (pb[2], pb[3]), None)
            yA0 = ep_adds(0, 0, (pb[4], pb[5]), Y2[0], a0)
            yB0 = ep_adds(0, 1, (pb[2], pb[3]), Y2[1], b0)
            a1 = ep_oproj_pass(1, 0, (pb[4], pb[5]), yA0)
            b1 = ep_oproj_pass(1, 1, (pb[2], pb[3]), yB0)
            ep_adds(1, 0, (pb[4], pb[5]), Y2[2], a1)
            yB1 = ep_adds(1, 1, (pb[2], pb[3]), Y2[3], b1)
            ctxn_dve[0] = DVE.n
            opj23_free[0] = yB1
            pendf = [
                (lambda f=ep_lnb: f(0), "a", None),
                (lambda: ffn_part1a([XW[2][:, :], XW[3][:, :]]), "a", None),
                (ffn_part1b, "b", None),
                (lambda q0=q0: ffn_part2([XW[2][:, :], XW[3][:, :]], q0, out,
                                         sm["s_stout"]), "2", 0),
                (lambda f=ep_lnb: f(1), "a", None),
                (lambda: ffn_part1a([XW[0][:, :], XW[1][:, :]]), "a", None),
                (ffn_part1b, "b", None),
                (lambda q0=q0: ffn_part2([XW[0][:, :], XW[1][:, :]], q0 + 256,
                                         out, sm["s_stout"]), "2", 1),
            ]
        for fn, _, _ in pendf:
            fn()

    phase_a(False)
    barrier()
    phase_1b()
    barrier()
    phase_a(True)
    barrier()
    phase_2b()
    p.wait("sync", sm["s_stout"])
    p.wait("sync", sm["s_stx"])
    p.wait("sync", sm["s_dbg"])
    p.wait("sync", sm["s_cst"])

    p.finalize()
    with nc.Block() as block:
        @block.sync
        def _(eng):
            for fn in p.streams["sync"]:
                fn(eng)

        @block.gpsimd
        def _(eng):
            for fn in p.streams["gpsimd"]:
                fn(eng)

        @block.tensor
        def _(eng):
            for fn in p.streams["tensor"]:
                fn(eng)

        @block.vector
        def _(eng):
            for fn in p.streams["vector"]:
                fn(eng)

        @block.scalar
        def _(eng):
            for fn in p.streams["scalar"]:
                fn(eng)

    ctx_stack.close()
    return nc


_CACHE = {}


def kernel(L0_attn_a_in_features, L0_attn_a_w_q, L0_attn_a_w_k, L0_attn_a_w_v,
           L0_attn_a_w_o, L0_attn_b_Wq, L0_attn_b_Wk, L0_attn_b_Wv, L0_attn_b_Wo,
           L0_attn_b_gamma, L0_attn_b_beta, L0_ffn_Win, L0_ffn_bin_, L0_ffn_W1,
           L0_ffn_b1, L0_ffn_W2, L0_ffn_b2, L0_ffn_Wout, L0_ffn_bout):
    x = np.ascontiguousarray(np.asarray(L0_attn_a_in_features, dtype=np.float32))
    B = x.shape[0]
    if "nc" not in _CACHE:
        _CACHE["nc"] = build_nc()
    nc = _CACHE["nc"]
    c = lambda a: np.ascontiguousarray(np.asarray(a, dtype=np.float32))
    w = dict(wq_a=c(L0_attn_a_w_q), wk_a=c(L0_attn_a_w_k), wv_a=c(L0_attn_a_w_v),
             wo_a=c(L0_attn_a_w_o), wq_b=c(L0_attn_b_Wq), wk_b=c(L0_attn_b_Wk),
             wv_b=c(L0_attn_b_Wv), wo_b=c(L0_attn_b_Wo), gam=c(L0_attn_b_gamma),
             bet=c(L0_attn_b_beta), f_win=c(L0_ffn_Win), f_bin=c(L0_ffn_bin_),
             f_w1=c(L0_ffn_W1), f_b1=c(L0_ffn_b1), f_w2=c(L0_ffn_W2),
             f_b2=c(L0_ffn_b2), f_wout=c(L0_ffn_Wout), f_bout=c(L0_ffn_bout))
    in_maps = [dict(w, x=np.ascontiguousarray(x[b])) for b in range(B)]
    res = run_bass_kernel_spmd(nc, in_maps, core_ids=list(range(B)))
    return np.stack([res.results[b]["out"] for b in range(B)], axis=0)
